# version 16
# baseline (speedup 1.0000x reference)
import numpy as np
import ml_dtypes
import concourse.bass as bass
import concourse.mybir as mybir
from concourse.bass_utils import run_bass_kernel_spmd
from contextlib import ExitStack

F32 = mybir.dt.float32
BF16 = mybir.dt.bfloat16
I32 = mybir.dt.int32
F16 = mybir.dt.float16
AF = mybir.ActivationFunctionType
ALU = mybir.AluOpType
AX = mybir.AxisListType

D = 1024
H = 16
DK = 64
HID = 4096
CH = 128
KT = 256.0
KW = 256.0

ENGS = ("tensor", "vector", "scalar", "sync", "gpsimd")
DMA_ENGS = {"sync", "gpsimd"}


class Sched:
    NSLOT = 8

    def __init__(self, nc, sems):
        self.nc = nc
        self.sems = sems
        self.idx = {}
        self.qcnt = {e: 0 for e in DMA_ENGS}
        self.last_w = {}
        self.readers = {}
        self.known = {e: {} for e in ENGS}
        self.snap = {}

    @staticmethod
    def phys(pe):
        if pe == "pool":
            return "gpsimd"
        return pe.split(":")[0]

    def _sem(self, pe):
        if ":" in pe:
            e, r = pe.split(":")
            return self.sems[e][int(r)]
        return self.sems[pe]

    def _wait(self, eng, kn, pe, di):
        if kn.get(pe, 0) >= di:
            return
        val = di * 16 if ":" in pe else di
        eng.wait_ge(self._sem(pe), val)
        sn = self.snap.get((pe, di))
        if sn is not None:
            for e3, v in sn.items():
                if v > kn.get(e3, 0):
                    kn[e3] = v
        if di > kn.get(pe, 0):
            kn[pe] = di

    def emit(self, e, fn, reads=(), writes=(), acc=False):
        eng = getattr(self.nc, "gpsimd" if e == "pool" else e)
        kn = self.known[self.phys(e)]
        if e in DMA_ENGS:
            r = self.qcnt[e] % self.NSLOT
            self.qcnt[e] += 1
            pe = f"{e}:{r}"
        else:
            pe = e
        deps = []
        for k in reads:
            w = self.last_w.get(k)
            if w is not None:
                deps.append(w)
        for k in writes:
            w = self.last_w.get(k)
            if w is not None:
                deps.append(w)
            for rd in self.readers.get(k, ()):
                deps.append(rd)
        if e in DMA_ENGS and self.idx.get(pe, 0) > 0:
            deps.append((pe, self.idx[pe]))
        myidx = self.idx.get(pe, 0) + 1
        need = {}
        for (de, di) in deps:
            if de == pe and di >= myidx:
                continue
            if acc and de == "tensor" and e == "tensor":
                continue
            if de not in need or need[de] < di:
                need[de] = di
        order = ([de for de in need if self.phys(de) != e]
                 + [de for de in need if self.phys(de) == e])
        for de in order:
            self._wait(eng, kn, de, need[de])
        if fn is None:
            return None
        inst = fn(eng)
        inst.then_inc(self._sem(pe), 16 if e in DMA_ENGS else 1)
        self.idx[pe] = myidx
        self.snap[(pe, myidx)] = dict(kn)
        op = (pe, myidx)
        for k in writes:
            self.last_w[k] = op
            self.readers[k] = []
        for k in reads:
            self.readers.setdefault(k, []).append(op)
        return op

    def barrier(self):
        tails = dict(self.idx)
        for e in ENGS:
            eng = getattr(self.nc, e)
            kn = self.known[e]
            for pe, di in tails.items():
                if self.phys(pe) == e or di == 0:
                    continue
                self._wait(eng, kn, pe, di)


def build_program(n_seq, n_chunks, debug_stop=0):
    T = n_chunks * CH
    NTOK = n_seq * T
    NTT = n_seq * n_chunks
    GRP = 2
    assert n_chunks % GRP == 0
    GT = GRP * CH

    nc = bass.Bass("TRN2", target_bir_lowering=False)

    xd = nc.dram_tensor("x", [NTOK, D], F32, kind="ExternalInput").ap()
    wT_d = {nm: nc.dram_tensor(nm, [D, D], F32, kind="ExternalInput").ap()
            for nm in ("wqT", "wkT", "wvT", "woT")}
    w1T_d = nc.dram_tensor("w1T", [D, HID], F32, kind="ExternalInput").ap()
    w2T_d = nc.dram_tensor("w2T", [HID, D], F32, kind="ExternalInput").ap()
    gl_d = nc.dram_tensor("gl", [4, H], F32, kind="ExternalInput").ap()
    ln_d = nc.dram_tensor("lnp", [4, HID], F32, kind="ExternalInput").ap()
    b1_d = nc.dram_tensor("b1", [1, HID], F32, kind="ExternalInput").ap()
    b2_d = nc.dram_tensor("b2", [1, D], F32, kind="ExternalInput").ap()
    identb_d = nc.dram_tensor("identb", [128, 128], BF16, kind="ExternalInput").ap()
    identf_d = nc.dram_tensor("identf", [128, 128], F32, kind="ExternalInput").ap()
    ieye2_d = nc.dram_tensor("ieye2", [128, 64], BF16, kind="ExternalInput").ap()
    maskmin_d = nc.dram_tensor("maskmin", [128, 128], F32, kind="ExternalInput").ap()
    selpc_d = nc.dram_tensor("selpc", [16, 8 * 128], BF16, kind="ExternalInput").ap()
    yd = nc.dram_tensor("y", [NTOK, D], F32, kind="ExternalOutput").ap()
    qw1_s = nc.dram_tensor("qw1s", [D, HID], BF16, kind="Internal").ap()
    qw2_s = nc.dram_tensor("qw2s", [HID, D], BF16, kind="Internal").ap()

    es = ExitStack()

    def sb(name, shape, dt):
        return es.enter_context(nc.sbuf_tensor("s_" + name, shape, dt))

    def pt(name, shape, dt=F32):
        return es.enter_context(nc.psum_tensor(name, shape, dt))

    sems = {}
    for e in ENGS:
        if e in DMA_ENGS:
            sems[e] = [es.enter_context(nc.semaphore(f"sem_{e}{r}"))
                       for r in range(Sched.NSLOT)]
        else:
            sems[e] = es.enter_context(nc.semaphore("sem_" + e))
    sems["pool"] = es.enter_context(nc.semaphore("sem_pool"))
    S = Sched(nc, sems)

    X2 = sb("X2", [128, NTT * D], BF16)
    identb = sb("identb", [128, 128], BF16)
    ones1f = sb("ones1f", [1, 128], F32)
    onesbf = sb("onesbf", [1, 128], BF16)
    onescolf = sb("onescolf", [128, 1], F32)
    statp = sb("statp", [128, 64], F32)
    statb1 = sb("statb1", [128, 1], BF16)
    g2048f = sb("g2048f", [128, 1024], F32)
    qi32buf = sb("qi32buf", [128, 1024], I32)
    swcol = sb("swcol", [128, 2], F32)
    rstgf = sb("rstgf", [1, 512], F32)
    rstgb = sb("rstgb", [1, 512], BF16)
    e24col = sb("e24col", [128, 1], F32)
    e30col = sb("e30col", [128, 1], F32)
    nKTcol = sb("nKTcol", [128, 1], F32)
    nKWcol = sb("nKWcol", [128, 1], F32)

    PA = pt("PA", [128, 1024])
    PB = pt("PB", [128, 1024])
    PC = pt("PC", [128, 512])
    PD = pt("PD", [128, 512])
    PE_ = pt("PE", [128, 512])
    PX = pt("PX", [128, 512])
    PD_ALL = ["PDq", "PDv"] + [f"PDs{h}" for h in range(H)]

    def dma(q, out, in_, rk, wk):
        return S.emit(q, lambda g: g.dma_start(out, in_), reads=rk, writes=wk)

    def T_(fn, rk, wk, acc=True):
        return S.emit("tensor", fn, reads=rk, writes=wk, acc=True)

    def V_(fn, rk, wk):
        return S.emit("vector", fn, reads=rk, writes=wk)

    def SC(fn, rk, wk):
        return S.emit("scalar", fn, reads=rk, writes=wk)

    def PL(fn, rk, wk):
        return S.emit("pool", fn, reads=rk, writes=wk)

    def st(i, n=1):
        return statp[:, i:i + n]

    def ln_stats(a, b, c, N, tag):
        V_(lambda g: g.tensor_scalar(st(a), st(a), 1.0 / N, None, ALU.mult),
           [f"{tag}a"], [f"{tag}a"])
        V_(lambda g: g.tensor_scalar(st(b), st(b), 1.0 / N, None, ALU.mult),
           [f"{tag}b"], [f"{tag}b"])
        V_(lambda g: g.scalar_tensor_tensor(st(b), st(a), st(a), st(b),
                                            ALU.mult, ALU.subtract),
           [f"{tag}a", f"{tag}b"], [f"{tag}b"])
        V_(lambda g: g.tensor_scalar(st(b), st(b), -1.0, 1e-5, ALU.mult, ALU.add),
           [f"{tag}b"], [f"{tag}b"])
        SC(lambda g: g.activation(st(c), st(b), AF.Sqrt),
           [f"{tag}b"], [f"{tag}c"])
        V_(lambda g: g.reciprocal(st(c), st(c)), [f"{tag}c"], [f"{tag}c"])
        V_(lambda g: g.scalar_tensor_tensor(st(c + 1), st(a), -1.0, st(c),
                                            ALU.mult, ALU.mult),
           [f"{tag}a", f"{tag}c"], [f"{tag}d"])

    dma("sync", identb[:], identb_d, [], ["identb"])
    V_(lambda g: g.memset(ones1f[:], 1.0), [], ["ones1f"])
    V_(lambda g: g.memset(onesbf[:], 1.0), [], ["onesbf"])
    V_(lambda g: g.memset(onescolf[:], 1.0), [], ["onescolf"])
    V_(lambda g: g.memset(e24col[:], 1e-24), [], ["ccol"])
    V_(lambda g: g.memset(e30col[:], 1e-30), [], ["ccol"])
    V_(lambda g: g.memset(nKTcol[:], -KT), [], ["ccol"])
    V_(lambda g: g.memset(nKWcol[:], -KW), [], ["ccol"])

    p1 = ExitStack()

    def sb1(name, shape, dt):
        return p1.enter_context(nc.sbuf_tensor("s_" + name, shape, dt))

    identf = sb1("identf", [128, 128], F32)
    maskmin = sb1("maskmin", [128, 128], F32)
    selpc = sb1("selpc", [16, 8 * 128], BF16)
    zeros16 = sb1("zeros16", [16, 128], F32)
    glb = sb1("glb", [128, 4 * H], F32)
    glrow = sb1("glrow", [1, 4 * H], F32)
    dma("sync", identf[:], identf_d, [], ["identf"])
    dma("sync", maskmin[:], maskmin_d, [], ["maskmin"])
    dma("sync", selpc[:], selpc_d, [], ["selpc"])
    V_(lambda g: g.memset(zeros16[:], 0.0), [], ["zeros16"])
    dma("sync", glrow[:], gl_d.rearrange("a b -> (a b)").rearrange("(o n) -> o n", o=1),
        [], ["glrow"])
    for j in range(4):
        T_(lambda g, j=j: g.matmul(PX[:, j * 16:(j + 1) * 16], ones1f[:],
                                   glrow[0:1, j * 16:(j + 1) * 16],
                                   start=True, stop=True),
           ["ones1f", "glrow"], ["PX0", "BPX"])
    V_(lambda g: g.tensor_copy(glb[:], PX[:, 0:64]), ["PX0", "BPX"], ["glb"])

    wq_sb = sb1("wq_sb", [128, 8 * D], BF16)
    wk_sb = sb1("wk_sb", [128, 8 * D], BF16)
    wv_sb = sb1("wv_sb", [128, 8 * D], BF16)
    wo_sb = sb1("wo_sb", [128, 8 * D], BF16)
    wstage = sb1("wstage", [128, 1024], F32)
    wqout = sb1("wqout", [128, 1024], BF16)
    absacc = sb1("absacc", [128, 64], F32)

    xstage = sb1("xstage", [128, D], F32)
    lnxtok = sb1("lnxtok", [128, D], BF16)
    lnxfm = sb1("lnxfm", [128, 8 * GT], BF16)
    Qsb = sb1("Qsb", [128, 8 * GT], BF16)
    Qpsb = sb1("Qpsb", [128, 8 * GT], BF16)
    PexpS = sb1("PexpS", [128, GT], F32)
    Vsb = sb1("Vsb", [128, 8 * GT], BF16)
    Ktok = sb1("Ktok", [128, GRP * D], BF16)
    Equad = sb1("Equad", [128, 12 * CH], F32)
    Abuf = sb1("Abuf", [128, H * CH], BF16)
    Vtbuf = sb1("Vtbuf", [128, H * DK], BF16)
    Lbcbuf = sb1("Lbcbuf", [128, 8 * CH], F32)
    Yfm = sb1("Yfm", [128, 8 * CH], BF16)
    aobuf = sb1("aobuf", [128, D], BF16)
    hnbuf = sb1("hnbuf", [128, D], BF16)
    Sf32 = sb1("Sf32", [128, 8 * DK], F32)
    Sbf2 = sb1("Sbf2", [128, H * DK], BF16)
    wcol = sb1("wcol", [128, 16], F32)
    wtm = sb1("wtm", [128, 16], F32)
    llrow = sb1("llrow", [1, 16], F32)
    V_(lambda g: g.memset(Sbf2[:], 0.0), [], ["Sbf2"])
    LTf = sb1("LTf", [16, GRP * CH], F32)
    nLT = sb1("nLT", [16, CH], F32)
    negLtok = sb1("negLtok", [128, GRP * H], F32)
    expLTf = sb1("expLTf", [16, CH], F32)
    expLTb = sb1("expLTb", [16, GT], BF16)
    pCrow = sb1("pCrow", [1, 16], F32)
    pCcols = sb1("pCcols", [128, GRP * 16], F32)
    lrtok = sb1("lrtok", [128, 16], F32)
    dsc = sb1("dsc", [128, 16 * 8], F32)
    outst_dbg = sb1("outst_dbg", [128, D], F32) if debug_stop else None

    def dsl(i):
        return dsc[:, i * 16:(i + 1) * 16]

    KB, GA, LA, RHO, ONEMR, KFAC, T1, T2 = range(8)

    for wi, (nm, wsb) in enumerate((("wqT", wq_sb), ("wkT", wk_sb),
                                    ("wvT", wv_sb), ("woT", wo_sb))):
        for kc in range(8):
            dma("sync", wstage[:, 0:D],
                wT_d[nm][kc * 128:(kc + 1) * 128, :], [], ["wstg0"])
            if kc % 2 == 0:
                V_(lambda g, wsb=wsb, kc=kc: g.tensor_copy(
                    wsb[:, kc * D:(kc + 1) * D], wstage[:, 0:D]),
                    ["wstg0"], [f"w{wi}_{kc}"])
            else:
                SC(lambda g, wsb=wsb, kc=kc: g.copy(
                    wsb[:, kc * D:(kc + 1) * D], wstage[:, 0:D]),
                    ["wstg0"], [f"w{wi}_{kc}"])

    def wq_params(which):
        wd, qd, nchunk, cw = ((w1T_d, qw1_s, 8, HID) if which == 0
                              else (w2T_d, qw2_s, 32, D))
        return wd, qd, nchunk, cw // 1024, 1024, (0 if which == 0 else 32)

    def wquant_A(which, items):
        wd, qd, nchunk, nh, hw, acol = wq_params(which)
        for (ch, hf) in items:
            ai = ch * nh + hf
            dma("gpsimd", wstage[:, 0:hw], wd[ch * 128:(ch + 1) * 128,
                                              hf * hw:(hf + 1) * hw],
                [], ["wstg0"])
            V_(lambda g, ai=ai, hw=hw, acol=acol: g.tensor_reduce(
                absacc[:, acol + ai:acol + ai + 1], wstage[:, 0:hw], AX.X,
                ALU.add, apply_absolute_value=True),
                ["wstg0"], [f"aacc{which}_{ai}"])

    def wquant_S(which):
        wd, qd, nchunk, nh, hw, acol = wq_params(which)
        nacc = nchunk * nh
        V_(lambda g, nacc=nacc, acol=acol, which=which: g.tensor_reduce(
            st(60 + which), absacc[:, acol:acol + nacc], AX.X, ALU.add),
            [f"aacc{which}_{a2}" for a2 in range(nacc)], [f"csum{which}"])
        T_(lambda g, which=which: g.matmul(PX[0:1, 432 + which:433 + which],
                                           st(60 + which), onescolf[:],
                                           start=True, stop=True),
           [f"csum{which}", "onescolf"], ["PXw", "BPX"])
        nels = float(D * HID)
        V_(lambda g, which=which, nels=nels: g.tensor_scalar(
            st(56 + which)[0:1, :], PX[0:1, 432 + which:433 + which],
            1.0 / nels, 1e-5, ALU.mult, ALU.max), ["PXw", "BPX"], [f"wmean{which}"])
        V_(lambda g, which=which: g.reciprocal(
            st(58 + which)[0:1, :], st(56 + which)[0:1, :]),
            [f"wmean{which}"], [f"wsc{which}"])
        T_(lambda g, which=which: g.matmul(
            PX[:, 436 + which:437 + which], ones1f[:], st(58 + which)[0:1, :],
            start=True, stop=True), [f"wsc{which}", "ones1f"], ["PXw", "BPX"])
        V_(lambda g, which=which: g.tensor_copy(
            swcol[:, which:which + 1], PX[:, 436 + which:437 + which]),
            ["PXw", "BPX"], [f"swcol{which}"])

    def wquant_B(which, items):
        wd, qd, nchunk, nh, hw, acol = wq_params(which)
        for (ch, hf) in items:
            wsk = ["wstg0"]
            dma("gpsimd", wstage[:, 0:hw], wd[ch * 128:(ch + 1) * 128,
                                              hf * hw:(hf + 1) * hw],
                [], wsk)
            V_(lambda g, hw=hw, which=which: g.tensor_scalar(
                g2048f[:, 0:hw], wstage[:, 0:hw], swcol[:, which:which + 1],
                None, ALU.mult),
                wsk + [f"swcol{which}", "g2048a"], ["g2048a"])
            V_(lambda g, hw=hw: g.tensor_copy(qi32buf[:, 0:hw], g2048f[:, 0:hw]),
               ["g2048a"], ["qi32"])
            V_(lambda g, hw=hw: g.scalar_tensor_tensor(
                wstage[:, 0:hw], qi32buf[:, 0:hw], 0.5, g2048f[:, 0:hw],
                ALU.add, ALU.is_le), ["qi32", "g2048a"] + wsk, wsk)
            V_(lambda g, hw=hw: g.scalar_tensor_tensor(
                wqout[:, 0:hw], qi32buf[:, 0:hw], 0.0, wstage[:, 0:hw],
                ALU.add, ALU.add), ["qi32"] + wsk, ["wqo"])
            V_(lambda g, hw=hw: g.scalar_tensor_tensor(
                wstage[:, 0:hw], qi32buf[:, 0:hw], -0.5, g2048f[:, 0:hw],
                ALU.add, ALU.is_ge), ["qi32", "g2048a"] + wsk, wsk)
            V_(lambda g, hw=hw: g.tensor_tensor(
                wqout[:, 0:hw], wqout[:, 0:hw], wstage[:, 0:hw], ALU.subtract),
                ["wqo"] + wsk, ["wqo"])
            V_(lambda g, hw=hw: g.tensor_scalar(wqout[:, 0:hw], wqout[:, 0:hw],
                                                1.0, -1.0, ALU.min, ALU.max),
               ["wqo"], ["wqo"])
            dma("gpsimd", qd[ch * 128:(ch + 1) * 128, hf * hw:(hf + 1) * hw],
                wqout[:, 0:hw], ["wqo"], ["wqstore"])

    wq_items = {0: [(c, f) for c in range(8) for f in range(4)],
                1: [(c, 0) for c in range(32)]}

    if debug_stop == 2:
        for tt in range(NTT):
            dma("gpsimd", yd[tt * CH:tt * CH + 128, :], wstage[:, 0:D],
                ["wstg0"], [f"ydone{tt}"])
        S.emit("sync", None, reads=[f"ydone{tt}" for tt in range(NTT)], writes=[])
        S.emit("gpsimd", None, reads=[f"ydone{tt}" for tt in range(NTT)], writes=[])
        return nc
    n_grp_b = n_chunks // GRP
    for b in range(n_seq):
        V_(lambda g: g.memset(Sf32[:], 0.0), [], ["Sf32"])
        for gg in range(n_grp_b):
            gi = b * n_grp_b + gg
            ngi = n_seq * n_grp_b
            def slice_items(items, i, n):
                k = len(items)
                return items[(i * k) // n:((i + 1) * k) // n]
            if ngi == 1:
                wquant_A(0, wq_items[0]); wquant_A(1, wq_items[1])
                wquant_S(0); wquant_S(1)
                wquant_B(0, wq_items[0]); wquant_B(1, wq_items[1])
            else:
                half = ngi // 2
                if gi < half:
                    wquant_A(0, slice_items(wq_items[0], gi, half))
                    wquant_A(1, slice_items(wq_items[1], gi, half))
                if gi == half:
                    wquant_S(0)
                    wquant_S(1)
                if gi >= half:
                    i2 = gi - half
                    n2 = ngi - half
                    wquant_B(0, slice_items(wq_items[0], i2, n2))
                    wquant_B(1, slice_items(wq_items[1], i2, n2))
            c0 = gg * GRP
            for cc in range(GRP):
                c = c0 + cc
                tt = b * n_chunks + c
                xs = xstage[:, 0:D]
                xrow0 = b * T + c * CH
                dma("sync", xs, xd[xrow0:xrow0 + 128, :], [], ["xstg0"])
                x2sl = X2[:, tt * D:(tt + 1) * D]
                V_(lambda g, xs=xs, x2sl=x2sl: g.tensor_scalar(
                    x2sl, xs, 1.0, 0.0, ALU.mult, ALU.add, accum_out=st(0)),
                    ["xstg0"], [f"x2:{tt}", "ln1a"])
                SC(lambda g, xs=xs: g.activation(
                    g2048f[:, 0:D], xs, AF.Square, accum_out=st(1)),
                    ["xstg0"], ["g2048a", "ln1b"])
                ln_stats(0, 1, 2, D, "ln1")
                lnx = lnxtok[:, 0:D]
                SC(lambda g, x2sl=x2sl, lnx=lnx: g.activation(
                    lnx, x2sl, AF.Identity, bias=st(3), scale=st(2)),
                    [f"x2:{tt}", "ln1c", "ln1d"], ["lnx0"])
                for half in range(2):
                    for kq in range(4):
                        kc = half * 4 + kq
                        T_(lambda g, lnx=lnx, kc=kc, kq=kq: g.matmul(
                            PE_[:, kq * 128:(kq + 1) * 128],
                            lnx[:, kc * 128:(kc + 1) * 128], identb[:],
                            start=True, stop=True),
                            ["lnx0", "identb"], [f"PEq{kq}", "BPE"])
                    V_(lambda g, half=half, cc=cc: g.tensor_copy(
                        lnxfm[:].rearrange("p (k t) -> p k t", k=8)
                        [:, half * 4:half * 4 + 4, cc * 128:(cc + 1) * 128],
                        PE_[:].rearrange("p (k t) -> p k t", k=4)),
                        [f"PEq{kq}" for kq in range(4)] + ["BPE"],
                        [f"lnxfm{k2}_{cc}" for k2 in range(half * 4, half * 4 + 4)])
                for eh in range(2):
                    for kc in range(8):
                        T_(lambda g, eh=eh, kc=kc, cc=cc: g.matmul(
                            PA[:, eh * 512:(eh + 1) * 512],
                            lnxfm[:, kc * GT + cc * 128:kc * GT + (cc + 1) * 128],
                            wk_sb[:, kc * D + eh * 512:kc * D + (eh + 1) * 512],
                            start=(kc == 0), stop=(kc == 7)),
                            [f"lnxfm{kc}_{cc}", f"w1_{kc}"],
                            ["PA", f"BPA{eh}"], acc=(kc != 0 or eh != 0))
                SC(lambda g: g.activation(g2048f[:, 0:D], PA[:], AF.Square),
                   ["PA", "BPA0", "BPA1"], ["g2048a"])
                V_(lambda g: g.tensor_reduce(
                    dsl(T1), g2048f[:, 0:D].rearrange("p (h j) -> p h j", h=16),
                    AX.X, ALU.add), ["g2048a"], ["dT1"])
                V_(lambda g: g.tensor_reduce(
                    dsl(T2), PA[:].rearrange("p (h j) -> p h j", h=16),
                    AX.X, ALU.add), ["PA", "BPA0", "BPA1"], ["dT2"])
                SC(lambda g: g.activation(dsl(KFAC), dsl(T1), AF.Sqrt,
                                          bias=e24col[:]),
                   ["dT1", "ccol"], ["dKF"])
                V_(lambda g: g.reciprocal(dsl(KFAC), dsl(KFAC)), ["dKF"], ["dKF"])
                V_(lambda g: g.scalar_tensor_tensor(
                    dsl(KB), dsl(T2), 1.0 / 64.0, dsl(KFAC), ALU.mult, ALU.mult),
                    ["dT2", "dKF"], ["dKB"])
                V_(lambda g: g.tensor_tensor(dsl(T1), dsl(KB), glb[:, 0:16], ALU.mult),
                   ["dKB", "glb"], ["dT1"])
                V_(lambda g: g.tensor_tensor(dsl(T1), dsl(T1), glb[:, 16:32], ALU.add),
                   ["dT1", "glb"], ["dT1"])
                SC(lambda g: g.activation(dsl(GA), dsl(T1), AF.Sigmoid), ["dT1"], ["dGA"])
                V_(lambda g: g.tensor_tensor(dsl(T2), dsl(KB), glb[:, 32:48], ALU.mult),
                   ["dKB", "glb"], ["dT2"])
                V_(lambda g: g.tensor_tensor(dsl(T2), dsl(T2), glb[:, 48:64], ALU.add),
                   ["dT2", "glb"], ["dT2"])
                SC(lambda g: g.activation(dsl(LA), dsl(T2), AF.Sigmoid), ["dT2"], ["dLA"])
                V_(lambda g: g.tensor_scalar(dsl(T2), dsl(LA), -1.0, 1.0,
                                             ALU.mult, ALU.add), ["dLA"], ["dT2"])
                V_(lambda g: g.tensor_tensor(dsl(RHO), dsl(T2), dsl(GA), ALU.mult),
                   ["dT2", "dGA"], ["dRHO"])
                V_(lambda g: g.tensor_scalar(dsl(ONEMR), dsl(RHO), -1.0, 1.0,
                                             ALU.mult, ALU.add), ["dRHO"], ["dOM"])
                SC(lambda g: g.activation(lrtok[:], dsl(RHO), AF.Ln,
                                          bias=e30col[:]),
                   ["dRHO", "ccol"], ["lrtok"])
                V_(lambda g: g.scalar_tensor_tensor(dsl(KFAC), dsl(KFAC), 1.0,
                                                    dsl(ONEMR), ALU.mult, ALU.mult),
                   ["dKF", "dOM"], ["dKF"])
                V_(lambda g, cc=cc: g.tensor_tensor(
                    Ktok[:, cc * D:(cc + 1) * D].rearrange("p (h j) -> p h j", h=16),
                    PA[:].rearrange("p (h j) -> p h j", h=16),
                    dsl(KFAC).rearrange("p (h o) -> p h o", o=1)
                    .broadcast_to([128, 16, 64]),
                    ALU.mult), ["PA", "BPA0", "BPA1", "dKF"], [f"ktok{cc}"])
                T_(lambda g: g.matmul(PX[0:16, 0:128], lrtok[:], identf[:],
                                      start=True, stop=True),
                   ["lrtok", "identf"], ["PX0", "BPX"])
                V_(lambda g, cc=cc: g.tensor_tensor_scan(
                    LTf[:, cc * CH:(cc + 1) * CH], PX[0:16, 0:128], zeros16[:],
                    0.0, ALU.add, ALU.add),
                    ["PX0", "BPX", "zeros16"], [f"LTf{cc}"])
                V_(lambda g, cc=cc: g.tensor_scalar(
                    nLT[:], LTf[:, cc * CH:(cc + 1) * CH], -1.0, None, ALU.mult),
                    [f"LTf{cc}"], ["nLT"])
                T_(lambda g: g.matmul(PX[0:128, 128:144], nLT[:], identf[0:16, 0:16],
                                      start=True, stop=True),
                   ["nLT", "identf"], ["PX0", "BPX"])
                V_(lambda g, cc=cc: g.tensor_copy(
                    negLtok[:, cc * H:(cc + 1) * H], PX[0:128, 128:144]),
                    ["PX0", "BPX"], [f"negL{cc}"])
                SC(lambda g, cc=cc: g.activation(
                    expLTf[:], LTf[:, cc * CH:(cc + 1) * CH], AF.Exp),
                    [f"LTf{cc}"], ["expLTf"])
                V_(lambda g, cc=cc: g.tensor_copy(
                    expLTb[:, cc * CH:(cc + 1) * CH], expLTf[:]),
                    ["expLTf"], [f"expLTb{cc}"])
                T_(lambda g: g.matmul(PX[0:1, 144:160], expLTf[:, 127:128],
                                      identf[0:16, 0:16], start=True, stop=True),
                   ["expLTf", "identf"], ["PX0", "BPX"])
                V_(lambda g: g.tensor_copy(pCrow[:], PX[0:1, 144:160]),
                   ["PX0", "BPX"], ["pCrow"])
                T_(lambda g: g.matmul(PX[0:128, 160:176], ones1f[:], pCrow[:],
                                      start=True, stop=True),
                   ["pCrow", "ones1f"], ["PX0", "BPX"])
                V_(lambda g, cc=cc: g.tensor_copy(
                    pCcols[:, cc * 16:(cc + 1) * 16], PX[0:128, 160:176]),
                    ["PX0", "BPX"], [f"pCcols{cc}"])
            for pc in range(8 if debug_stop not in (3,) else 0):
                T_(lambda g, pc=pc: g.matmul(
                    PX[:, 176:176 + GT], selpc[:, pc * 128:(pc + 1) * 128],
                    expLTb[:], start=True, stop=True),
                    [f"expLTb{c2}" for c2 in range(GRP)] + ["selpc"], ["PXb", "BPX"])
                for kc in range(8):
                    T_(lambda g, kc=kc, pc=pc: g.matmul(
                        PD[0:128, 0:GT],
                        wq_sb[:, kc * D + pc * 128:kc * D + (pc + 1) * 128],
                        lnxfm[:, kc * GT:(kc + 1) * GT],
                        start=(kc == 0), stop=(kc == 7)),
                        [f"lnxfm{kc}_{c2}" for c2 in range(GRP)] + [f"w0_{kc}"],
                        ["PDq", "BPD"] + (PD_ALL if kc == 0 else []), acc=(kc != 0))
                V_(lambda g: g.tensor_copy(PexpS[:], PX[:, 176:176 + GT]),
                   ["PXb", "BPX"], ["PexpS"])
                V_(lambda g, pc=pc: g.tensor_tensor(
                    Qpsb[:, pc * GT:(pc + 1) * GT], PD[0:128, 0:GT], PexpS[:],
                    ALU.mult), ["PDq", "BPD", "PexpS"], [f"qp{pc}"])
                V_(lambda g, pc=pc: g.tensor_copy(
                    Qsb[:, pc * GT:(pc + 1) * GT], PD[0:128, 0:GT]),
                    ["PDq", "BPD"], [f"q{pc}"])
                for kc in range(8):
                    T_(lambda g, kc=kc, pc=pc: g.matmul(
                        PD[0:128, 0:GT],
                        wv_sb[:, kc * D + pc * 128:kc * D + (pc + 1) * 128],
                        lnxfm[:, kc * GT:(kc + 1) * GT],
                        start=(kc == 0), stop=(kc == 7)),
                        [f"lnxfm{kc}_{c2}" for c2 in range(GRP)] + [f"w2_{kc}"],
                        ["PDv", "BPD"] + (PD_ALL if kc == 0 else []), acc=(kc != 0))
                SC(lambda g, pc=pc: g.copy(
                    Vsb[:, pc * GT:(pc + 1) * GT], PD[0:128, 0:GT]),
                    ["PDv", "BPD"], [f"v{pc}"])
            QUADS = [[0, 2, 4, 6], [8, 10, 12, 14], [1, 3, 5, 7], [9, 11, 13, 15]]
            acol = {}
            for q in range(4):
                for j, h in enumerate(QUADS[q]):
                    acol[h] = q * 512 + j * 128
            for cc in range(GRP if debug_stop not in (3, 4) else 0):
                c = c0 + cc
                tt = b * n_chunks + c
                T_(lambda g, cc=cc: g.matmul(
                    PX[0:1, 144:160], LTf[:, cc * CH + 127:cc * CH + 128],
                    identf[0:16, 0:16], start=True, stop=True),
                    [f"LTf{cc}", "identf"], ["PX0", "BPX"])
                V_(lambda g: g.tensor_copy(llrow[:], PX[0:1, 144:160]),
                   ["PX0", "BPX"], ["llrow"])
                T_(lambda g: g.matmul(PX[0:128, 160:176], ones1f[:], llrow[:],
                                      start=True, stop=True),
                   ["llrow", "ones1f"], ["PX0", "BPX"])
                V_(lambda g, cc=cc: g.tensor_tensor(
                    wtm[:], PX[0:128, 160:176],
                    negLtok[:, cc * H:(cc + 1) * H], ALU.add),
                    ["PX0", "BPX", f"negL{cc}"], ["wtm"])
                SC(lambda g: g.activation(wcol[:], wtm[:], AF.Exp),
                   ["wtm"], ["wcol"])
                def emit_E(q, cc=cc):
                    heads = QUADS[q]
                    q2 = (cc * 4 + q) % 3
                    eo = q2 * 512
                    l2 = (cc * 4 + q) % 2
                    lo = l2 * 512
                    for j, h in enumerate(heads):
                        dma("gpsimd", Lbcbuf[:, lo + j * 128:lo + (j + 1) * 128],
                            LTf[h:h + 1, cc * CH:(cc + 1) * CH]
                            .rearrange("p (o f) -> p o f", o=1)
                            .broadcast_to([1, 128, 128]),
                            [f"LTf{cc}"], [f"lbc{l2}_{j}"])
                    V_(lambda g, lo=lo: g.scalar_tensor_tensor(
                        Lbcbuf[:, lo:lo + 512].rearrange("p (o f) -> p o f", o=4),
                        Lbcbuf[:, lo:lo + 512].rearrange("p (o f) -> p o f", o=4),
                        1.0,
                        maskmin[:].rearrange("p (o f) -> p o f", o=1)
                        .broadcast_to([128, 4, 128]),
                        ALU.mult, ALU.min),
                        [f"lbc{l2}_{j2}" for j2 in range(4)] + ["maskmin"],
                        [f"lbc{l2}_{j2}" for j2 in range(4)])
                    for j, h in enumerate(heads):
                        SC(lambda g, h=h, j=j, eo=eo, lo=lo: g.activation(
                            Equad[:, eo + j * 128:eo + (j + 1) * 128],
                            Lbcbuf[:, lo + j * 128:lo + (j + 1) * 128], AF.Exp,
                            bias=negLtok[:, cc * H + h:cc * H + h + 1]),
                            [f"lbc{l2}_{j}", f"negL{cc}"], [f"Eq{q2}_{j}"])
                for q in range(3):
                    emit_E(q)
                if debug_stop == 8:
                    continue
                for q in range(4):
                    heads = QUADS[q]
                    qp = 0 if q < 2 else 1
                    APS = PE_ if qp == 0 else PC
                    akey = "PEq" if qp == 0 else "PCo"
                    bkey = "BPE" if qp == 0 else "BPC"
                    q2 = (cc * 4 + q) % 3
                    eo = q2 * 512
                    for j, h in enumerate(heads):
                        hp, pc = h % 2, h // 2
                        T_(lambda g, hp=hp, pc=pc, j=j, cc=cc, APS=APS: g.matmul(
                            APS[:, j * 128:(j + 1) * 128],
                            Vsb[hp * 64:(hp + 1) * 64,
                                pc * GT + cc * 128:pc * GT + (cc + 1) * 128],
                            Qsb[hp * 64:(hp + 1) * 64,
                                pc * GT + cc * 128:pc * GT + (cc + 1) * 128],
                            start=True, stop=True),
                            [f"v{pc}", f"q{pc}"], [f"{akey}{j}", bkey])
                    V_(lambda g, q=q, APS=APS, eo=eo: g.tensor_tensor(
                        Abuf[:, q * 512:(q + 1) * 512], APS[:],
                        Equad[:, eo:eo + 512], ALU.mult),
                        [f"{akey}{j2}" for j2 in range(4)] + [bkey]
                        + [f"Eq{q2}_{j2}" for j2 in range(4)],
                        [f"Ab{q}"])
                    if q == 0:
                        emit_E(3)
                if debug_stop in (8, 9):
                    continue
                for pc in range(8 if debug_stop != 12 else 0):
                    sl = pc % 2
                    T_(lambda g, pc=pc, cc=cc, sl=sl: g.matmul(
                        PX[:, 176 + sl * 128:176 + (sl + 1) * 128],
                        Vsb[:, pc * GT + cc * 128:pc * GT + (cc + 1) * 128],
                        identb[:], start=True, stop=True),
                        [f"v{pc}", "identb"], ["PXb" if sl == 0 else "PXb2", "BPX"])
                    V_(lambda g, pc=pc, sl=sl: g.tensor_tensor(
                        Vtbuf[:, pc * 128:(pc + 1) * 128]
                        .rearrange("p (h j) -> p h j", h=2),
                        PX[:, 176 + sl * 128:176 + (sl + 1) * 128]
                        .rearrange("p (h j) -> p h j", h=2),
                        wcol[:, 2 * pc:2 * pc + 2]
                        .rearrange("p (h o) -> p h o", o=1)
                        .broadcast_to([128, 2, 64]),
                        ALU.mult),
                        ["PXb" if sl == 0 else "PXb2", "BPX", "wcol"], [f"Vt{pc}"])
                for h in range(H if debug_stop not in (5, 12, 13) else 0):
                    hp, pc = h % 2, h // 2
                    first = (c == 0)
                    if not first:
                        T_(lambda g, hp=hp, pc=pc, cc=cc, h=h: g.matmul(
                            PB[hp * 64:(hp + 1) * 64, pc * 128:(pc + 1) * 128],
                            Sbf2[:, h * 64:(h + 1) * 64],
                            Qpsb[:, pc * GT + cc * 128:pc * GT + (cc + 1) * 128],
                            start=True, stop=False),
                            ["Sbf2", f"qp{pc}"], [f"PBy{pc}_{hp}", f"BPB{pc // 4}"])
                    T_(lambda g, hp=hp, pc=pc, cc=cc, h=h, first=first: g.matmul(
                        PB[hp * 64:(hp + 1) * 64, pc * 128:(pc + 1) * 128],
                        Ktok[:, cc * D + h * 64:cc * D + (h + 1) * 64],
                        Abuf[:, acol[h]:acol[h] + 128],
                        start=first, stop=True),
                        [f"ktok{cc}", f"Ab{acol[h] // 512}"],
                        [f"PBy{pc}_{hp}", f"BPB{pc // 4}"])
                for h in range(H if debug_stop not in (5, 12, 13) else 0):
                    hp, pc = h % 2, h // 2
                    T_(lambda g, hp=hp, pc=pc, cc=cc, h=h: g.matmul(
                        PD[hp * 64:(hp + 1) * 64, pc * 64:(pc + 1) * 64],
                        Vtbuf[:, h * 64:(h + 1) * 64],
                        Ktok[:, cc * D + h * 64:cc * D + (h + 1) * 64],
                        start=True, stop=True),
                        [f"Vt{pc}", f"ktok{cc}"],
                        [f"PDs{h}", "BPD"] + (["PDq", "PDv"] if h == 0 else []))
                for h in range(H if debug_stop not in (5, 12, 13) else 0):
                    hp, pc = h % 2, h // 2
                    V_(lambda g, hp=hp, pc=pc, cc=cc, h=h: g.scalar_tensor_tensor(
                        Sf32[hp * 64:(hp + 1) * 64, pc * 64:(pc + 1) * 64],
                        Sf32[hp * 64:(hp + 1) * 64, pc * 64:(pc + 1) * 64],
                        pCcols[hp * 64:(hp + 1) * 64, cc * 16 + h:cc * 16 + h + 1],
                        PD[hp * 64:(hp + 1) * 64, pc * 64:(pc + 1) * 64],
                        ALU.mult, ALU.add),
                        [f"pCcols{cc}", f"PDs{h}", "BPD", "Sf32"], ["Sf32"])
                yp = 0
                for pc in range(8 if debug_stop not in (5, 12, 13) else 0):
                    SC(lambda g, pc=pc, yp=yp: g.copy(
                        Yfm[:, yp * 1024 + pc * 128:yp * 1024 + (pc + 1) * 128],
                        PB[:, pc * 128:(pc + 1) * 128]),
                        [f"PBy{pc}_0", f"PBy{pc}_1", f"BPB{pc // 4}"],
                        [f"yfm{yp}_{pc}"])
                if debug_stop in (12, 13):
                    continue
                SC(lambda g: g.copy(
                    Sbf2[0:64, :].rearrange("p (b two j) -> p b two j", two=2, j=64)
                    [:, :, 0, :],
                    Sf32[0:64, :].rearrange("p (b j) -> p b j", j=64)),
                   ["Sf32"], ["Sbf2"])
                SC(lambda g: g.copy(
                    Sbf2[64:128, :].rearrange("p (b two j) -> p b two j", two=2, j=64)
                    [:, :, 1, :],
                    Sf32[64:128, :].rearrange("p (b j) -> p b j", j=64)),
                   ["Sf32"], ["Sbf2"])
                for eh in range(2 if debug_stop not in (5, 6, 8, 9, 12, 13) else 0):
                    for kc in range(8):
                        T_(lambda g, eh=eh, kc=kc, yp=yp: g.matmul(
                            PA[:, eh * 512:(eh + 1) * 512],
                            Yfm[:, yp * 1024 + kc * 128:yp * 1024 + (kc + 1) * 128],
                            wo_sb[:, kc * D + eh * 512:kc * D + (eh + 1) * 512],
                            start=(kc == 0), stop=(kc == 7)),
                            [f"yfm{yp}_{kc}", f"w3_{kc}"],
                            ["PA", f"BPA{eh}"])
                if debug_stop in (5, 6, 8, 9, 12, 13):
                    continue
                SC(lambda g: g.activation(aobuf[:], PA[:], AF.Copy, accum_out=st(4)),
                   ["PA", "BPA0", "BPA1"], ["ao", "lnaa"])
                SC(lambda g: g.activation(g2048f[:, 0:D], aobuf[:], AF.Square,
                                          accum_out=st(5)),
                   ["ao"], ["g2048a", "lnab"])
                ln_stats(4, 5, 6, D, "lna")
                SC(lambda g: g.activation(hnbuf[:], aobuf[:], AF.Identity, bias=st(7),
                                          scale=st(6)),
                   ["ao", "lnac", "lnad"], ["hn"])
                x2sl = X2[:, tt * D:(tt + 1) * D]
                V_(lambda g, x2sl=x2sl: g.tensor_tensor(x2sl, x2sl, hnbuf[:], ALU.add),
                   [f"x2:{tt}", "hn"], [f"x2:{tt}"])

    S.barrier()
    if debug_stop in (1, 3, 4, 5, 6, 8, 9, 12, 13):
        for tt in range(NTT):
            V_(lambda g, tt=tt: g.tensor_copy(outst_dbg[:], X2[:, tt * D:(tt + 1) * D]),
               [f"x2:{tt}", "odbg"], ["odbg"])
            dma("gpsimd", yd[tt * CH:tt * CH + 128, :], outst_dbg[:],
                ["odbg"], [f"ydone{tt}", "odbg"])
        S.emit("sync", None, reads=[f"ydone{tt}" for tt in range(NTT)], writes=[])
        S.emit("gpsimd", None, reads=[f"ydone{tt}" for tt in range(NTT)], writes=[])
        return nc
    p1.close()

    p2 = ExitStack()

    def sb2(name, shape, dt):
        return p2.enter_context(nc.sbuf_tensor("s_" + name, shape, dt))

    qw2sb = sb2("qw2sb", [128, 32 * D], BF16)
    wexp1 = sb2("wexp1", [128, D], F16)
    bexp1 = sb2("bexp1", [128, D], F16)
    wexp2 = sb2("wexp2", [128, HID], F16)
    bexp2 = sb2("bexp2", [128, HID], F16)
    H2 = sb2("H2", [128, 2 * HID], F16)
    qhtok = sb2("qhtok", [128, HID], BF16)
    qhfm = sb2("qhfm", [128, 2 * 512], BF16)
    qxtok = sb2("qxtok", [128, D], BF16)
    qxfm = sb2("qxfm", [128, 8 * GT], BF16)
    h1b = sb2("h1b", [128, D], F16)
    qw1buf = sb2("qw1buf", [128, 2 * 512], BF16)
    outstage = sb2("outstage", [128, D], F32)
    fobuf = sb2("fobuf", [128, D], F16)
    hn3buf = sb2("hn3buf", [128, D], BF16)
    screw = sb2("screw", [1, 2 * 128], BF16)
    screw2 = sb2("screw2", [1, 2 * 128], BF16)
    c1col = sb2("c1col", [128, 2], F32)
    c2col = sb2("c2col", [128, 2], F32)
    s2cols = sb2("s2cols", [128, 2], F32)
    gacc = sb2("gacc", [128, 2 * 8], F32)

    pxkeys = ["PX0", "PXw", "PXb", "PXb2", "BPX"]
    for (dst, row, n, key) in ((wexp1, 0, D, "wexp1"), (bexp1, 1, D, "bexp1"),
                               (wexp2, 2, HID, "wexp2"), (bexp2, 3, HID, "bexp2")):
        for j0 in range(0, n, 512):
            dma("sync", rstgf[:], ln_d[row:row + 1, j0:j0 + 512], [], ["rstgf"])
            V_(lambda g: g.tensor_copy(rstgb[:], rstgf[:]), ["rstgf"], ["rstgb"])
            T_(lambda g: g.matmul(PX[:, 0:512], onesbf[:], rstgb[:],
                                  start=True, stop=True),
               ["onesbf", "rstgb"], pxkeys)
            V_(lambda g, dst=dst, j0=j0: g.tensor_copy(dst[:, j0:j0 + 512],
                                                       PX[:, 0:512]),
               pxkeys, [key])

    for ch in range(32):
        dma("sync", qw2sb[:, ch * D:(ch + 1) * D], qw2_s[ch * 128:(ch + 1) * 128, :],
            ["wqstore"], [f"qw2r{ch}"])

    if debug_stop == 25:
        for tt in range(NTT):
            dma("sync", qhtok[:, 0:D], qw1_s[tt * 128:(tt + 1) * 128, 0:D],
                ["wqstore", "qhd"], ["qhq"])
            V_(lambda g: g.tensor_copy(outstage[:], qhtok[:, 0:D]),
               ["qhq", "ost"], ["ost", "qhd"])
            dma("gpsimd", yd[tt * CH:tt * CH + 128, :], outstage[:],
                ["ost"], [f"ydone{tt}", "ost"])
        S.emit("sync", None, reads=[f"ydone{tt}" for tt in range(NTT)], writes=[])
        S.emit("gpsimd", None, reads=[f"ydone{tt}" for tt in range(NTT)], writes=[])
        return nc
    NT2 = NTT // 2
    def emitA(g2):
            for tt2 in range(2):
                tt = g2 * 2 + tt2
                x2sl = X2[:, tt * D:(tt + 1) * D]
                V_(lambda g, x2sl=x2sl: g.tensor_scalar(
                    outstage[:], x2sl, 1.0, 0.0, ALU.mult, ALU.add,
                    accum_out=st(0)),
                    [f"x2:{tt}"], ["ost", "f1a"])
                SC(lambda g, x2sl=x2sl: g.activation(
                    g2048f[:, 0:D], x2sl, AF.Square, accum_out=st(1)),
                    [f"x2:{tt}"], ["g2048a", "f1b"])
                ln_stats(0, 1, 2, D, "f1")
                SC(lambda g, x2sl=x2sl: g.activation(h1b[:], x2sl, AF.Identity,
                                                     bias=st(3), scale=st(2)),
                   [f"x2:{tt}", "f1c", "f1d"], ["h1"])
                PL(lambda g: g.tensor_tensor(h1b[:], h1b[:], wexp1[:], ALU.mult),
                   ["h1", "wexp1"], ["h1"])
                PL(lambda g: g.tensor_tensor(h1b[:], h1b[:], bexp1[:], ALU.add),
                   ["h1", "bexp1"], ["h1"])
                V_(lambda g: g.tensor_reduce(st(8), h1b[:], AX.X, ALU.max,
                                             apply_absolute_value=True),
                   ["h1"], ["mx1"])
                V_(lambda g: g.tensor_scalar(st(8), st(8), 1e-5, None, ALU.max),
                   ["mx1"], ["mx1"])
                V_(lambda g: g.tensor_scalar(st(9), st(8), 1.0 / 127.0, None, ALU.mult),
                   ["mx1"], ["s1col"])
                V_(lambda g: g.reciprocal(st(9), st(9)), ["s1col"], ["s1col"])
                V_(lambda g: g.tensor_tensor(st(10), st(9), swcol[:, 0:1], ALU.mult),
                   ["s1col", "swcol0"], ["sc1f"])
                V_(lambda g, tt2=tt2: g.reciprocal(c1col[:, tt2:tt2 + 1], st(10)),
                   ["sc1f"], [f"c1col{tt2}"])
                PL(lambda g: g.tensor_copy(statb1[:], st(10)), ["sc1f"], ["statb1"])
                T_(lambda g: g.matmul(PX[0:1, 0:128], statb1[:], identb[:],
                                      start=True, stop=True),
                   ["statb1", "identb"], ["PX0", "BPX"])
                V_(lambda g, tt2=tt2: g.tensor_copy(
                    screw[:, tt2 * 128:(tt2 + 1) * 128], PX[0:1, 0:128]),
                    ["PX0", "BPX"], [f"sc1row{tt2}"])
                V_(lambda g: g.tensor_scalar(g2048f[:, 0:D], h1b[:], st(9), KT,
                                             ALU.mult, ALU.add),
                   ["h1", "s1col"], ["g2048a"])
                V_(lambda g: g.tensor_copy(qi32buf[:, 0:D], g2048f[:, 0:D]),
                   ["g2048a"], ["qi32"])
                V_(lambda g: g.scalar_tensor_tensor(
                    outstage[:], qi32buf[:, 0:D], 0.5, g2048f[:, 0:D],
                    ALU.add, ALU.is_le), ["qi32", "g2048a", "ost"], ["ost"])
                V_(lambda g: g.scalar_tensor_tensor(
                    qxtok[:], qi32buf[:, 0:D], -KT, outstage[:],
                    ALU.add, ALU.add), ["qi32", "ost"], ["qx"])
                if debug_stop in (22, 23):
                    tt_d = g2 * 2 + tt2
                    srcb = qxtok if debug_stop == 22 else h1b
                    V_(lambda g, srcb=srcb: g.tensor_copy(outstage[:], srcb[:]),
                       ["qx", "h1", "ost"], ["ost"])
                    dma("gpsimd", yd[tt_d * CH:tt_d * CH + 128, :], outstage[:],
                        ["ost"], [f"ydone{tt_d}", "ost"])
                    continue
                for half in range(2):
                    for kq in range(4):
                        kc = half * 4 + kq
                        T_(lambda g, kc=kc, kq=kq: g.matmul(
                            PE_[:, kq * 128:(kq + 1) * 128],
                            qxtok[:, kc * 128:(kc + 1) * 128], identb[:],
                            start=True, stop=True),
                            ["qx", "identb"], [f"PEq{kq}", "BPE"])
                    V_(lambda g, half=half, tt2=tt2: g.tensor_copy(
                        qxfm[:].rearrange("p (k t) -> p k t", k=8)
                        [:, half * 4:half * 4 + 4, tt2 * 128:(tt2 + 1) * 128],
                        PE_[:].rearrange("p (k t) -> p k t", k=4)),
                        [f"PEq{kq}" for kq in range(4)] + ["BPE"],
                        [f"qxfm{k2}_{tt2}" for k2 in range(half * 4, half * 4 + 4)])

    emitA(0)
    for g2 in range(NT2):
        if debug_stop in (22, 23):
            continue
        for hs in range(8):
            for kc in range(8):
                sl = (hs * 8 + kc) % 2
                dma("sync", qw1buf[:, sl * 512:(sl + 1) * 512],
                    qw1_s[kc * 128:(kc + 1) * 128, hs * 512:(hs + 1) * 512],
                    ["wqstore"], [f"qw1b{sl}"])
                for tt2 in range(2):
                    T_(lambda g, kc=kc, sl=sl, tt2=tt2: g.matmul(
                        PA[:, tt2 * 512:(tt2 + 1) * 512],
                        qxfm[:, kc * GT + tt2 * 128:kc * GT + (tt2 + 1) * 128],
                        qw1buf[:, sl * 512:(sl + 1) * 512],
                        start=(kc == 0), stop=False),
                        [f"qxfm{kc}_{tt2}", f"qw1b{sl}"],
                        [f"PAf{tt2}", f"BPA{tt2}"], acc=(kc != 0))
            dma("sync", rstgf[:], b1_d[0:1, hs * 512:(hs + 1) * 512],
                [], ["rstgf"])
            PL(lambda g: g.tensor_copy(rstgb[:], rstgf[:]), ["rstgf"], ["rstgb"])
            for tt2 in range(2):
                T_(lambda g, tt2=tt2: g.matmul(
                    PA[:, tt2 * 512:(tt2 + 1) * 512],
                    screw[0:1, tt2 * 128:(tt2 + 1) * 128],
                    rstgb[:], start=False, stop=True),
                    [f"sc1row{tt2}", "rstgb"], [f"PAf{tt2}", f"BPA{tt2}"], acc=True)
                SC(lambda g, hs=hs, tt2=tt2: g.activation(
                    H2[:, tt2 * HID + hs * 512:tt2 * HID + (hs + 1) * 512],
                    PA[:, tt2 * 512:(tt2 + 1) * 512], AF.Gelu,
                    scale=c1col[:, tt2:tt2 + 1],
                    accum_out=gacc[:, tt2 * 8 + hs:tt2 * 8 + hs + 1]),
                    [f"PAf{tt2}", f"BPA{tt2}", f"c1col{tt2}"],
                    [f"H2_{tt2}", f"gacc{tt2}_{hs}"])
        if debug_stop == 21:
            for tt2 in range(2):
                tt = g2 * 2 + tt2
                V_(lambda g, tt2=tt2: g.tensor_copy(
                    outstage[:], H2[:, tt2 * HID:tt2 * HID + D]),
                    [f"H2_{tt2}", "ost"], ["ost"])
                dma("gpsimd", yd[tt * CH:tt * CH + 128, :], outstage[:],
                    ["ost"], [f"ydone{tt}", "ost"])
            continue
        if g2 + 1 < NT2:
            emitA(g2 + 1)
        for tt2 in range(2):
            tt = g2 * 2 + tt2
            hsl = H2[:, tt2 * HID:(tt2 + 1) * HID]
            V_(lambda g, tt2=tt2: g.tensor_reduce(
                st(12), gacc[:, tt2 * 8:(tt2 + 1) * 8], AX.X, ALU.add),
                [f"gacc{tt2}_{hs}" for hs in range(8)], ["f2a"])
            for qtr in range(4):
                SC(lambda g, hsl=hsl, qtr=qtr: g.activation(
                    g2048f[:, 0:1024], hsl[:, qtr * 1024:(qtr + 1) * 1024],
                    AF.Square, accum_out=st(28 + qtr)),
                    [f"H2_{tt2}"], ["g2048a", f"f2sq{qtr}"])
            V_(lambda g: g.tensor_reduce(st(13), statp[:, 28:32], AX.X, ALU.add),
               [f"f2sq{q2}" for q2 in range(4)], ["f2b"])
            ln_stats(12, 13, 14, HID, "f2")
            SC(lambda g, hsl=hsl, tt2=tt2: g.activation(
                hsl, hsl, AF.Identity, bias=st(15), scale=st(14)),
                [f"H2_{tt2}", "f2c", "f2d"], [f"H2_{tt2}"])
            PL(lambda g, hsl=hsl, tt2=tt2: g.tensor_tensor(
                hsl, hsl, wexp2[:], ALU.mult), [f"H2_{tt2}", "wexp2"], [f"H2_{tt2}"])
            PL(lambda g, hsl=hsl, tt2=tt2: g.tensor_tensor(
                hsl, hsl, bexp2[:], ALU.add), [f"H2_{tt2}", "bexp2"], [f"H2_{tt2}"])
            V_(lambda g, hsl=hsl: g.tensor_reduce(
                st(24), hsl, AX.X, ALU.max, apply_absolute_value=True),
                [f"H2_{tt2}"], ["mx2"])
            V_(lambda g: g.tensor_scalar(st(24), st(24), 1e-5, None, ALU.max),
               ["mx2"], ["mx2"])
            V_(lambda g, tt2=tt2: g.tensor_scalar(
                s2cols[:, tt2:tt2 + 1], st(24), 1.0 / 127.0, None, ALU.mult),
               ["mx2"], [f"s2col{tt2}"])
            V_(lambda g, tt2=tt2: g.reciprocal(s2cols[:, tt2:tt2 + 1],
                                               s2cols[:, tt2:tt2 + 1]),
               [f"s2col{tt2}"], [f"s2col{tt2}"])
            V_(lambda g, tt2=tt2: g.tensor_tensor(st(26), s2cols[:, tt2:tt2 + 1],
                                                  swcol[:, 1:2], ALU.mult),
               [f"s2col{tt2}", "swcol1"], ["sc2f"])
            V_(lambda g, tt2=tt2: g.reciprocal(c2col[:, tt2:tt2 + 1], st(26)),
               ["sc2f"], [f"c2col{tt2}"])
            PL(lambda g: g.tensor_copy(statb1[:], st(26)), ["sc2f"], ["statb1"])
            T_(lambda g: g.matmul(PX[0:1, 0:128], statb1[:], identb[:],
                                  start=True, stop=True),
               ["statb1", "identb"], ["PX0", "BPX"])
            V_(lambda g, tt2=tt2: g.tensor_copy(
                screw2[:, tt2 * 128:(tt2 + 1) * 128], PX[0:1, 0:128]),
                ["PX0", "BPX"], [f"sc2row{tt2}"])
        for tt2 in range(2):
            tt = g2 * 2 + tt2
            hsl = H2[:, tt2 * HID:(tt2 + 1) * HID]
            for qtr in range(4):
                V_(lambda g, hsl=hsl, qtr=qtr, tt2=tt2: g.tensor_scalar(
                    g2048f[:, 0:1024], hsl[:, qtr * 1024:(qtr + 1) * 1024],
                    s2cols[:, tt2:tt2 + 1], KT, ALU.mult, ALU.add),
                    [f"H2_{tt2}", f"s2col{tt2}"], ["g2048a"])
                V_(lambda g: g.tensor_copy(qi32buf[:, 0:1024], g2048f[:, 0:1024]),
                   ["g2048a"], ["qi32"])
                V_(lambda g: g.scalar_tensor_tensor(
                    outstage[:], qi32buf[:, 0:1024], 0.5, g2048f[:, 0:1024],
                    ALU.add, ALU.is_le), ["qi32", "g2048a", "ost"], ["ost"])
                V_(lambda g, qtr=qtr: g.scalar_tensor_tensor(
                    qhtok[:, qtr * 1024:(qtr + 1) * 1024], qi32buf[:, 0:1024],
                    -KT, outstage[:], ALU.add, ALU.add),
                    ["qi32", "ost"], [f"qh{qtr // 2}"])
            for kb in range(8):
                for kq in range(4):
                    kc = kb * 4 + kq
                    T_(lambda g, kc=kc, kq=kq: g.matmul(
                        PE_[:, kq * 128:(kq + 1) * 128],
                        qhtok[:, kc * 128:(kc + 1) * 128], identb[:],
                        start=True, stop=True),
                        [f"qh{kc // 16}", "identb"], [f"PEq{kq}", "BPE"])
                sl2 = kb % 2
                if sl2 == 0:
                    V_(lambda g, sl2=sl2: g.tensor_copy(
                        qhfm[:, sl2 * 512:(sl2 + 1) * 512], PE_[:]),
                        [f"PEq{kq}" for kq in range(4)] + ["BPE"], [f"qhfm{sl2}"])
                else:
                    SC(lambda g, sl2=sl2: g.copy(
                        qhfm[:, sl2 * 512:(sl2 + 1) * 512], PE_[:]),
                        [f"PEq{kq}" for kq in range(4)] + ["BPE"], [f"qhfm{sl2}"])
                for kq in range(4):
                    kc = kb * 4 + kq
                    for eh in range(2):
                        PT = PC if eh == 0 else PD
                        T_(lambda g, kc=kc, eh=eh, PT=PT, sl2=sl2, kq=kq: g.matmul(
                            PT[:], qhfm[:, sl2 * 512 + kq * 128:sl2 * 512 + (kq + 1) * 128],
                            qw2sb[:, kc * D + eh * 512:kc * D + (eh + 1) * 512],
                            start=(kc == 0), stop=False),
                            [f"qhfm{sl2}", f"qw2r{kc}"],
                            [f"Pw2{eh}", "BPC" if eh == 0 else "BPD"],
                            acc=(kc != 0))
            for eh in range(2):
                PT = PC if eh == 0 else PD
                dma("sync", rstgf[:], b2_d[0:1, eh * 512:(eh + 1) * 512],
                    [], ["rstgf"])
                PL(lambda g: g.tensor_copy(rstgb[:], rstgf[:]), ["rstgf"], ["rstgb"])
                T_(lambda g, eh=eh, tt2=tt2, PT=PT: g.matmul(
                    PT[:], screw2[0:1, tt2 * 128:(tt2 + 1) * 128],
                    rstgb[:], start=False, stop=True),
                    [f"sc2row{tt2}", "rstgb"],
                    [f"Pw2{eh}", "BPC" if eh == 0 else "BPD"], acc=True)
                SC(lambda g, eh=eh, tt2=tt2, PT=PT: g.activation(
                    fobuf[:, eh * 512:(eh + 1) * 512], PT[:], AF.Copy,
                    scale=c2col[:, tt2:tt2 + 1], accum_out=st(16 + eh)),
                    [f"Pw2{eh}", "BPC" if eh == 0 else "BPD", f"c2col{tt2}"],
                    [f"fo{eh}", f"foacc{eh}"])
            V_(lambda g: g.tensor_tensor(st(18), st(16), st(17), ALU.add),
               ["foacc0", "foacc1"], ["f3a"])
            SC(lambda g: g.activation(g2048f[:, 0:D], fobuf[:], AF.Square,
                                      accum_out=st(19)),
               ["fo0", "fo1"], ["g2048a", "f3b"])
            ln_stats(18, 19, 20, D, "f3")
            SC(lambda g: g.activation(hn3buf[:], fobuf[:], AF.Identity, bias=st(21),
                                      scale=st(20)),
               ["fo0", "fo1", "f3c", "f3d"], ["hn3"])
            V_(lambda g, tt=tt: g.tensor_tensor(
                outstage[:], hn3buf[:], X2[:, tt * D:(tt + 1) * D], ALU.add),
                ["hn3", f"x2:{tt}"], ["ost"])
            dma("gpsimd", yd[tt * CH:tt * CH + 128, :], outstage[:],
                ["ost"], [f"ydone{tt}", "ost"])

    S.emit("sync", None, reads=[f"ydone{tt}" for tt in range(NTT)], writes=[])
    S.emit("gpsimd", None, reads=[f"ydone{tt}" for tt in range(NTT)], writes=[])
    p2.close()
    es.close()
    return nc


def kernel(**inputs):
    n_cores = 8
    B, Tfull, Dm = inputs["L0_attn_x"].shape
    n_seq = B // n_cores
    n_chunks = Tfull // CH

    x = np.ascontiguousarray(inputs["L0_attn_x"], dtype=np.float32)
    wqT = np.ascontiguousarray(np.asarray(inputs["L0_attn_Wq"]).T)
    wkT = np.ascontiguousarray(np.asarray(inputs["L0_attn_Wk"]).T)
    wvT = np.ascontiguousarray(np.asarray(inputs["L0_attn_Wv"]).T)
    woT = np.ascontiguousarray(np.asarray(inputs["L0_attn_Wo"]).T)
    w1T = np.ascontiguousarray(np.asarray(inputs["L0_ffn_w1"]).T)
    w2T = np.ascontiguousarray(np.asarray(inputs["L0_ffn_w2"]).T)
    gl = np.stack([inputs["L0_attn_W_gamma"], inputs["L0_attn_b_gamma"],
                   inputs["L0_attn_W_lambda"], inputs["L0_attn_b_lambda"]]
                  ).astype(np.float32)
    lnp = np.zeros((4, HID), np.float32)
    lnp[0, :Dm] = inputs["L0_ffn_ln1_w"]
    lnp[1, :Dm] = inputs["L0_ffn_ln1_b"]
    lnp[2, :] = inputs["L0_ffn_ln2_w"]
    lnp[3, :] = inputs["L0_ffn_ln2_b"]
    b1 = np.asarray(inputs["L0_ffn_b1"]).reshape(1, HID).astype(np.float32)
    b2 = np.asarray(inputs["L0_ffn_b2"]).reshape(1, Dm).astype(np.float32)

    identb = np.eye(128, dtype=ml_dtypes.bfloat16)
    identf = np.eye(128, dtype=np.float32)
    ieye2 = np.concatenate([np.eye(64), np.eye(64)], axis=0).astype(ml_dtypes.bfloat16)
    sidx = np.arange(128)[:, None]
    tidx = np.arange(128)[None, :]
    maskmin = np.where(sidx <= tidx, 0.0, -1e30).astype(np.float32)
    selpc = np.zeros((16, 8 * 128), np.float32)
    for pc in range(8):
        selpc[2 * pc, pc * 128:pc * 128 + 64] = 1.0
        selpc[2 * pc + 1, pc * 128 + 64:pc * 128 + 128] = 1.0
    selpc = selpc.astype(ml_dtypes.bfloat16)

    nc = build_program(n_seq, n_chunks)

    shared = {"wqT": wqT, "wkT": wkT, "wvT": wvT, "woT": woT, "w1T": w1T,
              "w2T": w2T, "gl": gl, "lnp": lnp, "b1": b1, "b2": b2,
              "identb": identb, "identf": identf, "ieye2": ieye2,
              "maskmin": maskmin, "selpc": selpc}
    in_maps = []
    for ci in range(n_cores):
        m = dict(shared)
        m["x"] = np.ascontiguousarray(
            x[ci * n_seq:(ci + 1) * n_seq].reshape(n_seq * Tfull, Dm))
        in_maps.append(m)
    res = run_bass_kernel_spmd(nc, in_maps, core_ids=list(range(n_cores)))
    out = np.empty((B, Tfull, Dm), np.float32)
    for ci in range(n_cores):
        out[ci * n_seq:(ci + 1) * n_seq] = np.asarray(
            res.results[ci]["y"]).reshape(n_seq, Tfull, Dm)
    return out


# version 17
# speedup vs baseline: 1.0600x; 1.0600x over previous
import numpy as np
import ml_dtypes
import concourse.bass as bass
import concourse.mybir as mybir
from concourse.bass_utils import run_bass_kernel_spmd
from contextlib import ExitStack

F32 = mybir.dt.float32
BF16 = mybir.dt.bfloat16
I32 = mybir.dt.int32
F16 = mybir.dt.float16
AF = mybir.ActivationFunctionType
ALU = mybir.AluOpType
AX = mybir.AxisListType

D = 1024
H = 16
DK = 64
HID = 4096
CH = 128
KT = 256.0
KW = 256.0

ENGS = ("tensor", "vector", "scalar", "sync", "gpsimd")
DMA_ENGS = {"sync", "gpsimd"}


class Sched:
    NSLOT = 8

    def __init__(self, nc, sems):
        self.nc = nc
        self.sems = sems
        self.idx = {}
        self.qcnt = {e: 0 for e in DMA_ENGS}
        self.last_w = {}
        self.readers = {}
        self.known = {e: {} for e in ENGS}
        self.snap = {}

    @staticmethod
    def phys(pe):
        if pe == "pool":
            return "gpsimd"
        return pe.split(":")[0]

    def _sem(self, pe):
        if ":" in pe:
            e, r = pe.split(":")
            return self.sems[e][int(r)]
        return self.sems[pe]

    def _wait(self, eng, kn, pe, di):
        if kn.get(pe, 0) >= di:
            return
        val = di * 16 if ":" in pe else di
        eng.wait_ge(self._sem(pe), val)
        sn = self.snap.get((pe, di))
        if sn is not None:
            for e3, v in sn.items():
                if v > kn.get(e3, 0):
                    kn[e3] = v
        if di > kn.get(pe, 0):
            kn[pe] = di

    def emit(self, e, fn, reads=(), writes=(), acc=False):
        eng = getattr(self.nc, "gpsimd" if e == "pool" else e)
        kn = self.known[self.phys(e)]
        if e in DMA_ENGS:
            r = self.qcnt[e] % self.NSLOT
            self.qcnt[e] += 1
            pe = f"{e}:{r}"
        else:
            pe = e
        deps = []
        for k in reads:
            w = self.last_w.get(k)
            if w is not None:
                deps.append(w)
        for k in writes:
            w = self.last_w.get(k)
            if w is not None:
                deps.append(w)
            for rd in self.readers.get(k, ()):
                deps.append(rd)
        if e in DMA_ENGS and self.idx.get(pe, 0) > 0:
            deps.append((pe, self.idx[pe]))
        myidx = self.idx.get(pe, 0) + 1
        need = {}
        for (de, di) in deps:
            if de == pe and di >= myidx:
                continue
            if acc and de == "tensor" and e == "tensor":
                continue
            if de not in need or need[de] < di:
                need[de] = di
        order = ([de for de in need if self.phys(de) != e]
                 + [de for de in need if self.phys(de) == e])
        for de in order:
            self._wait(eng, kn, de, need[de])
        if fn is None:
            return None
        inst = fn(eng)
        inst.then_inc(self._sem(pe), 16 if e in DMA_ENGS else 1)
        self.idx[pe] = myidx
        self.snap[(pe, myidx)] = dict(kn)
        op = (pe, myidx)
        for k in writes:
            self.last_w[k] = op
            self.readers[k] = []
        for k in reads:
            self.readers.setdefault(k, []).append(op)
        return op

    def barrier(self):
        tails = dict(self.idx)
        for e in ENGS:
            eng = getattr(self.nc, e)
            kn = self.known[e]
            for pe, di in tails.items():
                if self.phys(pe) == e or di == 0:
                    continue
                self._wait(eng, kn, pe, di)


def build_program(n_seq, n_chunks, debug_stop=0):
    T = n_chunks * CH
    NTOK = n_seq * T
    NTT = n_seq * n_chunks
    GRP = 2
    assert n_chunks % GRP == 0
    GT = GRP * CH

    nc = bass.Bass("TRN2", target_bir_lowering=False)

    xd = nc.dram_tensor("x", [NTOK, D], F32, kind="ExternalInput").ap()
    wT_d = {nm: nc.dram_tensor(nm, [D, D], F32, kind="ExternalInput").ap()
            for nm in ("wqT", "wkT", "wvT", "woT")}
    w1T_d = nc.dram_tensor("w1T", [D, HID], F32, kind="ExternalInput").ap()
    w2T_d = nc.dram_tensor("w2T", [HID, D], F32, kind="ExternalInput").ap()
    gl_d = nc.dram_tensor("gl", [4, H], F32, kind="ExternalInput").ap()
    ln_d = nc.dram_tensor("lnp", [4, HID], F32, kind="ExternalInput").ap()
    b1_d = nc.dram_tensor("b1", [1, HID], F32, kind="ExternalInput").ap()
    b2_d = nc.dram_tensor("b2", [1, D], F32, kind="ExternalInput").ap()
    identb_d = nc.dram_tensor("identb", [128, 128], BF16, kind="ExternalInput").ap()
    identf_d = nc.dram_tensor("identf", [128, 128], F32, kind="ExternalInput").ap()
    ieye2_d = nc.dram_tensor("ieye2", [128, 64], BF16, kind="ExternalInput").ap()
    maskmin_d = nc.dram_tensor("maskmin", [128, 128], F32, kind="ExternalInput").ap()
    selpc_d = nc.dram_tensor("selpc", [16, 8 * 128], BF16, kind="ExternalInput").ap()
    yd = nc.dram_tensor("y", [NTOK, D], F32, kind="ExternalOutput").ap()
    qw1_s = nc.dram_tensor("qw1s", [D, HID], BF16, kind="Internal").ap()
    qw2_s = nc.dram_tensor("qw2s", [HID, D], BF16, kind="Internal").ap()

    es = ExitStack()

    def sb(name, shape, dt):
        return es.enter_context(nc.sbuf_tensor("s_" + name, shape, dt))

    def pt(name, shape, dt=F32):
        return es.enter_context(nc.psum_tensor(name, shape, dt))

    sems = {}
    for e in ENGS:
        if e in DMA_ENGS:
            sems[e] = [es.enter_context(nc.semaphore(f"sem_{e}{r}"))
                       for r in range(Sched.NSLOT)]
        else:
            sems[e] = es.enter_context(nc.semaphore("sem_" + e))
    sems["pool"] = es.enter_context(nc.semaphore("sem_pool"))
    S = Sched(nc, sems)

    X2 = sb("X2", [128, NTT * D], BF16)
    identb = sb("identb", [128, 128], BF16)
    ones1f = sb("ones1f", [1, 128], F32)
    onesbf = sb("onesbf", [1, 128], BF16)
    onescolf = sb("onescolf", [128, 1], F32)
    statp = sb("statp", [128, 64], F32)
    statb1 = sb("statb1", [128, 1], BF16)
    g2048f = sb("g2048f", [128, 1024], F32)
    qi32buf = sb("qi32buf", [128, 1024], I32)
    swcol = sb("swcol", [128, 2], F32)
    rstgf = sb("rstgf", [1, 512], F32)
    rstgb = sb("rstgb", [1, 512], BF16)
    e24col = sb("e24col", [128, 1], F32)
    ktcol = sb("ktcol", [128, 1], F32)
    e30col = sb("e30col", [128, 1], F32)
    nKTcol = sb("nKTcol", [128, 1], F32)
    nKWcol = sb("nKWcol", [128, 1], F32)

    PA = pt("PA", [128, 1024])
    PB = pt("PB", [128, 1024])
    PC = pt("PC", [128, 512])
    PD = pt("PD", [128, 512])
    PE_ = pt("PE", [128, 512])
    PX = pt("PX", [128, 512])
    PD_ALL = ["PDq", "PDv"] + [f"PDs{h}" for h in range(H)]

    def dma(q, out, in_, rk, wk):
        return S.emit(q, lambda g: g.dma_start(out, in_), reads=rk, writes=wk)

    def T_(fn, rk, wk, acc=True):
        return S.emit("tensor", fn, reads=rk, writes=wk, acc=True)

    def V_(fn, rk, wk):
        return S.emit("vector", fn, reads=rk, writes=wk)

    def SC(fn, rk, wk):
        return S.emit("scalar", fn, reads=rk, writes=wk)

    def PL(fn, rk, wk):
        return S.emit("pool", fn, reads=rk, writes=wk)

    def st(i, n=1):
        return statp[:, i:i + n]

    def ln_stats(a, b, c, N, tag):
        V_(lambda g: g.tensor_scalar(st(a), st(a), 1.0 / N, None, ALU.mult),
           [f"{tag}a"], [f"{tag}a"])
        V_(lambda g: g.tensor_scalar(st(b), st(b), 1.0 / N, None, ALU.mult),
           [f"{tag}b"], [f"{tag}b"])
        V_(lambda g: g.scalar_tensor_tensor(st(b), st(a), st(a), st(b),
                                            ALU.mult, ALU.subtract),
           [f"{tag}a", f"{tag}b"], [f"{tag}b"])
        V_(lambda g: g.tensor_scalar(st(b), st(b), -1.0, 1e-5, ALU.mult, ALU.add),
           [f"{tag}b"], [f"{tag}b"])
        SC(lambda g: g.activation(st(c), st(b), AF.Sqrt),
           [f"{tag}b"], [f"{tag}c"])
        V_(lambda g: g.reciprocal(st(c), st(c)), [f"{tag}c"], [f"{tag}c"])
        V_(lambda g: g.scalar_tensor_tensor(st(c + 1), st(a), -1.0, st(c),
                                            ALU.mult, ALU.mult),
           [f"{tag}a", f"{tag}c"], [f"{tag}d"])

    dma("sync", identb[:], identb_d, [], ["identb"])
    V_(lambda g: g.memset(ones1f[:], 1.0), [], ["ones1f"])
    V_(lambda g: g.memset(onesbf[:], 1.0), [], ["onesbf"])
    V_(lambda g: g.memset(onescolf[:], 1.0), [], ["onescolf"])
    V_(lambda g: g.memset(e24col[:], 1e-24), [], ["ccol"])
    V_(lambda g: g.memset(ktcol[:], KT), [], ["ktcol"])
    V_(lambda g: g.memset(e30col[:], 1e-30), [], ["ccol"])
    V_(lambda g: g.memset(nKTcol[:], -KT), [], ["ccol"])
    V_(lambda g: g.memset(nKWcol[:], -KW), [], ["ccol"])

    p1 = ExitStack()

    def sb1(name, shape, dt):
        return p1.enter_context(nc.sbuf_tensor("s_" + name, shape, dt))

    identf = sb1("identf", [128, 128], F32)
    maskmin = sb1("maskmin", [128, 128], F32)
    selpc = sb1("selpc", [16, 8 * 128], BF16)
    zeros16 = sb1("zeros16", [16, 128], F32)
    glb = sb1("glb", [128, 4 * H], F32)
    glrow = sb1("glrow", [1, 4 * H], F32)
    dma("sync", identf[:], identf_d, [], ["identf"])
    dma("sync", maskmin[:], maskmin_d, [], ["maskmin"])
    dma("sync", selpc[:], selpc_d, [], ["selpc"])
    V_(lambda g: g.memset(zeros16[:], 0.0), [], ["zeros16"])
    dma("sync", glrow[:], gl_d.rearrange("a b -> (a b)").rearrange("(o n) -> o n", o=1),
        [], ["glrow"])
    for j in range(4):
        T_(lambda g, j=j: g.matmul(PX[:, j * 16:(j + 1) * 16], ones1f[:],
                                   glrow[0:1, j * 16:(j + 1) * 16],
                                   start=True, stop=True),
           ["ones1f", "glrow"], ["PX0", "BPX"])
    V_(lambda g: g.tensor_copy(glb[:], PX[:, 0:64]), ["PX0", "BPX"], ["glb"])

    wq_sb = sb1("wq_sb", [128, 8 * D], BF16)
    wk_sb = sb1("wk_sb", [128, 8 * D], BF16)
    wv_sb = sb1("wv_sb", [128, 8 * D], BF16)
    wo_sb = sb1("wo_sb", [128, 8 * D], BF16)
    wstage = sb1("wstage", [128, 1024], F32)
    wqout = sb1("wqout", [128, 1024], BF16)
    absacc = sb1("absacc", [128, 64], F32)

    xstage = sb1("xstage", [128, D], F32)
    lnxtok = sb1("lnxtok", [128, D], BF16)
    lnxfm = sb1("lnxfm", [128, 8 * GT], BF16)
    Qsb = sb1("Qsb", [128, 8 * GT], BF16)
    Qpsb = sb1("Qpsb", [128, 8 * GT], BF16)
    PexpS = sb1("PexpS", [128, GT], F32)
    Vsb = sb1("Vsb", [128, 8 * GT], BF16)
    Ktok = sb1("Ktok", [128, GRP * D], BF16)
    Equad = sb1("Equad", [128, 12 * CH], F32)
    Abuf = sb1("Abuf", [128, H * CH], BF16)
    Vtbuf = sb1("Vtbuf", [128, H * DK], BF16)
    Lbcbuf = sb1("Lbcbuf", [128, 8 * CH], F32)
    Yfm = sb1("Yfm", [128, 8 * CH], BF16)
    aobuf = sb1("aobuf", [128, D], BF16)
    hnbuf = sb1("hnbuf", [128, D], BF16)
    Sf32 = sb1("Sf32", [128, 8 * DK], F32)
    Sbf2 = sb1("Sbf2", [128, H * DK], BF16)
    wcol = sb1("wcol", [128, 16], F32)
    wtm = sb1("wtm", [128, 16], F32)
    llrow = sb1("llrow", [1, 16], F32)
    V_(lambda g: g.memset(Sbf2[:], 0.0), [], ["Sbf2"])
    LTf = sb1("LTf", [16, GRP * CH], F32)
    nLT = sb1("nLT", [16, CH], F32)
    negLtok = sb1("negLtok", [128, GRP * H], F32)
    expLTf = sb1("expLTf", [16, CH], F32)
    expLTb = sb1("expLTb", [16, GT], BF16)
    pCrow = sb1("pCrow", [1, 16], F32)
    pCcols = sb1("pCcols", [128, GRP * 16], F32)
    lrtok = sb1("lrtok", [128, 16], F32)
    dsc = sb1("dsc", [128, 16 * 8], F32)
    outst_dbg = sb1("outst_dbg", [128, D], F32) if debug_stop else None

    def dsl(i):
        return dsc[:, i * 16:(i + 1) * 16]

    KB, GA, LA, RHO, ONEMR, KFAC, T1, T2 = range(8)

    for wi, (nm, wsb) in enumerate((("wqT", wq_sb), ("wkT", wk_sb),
                                    ("wvT", wv_sb), ("woT", wo_sb))):
        for kc in range(8):
            dma("sync", wstage[:, 0:D],
                wT_d[nm][kc * 128:(kc + 1) * 128, :], [], ["wstg0"])
            if kc % 2 == 0:
                V_(lambda g, wsb=wsb, kc=kc: g.tensor_copy(
                    wsb[:, kc * D:(kc + 1) * D], wstage[:, 0:D]),
                    ["wstg0"], [f"w{wi}_{kc}"])
            else:
                SC(lambda g, wsb=wsb, kc=kc: g.copy(
                    wsb[:, kc * D:(kc + 1) * D], wstage[:, 0:D]),
                    ["wstg0"], [f"w{wi}_{kc}"])

    def wq_params(which):
        wd, qd, nchunk, cw = ((w1T_d, qw1_s, 8, HID) if which == 0
                              else (w2T_d, qw2_s, 32, D))
        return wd, qd, nchunk, cw // 1024, 1024, (0 if which == 0 else 32)

    def wquant_A(which, items):
        wd, qd, nchunk, nh, hw, acol = wq_params(which)
        for (ch, hf) in items:
            ai = ch * nh + hf
            dma("gpsimd", wstage[:, 0:hw], wd[ch * 128:(ch + 1) * 128,
                                              hf * hw:(hf + 1) * hw],
                [], ["wstg0"])
            V_(lambda g, ai=ai, hw=hw, acol=acol: g.tensor_reduce(
                absacc[:, acol + ai:acol + ai + 1], wstage[:, 0:hw], AX.X,
                ALU.add, apply_absolute_value=True),
                ["wstg0"], [f"aacc{which}_{ai}"])

    def wquant_S(which):
        wd, qd, nchunk, nh, hw, acol = wq_params(which)
        nacc = nchunk * nh
        V_(lambda g, nacc=nacc, acol=acol, which=which: g.tensor_reduce(
            st(60 + which), absacc[:, acol:acol + nacc], AX.X, ALU.add),
            [f"aacc{which}_{a2}" for a2 in range(nacc)], [f"csum{which}"])
        T_(lambda g, which=which: g.matmul(PX[0:1, 432 + which:433 + which],
                                           st(60 + which), onescolf[:],
                                           start=True, stop=True),
           [f"csum{which}", "onescolf"], ["PXw", "BPX"])
        nels = float(D * HID)
        V_(lambda g, which=which, nels=nels: g.tensor_scalar(
            st(56 + which)[0:1, :], PX[0:1, 432 + which:433 + which],
            1.0 / nels, 1e-5, ALU.mult, ALU.max), ["PXw", "BPX"], [f"wmean{which}"])
        V_(lambda g, which=which: g.reciprocal(
            st(58 + which)[0:1, :], st(56 + which)[0:1, :]),
            [f"wmean{which}"], [f"wsc{which}"])
        T_(lambda g, which=which: g.matmul(
            PX[:, 436 + which:437 + which], ones1f[:], st(58 + which)[0:1, :],
            start=True, stop=True), [f"wsc{which}", "ones1f"], ["PXw", "BPX"])
        V_(lambda g, which=which: g.tensor_copy(
            swcol[:, which:which + 1], PX[:, 436 + which:437 + which]),
            ["PXw", "BPX"], [f"swcol{which}"])

    def wquant_B(which, items):
        wd, qd, nchunk, nh, hw, acol = wq_params(which)
        for (ch, hf) in items:
            wsk = ["wstg0"]
            dma("gpsimd", wstage[:, 0:hw], wd[ch * 128:(ch + 1) * 128,
                                              hf * hw:(hf + 1) * hw],
                [], wsk)
            V_(lambda g, hw=hw, which=which: g.tensor_scalar(
                g2048f[:, 0:hw], wstage[:, 0:hw], swcol[:, which:which + 1],
                None, ALU.mult),
                wsk + [f"swcol{which}", "g2048a"], ["g2048a"])
            V_(lambda g, hw=hw: g.tensor_copy(qi32buf[:, 0:hw], g2048f[:, 0:hw]),
               ["g2048a"], ["qi32"])
            V_(lambda g, hw=hw: g.scalar_tensor_tensor(
                wstage[:, 0:hw], qi32buf[:, 0:hw], 0.5, g2048f[:, 0:hw],
                ALU.add, ALU.is_le), ["qi32", "g2048a"] + wsk, wsk)
            V_(lambda g, hw=hw: g.scalar_tensor_tensor(
                wqout[:, 0:hw], qi32buf[:, 0:hw], 0.0, wstage[:, 0:hw],
                ALU.add, ALU.add), ["qi32"] + wsk, ["wqo"])
            V_(lambda g, hw=hw: g.scalar_tensor_tensor(
                wstage[:, 0:hw], qi32buf[:, 0:hw], -0.5, g2048f[:, 0:hw],
                ALU.add, ALU.is_ge), ["qi32", "g2048a"] + wsk, wsk)
            V_(lambda g, hw=hw: g.tensor_tensor(
                wqout[:, 0:hw], wqout[:, 0:hw], wstage[:, 0:hw], ALU.subtract),
                ["wqo"] + wsk, ["wqo"])
            V_(lambda g, hw=hw: g.tensor_scalar(wqout[:, 0:hw], wqout[:, 0:hw],
                                                1.0, -1.0, ALU.min, ALU.max),
               ["wqo"], ["wqo"])
            dma("gpsimd", qd[ch * 128:(ch + 1) * 128, hf * hw:(hf + 1) * hw],
                wqout[:, 0:hw], ["wqo"], ["wqstore"])

    wq_items = {0: [(c, f) for c in range(8) for f in range(4)],
                1: [(c, 0) for c in range(32)]}

    if debug_stop == 2:
        for tt in range(NTT):
            dma("gpsimd", yd[tt * CH:tt * CH + 128, :], wstage[:, 0:D],
                ["wstg0"], [f"ydone{tt}"])
        S.emit("sync", None, reads=[f"ydone{tt}" for tt in range(NTT)], writes=[])
        S.emit("gpsimd", None, reads=[f"ydone{tt}" for tt in range(NTT)], writes=[])
        return nc
    n_grp_b = n_chunks // GRP
    for b in range(n_seq):
        V_(lambda g: g.memset(Sf32[:], 0.0), [], ["Sf32"])
        for gg in range(n_grp_b):
            gi = b * n_grp_b + gg
            ngi = n_seq * n_grp_b
            def slice_items(items, i, n):
                k = len(items)
                return items[(i * k) // n:((i + 1) * k) // n]
            if ngi == 1:
                wquant_A(0, wq_items[0]); wquant_A(1, wq_items[1])
                wquant_S(0); wquant_S(1)
                wquant_B(0, wq_items[0]); wquant_B(1, wq_items[1])
            else:
                half = ngi // 2
                if gi < half:
                    wquant_A(0, slice_items(wq_items[0], gi, half))
                    wquant_A(1, slice_items(wq_items[1], gi, half))
                if gi == half:
                    wquant_S(0)
                    wquant_S(1)
                if gi >= half:
                    i2 = gi - half
                    n2 = ngi - half
                    wquant_B(0, slice_items(wq_items[0], i2, n2))
                    wquant_B(1, slice_items(wq_items[1], i2, n2))
            c0 = gg * GRP
            for cc in range(GRP):
                c = c0 + cc
                tt = b * n_chunks + c
                xs = xstage[:, 0:D]
                xrow0 = b * T + c * CH
                dma("sync", xs, xd[xrow0:xrow0 + 128, :], [], ["xstg0"])
                x2sl = X2[:, tt * D:(tt + 1) * D]
                V_(lambda g, xs=xs, x2sl=x2sl: g.tensor_scalar(
                    x2sl, xs, 1.0, 0.0, ALU.mult, ALU.add, accum_out=st(0)),
                    ["xstg0"], [f"x2:{tt}", "ln1a"])
                SC(lambda g, xs=xs: g.activation(
                    g2048f[:, 0:D], xs, AF.Square, accum_out=st(1)),
                    ["xstg0"], ["g2048a", "ln1b"])
                ln_stats(0, 1, 2, D, "ln1")
                lnx = lnxtok[:, 0:D]
                SC(lambda g, x2sl=x2sl, lnx=lnx: g.activation(
                    lnx, x2sl, AF.Identity, bias=st(3), scale=st(2)),
                    [f"x2:{tt}", "ln1c", "ln1d"], ["lnx0"])
                for half in range(2):
                    for kq in range(4):
                        kc = half * 4 + kq
                        T_(lambda g, lnx=lnx, kc=kc, kq=kq: g.matmul(
                            PE_[:, kq * 128:(kq + 1) * 128],
                            lnx[:, kc * 128:(kc + 1) * 128], identb[:],
                            start=True, stop=True),
                            ["lnx0", "identb"], [f"PEq{kq}", "BPE"])
                    V_(lambda g, half=half, cc=cc: g.tensor_copy(
                        lnxfm[:].rearrange("p (k t) -> p k t", k=8)
                        [:, half * 4:half * 4 + 4, cc * 128:(cc + 1) * 128],
                        PE_[:].rearrange("p (k t) -> p k t", k=4)),
                        [f"PEq{kq}" for kq in range(4)] + ["BPE"],
                        [f"lnxfm{k2}_{cc}" for k2 in range(half * 4, half * 4 + 4)])
                for eh in range(2):
                    for kc in range(8):
                        T_(lambda g, eh=eh, kc=kc, cc=cc: g.matmul(
                            PA[:, eh * 512:(eh + 1) * 512],
                            lnxfm[:, kc * GT + cc * 128:kc * GT + (cc + 1) * 128],
                            wk_sb[:, kc * D + eh * 512:kc * D + (eh + 1) * 512],
                            start=(kc == 0), stop=(kc == 7)),
                            [f"lnxfm{kc}_{cc}", f"w1_{kc}"],
                            ["PA", f"BPA{eh}"], acc=(kc != 0 or eh != 0))
                SC(lambda g: g.activation(g2048f[:, 0:D], PA[:], AF.Square),
                   ["PA", "BPA0", "BPA1"], ["g2048a"])
                V_(lambda g: g.tensor_reduce(
                    dsl(T1), g2048f[:, 0:D].rearrange("p (h j) -> p h j", h=16),
                    AX.X, ALU.add), ["g2048a"], ["dT1"])
                V_(lambda g: g.tensor_reduce(
                    dsl(T2), PA[:].rearrange("p (h j) -> p h j", h=16),
                    AX.X, ALU.add), ["PA", "BPA0", "BPA1"], ["dT2"])
                SC(lambda g: g.activation(dsl(KFAC), dsl(T1), AF.Sqrt,
                                          bias=e24col[:]),
                   ["dT1", "ccol"], ["dKF"])
                V_(lambda g: g.reciprocal(dsl(KFAC), dsl(KFAC)), ["dKF"], ["dKF"])
                V_(lambda g: g.scalar_tensor_tensor(
                    dsl(KB), dsl(T2), 1.0 / 64.0, dsl(KFAC), ALU.mult, ALU.mult),
                    ["dT2", "dKF"], ["dKB"])
                V_(lambda g: g.tensor_tensor(dsl(T1), dsl(KB), glb[:, 0:16], ALU.mult),
                   ["dKB", "glb"], ["dT1"])
                V_(lambda g: g.tensor_tensor(dsl(T1), dsl(T1), glb[:, 16:32], ALU.add),
                   ["dT1", "glb"], ["dT1"])
                SC(lambda g: g.activation(dsl(GA), dsl(T1), AF.Sigmoid), ["dT1"], ["dGA"])
                V_(lambda g: g.tensor_tensor(dsl(T2), dsl(KB), glb[:, 32:48], ALU.mult),
                   ["dKB", "glb"], ["dT2"])
                V_(lambda g: g.tensor_tensor(dsl(T2), dsl(T2), glb[:, 48:64], ALU.add),
                   ["dT2", "glb"], ["dT2"])
                SC(lambda g: g.activation(dsl(LA), dsl(T2), AF.Sigmoid), ["dT2"], ["dLA"])
                V_(lambda g: g.tensor_scalar(dsl(T2), dsl(LA), -1.0, 1.0,
                                             ALU.mult, ALU.add), ["dLA"], ["dT2"])
                V_(lambda g: g.tensor_tensor(dsl(RHO), dsl(T2), dsl(GA), ALU.mult),
                   ["dT2", "dGA"], ["dRHO"])
                V_(lambda g: g.tensor_scalar(dsl(ONEMR), dsl(RHO), -1.0, 1.0,
                                             ALU.mult, ALU.add), ["dRHO"], ["dOM"])
                SC(lambda g: g.activation(lrtok[:], dsl(RHO), AF.Ln,
                                          bias=e30col[:]),
                   ["dRHO", "ccol"], ["lrtok"])
                V_(lambda g: g.scalar_tensor_tensor(dsl(KFAC), dsl(KFAC), 1.0,
                                                    dsl(ONEMR), ALU.mult, ALU.mult),
                   ["dKF", "dOM"], ["dKF"])
                V_(lambda g, cc=cc: g.tensor_tensor(
                    Ktok[:, cc * D:(cc + 1) * D].rearrange("p (h j) -> p h j", h=16),
                    PA[:].rearrange("p (h j) -> p h j", h=16),
                    dsl(KFAC).rearrange("p (h o) -> p h o", o=1)
                    .broadcast_to([128, 16, 64]),
                    ALU.mult), ["PA", "BPA0", "BPA1", "dKF"], [f"ktok{cc}"])
                T_(lambda g: g.matmul(PX[0:16, 0:128], lrtok[:], identf[:],
                                      start=True, stop=True),
                   ["lrtok", "identf"], ["PX0", "BPX"])
                V_(lambda g, cc=cc: g.tensor_tensor_scan(
                    LTf[:, cc * CH:(cc + 1) * CH], PX[0:16, 0:128], zeros16[:],
                    0.0, ALU.add, ALU.add),
                    ["PX0", "BPX", "zeros16"], [f"LTf{cc}"])
                V_(lambda g, cc=cc: g.tensor_scalar(
                    nLT[:], LTf[:, cc * CH:(cc + 1) * CH], -1.0, None, ALU.mult),
                    [f"LTf{cc}"], ["nLT"])
                T_(lambda g: g.matmul(PX[0:128, 128:144], nLT[:], identf[0:16, 0:16],
                                      start=True, stop=True),
                   ["nLT", "identf"], ["PX0", "BPX"])
                V_(lambda g, cc=cc: g.tensor_copy(
                    negLtok[:, cc * H:(cc + 1) * H], PX[0:128, 128:144]),
                    ["PX0", "BPX"], [f"negL{cc}"])
                SC(lambda g, cc=cc: g.activation(
                    expLTf[:], LTf[:, cc * CH:(cc + 1) * CH], AF.Exp),
                    [f"LTf{cc}"], ["expLTf"])
                V_(lambda g, cc=cc: g.tensor_copy(
                    expLTb[:, cc * CH:(cc + 1) * CH], expLTf[:]),
                    ["expLTf"], [f"expLTb{cc}"])
                T_(lambda g: g.matmul(PX[0:1, 144:160], expLTf[:, 127:128],
                                      identf[0:16, 0:16], start=True, stop=True),
                   ["expLTf", "identf"], ["PX0", "BPX"])
                V_(lambda g: g.tensor_copy(pCrow[:], PX[0:1, 144:160]),
                   ["PX0", "BPX"], ["pCrow"])
                T_(lambda g: g.matmul(PX[0:128, 160:176], ones1f[:], pCrow[:],
                                      start=True, stop=True),
                   ["pCrow", "ones1f"], ["PX0", "BPX"])
                V_(lambda g, cc=cc: g.tensor_copy(
                    pCcols[:, cc * 16:(cc + 1) * 16], PX[0:128, 160:176]),
                    ["PX0", "BPX"], [f"pCcols{cc}"])
            for pc in range(8 if debug_stop not in (3,) else 0):
                T_(lambda g, pc=pc: g.matmul(
                    PX[:, 176:176 + GT], selpc[:, pc * 128:(pc + 1) * 128],
                    expLTb[:], start=True, stop=True),
                    [f"expLTb{c2}" for c2 in range(GRP)] + ["selpc"], ["PXb", "BPX"])
                for kc in range(8):
                    T_(lambda g, kc=kc, pc=pc: g.matmul(
                        PD[0:128, 0:GT],
                        wq_sb[:, kc * D + pc * 128:kc * D + (pc + 1) * 128],
                        lnxfm[:, kc * GT:(kc + 1) * GT],
                        start=(kc == 0), stop=(kc == 7)),
                        [f"lnxfm{kc}_{c2}" for c2 in range(GRP)] + [f"w0_{kc}"],
                        ["PDq", "BPD"] + (PD_ALL if kc == 0 else []), acc=(kc != 0))
                V_(lambda g: g.tensor_copy(PexpS[:], PX[:, 176:176 + GT]),
                   ["PXb", "BPX"], ["PexpS"])
                V_(lambda g, pc=pc: g.tensor_tensor(
                    Qpsb[:, pc * GT:(pc + 1) * GT], PD[0:128, 0:GT], PexpS[:],
                    ALU.mult), ["PDq", "BPD", "PexpS"], [f"qp{pc}"])
                V_(lambda g, pc=pc: g.tensor_copy(
                    Qsb[:, pc * GT:(pc + 1) * GT], PD[0:128, 0:GT]),
                    ["PDq", "BPD"], [f"q{pc}"])
                for kc in range(8):
                    T_(lambda g, kc=kc, pc=pc: g.matmul(
                        PD[0:128, 0:GT],
                        wv_sb[:, kc * D + pc * 128:kc * D + (pc + 1) * 128],
                        lnxfm[:, kc * GT:(kc + 1) * GT],
                        start=(kc == 0), stop=(kc == 7)),
                        [f"lnxfm{kc}_{c2}" for c2 in range(GRP)] + [f"w2_{kc}"],
                        ["PDv", "BPD"] + (PD_ALL if kc == 0 else []), acc=(kc != 0))
                SC(lambda g, pc=pc: g.copy(
                    Vsb[:, pc * GT:(pc + 1) * GT], PD[0:128, 0:GT]),
                    ["PDv", "BPD"], [f"v{pc}"])
            QUADS = [[0, 2, 4, 6], [8, 10, 12, 14], [1, 3, 5, 7], [9, 11, 13, 15]]
            acol = {}
            for q in range(4):
                for j, h in enumerate(QUADS[q]):
                    acol[h] = q * 512 + j * 128
            for cc in range(GRP if debug_stop not in (3, 4) else 0):
                c = c0 + cc
                tt = b * n_chunks + c
                T_(lambda g, cc=cc: g.matmul(
                    PX[0:1, 144:160], LTf[:, cc * CH + 127:cc * CH + 128],
                    identf[0:16, 0:16], start=True, stop=True),
                    [f"LTf{cc}", "identf"], ["PX0", "BPX"])
                V_(lambda g: g.tensor_copy(llrow[:], PX[0:1, 144:160]),
                   ["PX0", "BPX"], ["llrow"])
                T_(lambda g: g.matmul(PX[0:128, 160:176], ones1f[:], llrow[:],
                                      start=True, stop=True),
                   ["llrow", "ones1f"], ["PX0", "BPX"])
                V_(lambda g, cc=cc: g.tensor_tensor(
                    wtm[:], PX[0:128, 160:176],
                    negLtok[:, cc * H:(cc + 1) * H], ALU.add),
                    ["PX0", "BPX", f"negL{cc}"], ["wtm"])
                SC(lambda g: g.activation(wcol[:], wtm[:], AF.Exp),
                   ["wtm"], ["wcol"])
                def emit_E(q, cc=cc):
                    heads = QUADS[q]
                    q2 = (cc * 4 + q) % 3
                    eo = q2 * 512
                    l2 = (cc * 4 + q) % 2
                    lo = l2 * 512
                    for j, h in enumerate(heads):
                        dma("gpsimd", Lbcbuf[:, lo + j * 128:lo + (j + 1) * 128],
                            LTf[h:h + 1, cc * CH:(cc + 1) * CH]
                            .rearrange("p (o f) -> p o f", o=1)
                            .broadcast_to([1, 128, 128]),
                            [f"LTf{cc}"], [f"lbc{l2}_{j}"])
                    V_(lambda g, lo=lo: g.scalar_tensor_tensor(
                        Lbcbuf[:, lo:lo + 512].rearrange("p (o f) -> p o f", o=4),
                        Lbcbuf[:, lo:lo + 512].rearrange("p (o f) -> p o f", o=4),
                        1.0,
                        maskmin[:].rearrange("p (o f) -> p o f", o=1)
                        .broadcast_to([128, 4, 128]),
                        ALU.mult, ALU.min),
                        [f"lbc{l2}_{j2}" for j2 in range(4)] + ["maskmin"],
                        [f"lbc{l2}_{j2}" for j2 in range(4)])
                    for j, h in enumerate(heads):
                        SC(lambda g, h=h, j=j, eo=eo, lo=lo: g.activation(
                            Equad[:, eo + j * 128:eo + (j + 1) * 128],
                            Lbcbuf[:, lo + j * 128:lo + (j + 1) * 128], AF.Exp,
                            bias=negLtok[:, cc * H + h:cc * H + h + 1]),
                            [f"lbc{l2}_{j}", f"negL{cc}"], [f"Eq{q2}_{j}"])
                for q in range(3):
                    emit_E(q)
                if debug_stop == 8:
                    continue
                for q in range(4):
                    heads = QUADS[q]
                    qp = 0 if q < 2 else 1
                    APS = PE_ if qp == 0 else PC
                    akey = "PEq" if qp == 0 else "PCo"
                    bkey = "BPE" if qp == 0 else "BPC"
                    q2 = (cc * 4 + q) % 3
                    eo = q2 * 512
                    for j, h in enumerate(heads):
                        hp, pc = h % 2, h // 2
                        T_(lambda g, hp=hp, pc=pc, j=j, cc=cc, APS=APS: g.matmul(
                            APS[:, j * 128:(j + 1) * 128],
                            Vsb[hp * 64:(hp + 1) * 64,
                                pc * GT + cc * 128:pc * GT + (cc + 1) * 128],
                            Qsb[hp * 64:(hp + 1) * 64,
                                pc * GT + cc * 128:pc * GT + (cc + 1) * 128],
                            start=True, stop=True),
                            [f"v{pc}", f"q{pc}"], [f"{akey}{j}", bkey])
                    V_(lambda g, q=q, APS=APS, eo=eo: g.tensor_tensor(
                        Abuf[:, q * 512:(q + 1) * 512], APS[:],
                        Equad[:, eo:eo + 512], ALU.mult),
                        [f"{akey}{j2}" for j2 in range(4)] + [bkey]
                        + [f"Eq{q2}_{j2}" for j2 in range(4)],
                        [f"Ab{q}"])
                    if q == 0:
                        emit_E(3)
                if debug_stop in (8, 9):
                    continue
                for pc in range(8 if debug_stop != 12 else 0):
                    sl = pc % 2
                    T_(lambda g, pc=pc, cc=cc, sl=sl: g.matmul(
                        PX[:, 176 + sl * 128:176 + (sl + 1) * 128],
                        Vsb[:, pc * GT + cc * 128:pc * GT + (cc + 1) * 128],
                        identb[:], start=True, stop=True),
                        [f"v{pc}", "identb"], ["PXb" if sl == 0 else "PXb2", "BPX"])
                    V_(lambda g, pc=pc, sl=sl: g.tensor_tensor(
                        Vtbuf[:, pc * 128:(pc + 1) * 128]
                        .rearrange("p (h j) -> p h j", h=2),
                        PX[:, 176 + sl * 128:176 + (sl + 1) * 128]
                        .rearrange("p (h j) -> p h j", h=2),
                        wcol[:, 2 * pc:2 * pc + 2]
                        .rearrange("p (h o) -> p h o", o=1)
                        .broadcast_to([128, 2, 64]),
                        ALU.mult),
                        ["PXb" if sl == 0 else "PXb2", "BPX", "wcol"], [f"Vt{pc}"])
                for h in range(H if debug_stop not in (5, 12, 13) else 0):
                    hp, pc = h % 2, h // 2
                    first = (c == 0)
                    if not first:
                        T_(lambda g, hp=hp, pc=pc, cc=cc, h=h: g.matmul(
                            PB[hp * 64:(hp + 1) * 64, pc * 128:(pc + 1) * 128],
                            Sbf2[:, h * 64:(h + 1) * 64],
                            Qpsb[:, pc * GT + cc * 128:pc * GT + (cc + 1) * 128],
                            start=True, stop=False),
                            ["Sbf2", f"qp{pc}"], [f"PBy{pc}_{hp}", f"BPB{pc // 4}"])
                    T_(lambda g, hp=hp, pc=pc, cc=cc, h=h, first=first: g.matmul(
                        PB[hp * 64:(hp + 1) * 64, pc * 128:(pc + 1) * 128],
                        Ktok[:, cc * D + h * 64:cc * D + (h + 1) * 64],
                        Abuf[:, acol[h]:acol[h] + 128],
                        start=first, stop=True),
                        [f"ktok{cc}", f"Ab{acol[h] // 512}"],
                        [f"PBy{pc}_{hp}", f"BPB{pc // 4}"])
                for h in range(H if debug_stop not in (5, 12, 13) else 0):
                    hp, pc = h % 2, h // 2
                    T_(lambda g, hp=hp, pc=pc, cc=cc, h=h: g.matmul(
                        PD[hp * 64:(hp + 1) * 64, pc * 64:(pc + 1) * 64],
                        Vtbuf[:, h * 64:(h + 1) * 64],
                        Ktok[:, cc * D + h * 64:cc * D + (h + 1) * 64],
                        start=True, stop=True),
                        [f"Vt{pc}", f"ktok{cc}"],
                        [f"PDs{h}", "BPD"] + (["PDq", "PDv"] if h == 0 else []))
                for h in range(H if debug_stop not in (5, 12, 13) else 0):
                    hp, pc = h % 2, h // 2
                    V_(lambda g, hp=hp, pc=pc, cc=cc, h=h: g.scalar_tensor_tensor(
                        Sf32[hp * 64:(hp + 1) * 64, pc * 64:(pc + 1) * 64],
                        Sf32[hp * 64:(hp + 1) * 64, pc * 64:(pc + 1) * 64],
                        pCcols[hp * 64:(hp + 1) * 64, cc * 16 + h:cc * 16 + h + 1],
                        PD[hp * 64:(hp + 1) * 64, pc * 64:(pc + 1) * 64],
                        ALU.mult, ALU.add),
                        [f"pCcols{cc}", f"PDs{h}", "BPD", "Sf32"], ["Sf32"])
                yp = 0
                for pc in range(8 if debug_stop not in (5, 12, 13) else 0):
                    SC(lambda g, pc=pc, yp=yp: g.copy(
                        Yfm[:, yp * 1024 + pc * 128:yp * 1024 + (pc + 1) * 128],
                        PB[:, pc * 128:(pc + 1) * 128]),
                        [f"PBy{pc}_0", f"PBy{pc}_1", f"BPB{pc // 4}"],
                        [f"yfm{yp}_{pc}"])
                if debug_stop in (12, 13):
                    continue
                SC(lambda g: g.copy(
                    Sbf2[0:64, :].rearrange("p (b two j) -> p b two j", two=2, j=64)
                    [:, :, 0, :],
                    Sf32[0:64, :].rearrange("p (b j) -> p b j", j=64)),
                   ["Sf32"], ["Sbf2"])
                SC(lambda g: g.copy(
                    Sbf2[64:128, :].rearrange("p (b two j) -> p b two j", two=2, j=64)
                    [:, :, 1, :],
                    Sf32[64:128, :].rearrange("p (b j) -> p b j", j=64)),
                   ["Sf32"], ["Sbf2"])
                for eh in range(2 if debug_stop not in (5, 6, 8, 9, 12, 13) else 0):
                    for kc in range(8):
                        T_(lambda g, eh=eh, kc=kc, yp=yp: g.matmul(
                            PA[:, eh * 512:(eh + 1) * 512],
                            Yfm[:, yp * 1024 + kc * 128:yp * 1024 + (kc + 1) * 128],
                            wo_sb[:, kc * D + eh * 512:kc * D + (eh + 1) * 512],
                            start=(kc == 0), stop=(kc == 7)),
                            [f"yfm{yp}_{kc}", f"w3_{kc}"],
                            ["PA", f"BPA{eh}"])
                if debug_stop in (5, 6, 8, 9, 12, 13):
                    continue
                SC(lambda g: g.activation(aobuf[:], PA[:], AF.Copy, accum_out=st(4)),
                   ["PA", "BPA0", "BPA1"], ["ao", "lnaa"])
                SC(lambda g: g.activation(g2048f[:, 0:D], aobuf[:], AF.Square,
                                          accum_out=st(5)),
                   ["ao"], ["g2048a", "lnab"])
                ln_stats(4, 5, 6, D, "lna")
                SC(lambda g: g.activation(hnbuf[:], aobuf[:], AF.Identity, bias=st(7),
                                          scale=st(6)),
                   ["ao", "lnac", "lnad"], ["hn"])
                x2sl = X2[:, tt * D:(tt + 1) * D]
                V_(lambda g, x2sl=x2sl: g.tensor_tensor(x2sl, x2sl, hnbuf[:], ALU.add),
                   [f"x2:{tt}", "hn"], [f"x2:{tt}"])

    S.barrier()
    if debug_stop in (1, 3, 4, 5, 6, 8, 9, 12, 13):
        for tt in range(NTT):
            V_(lambda g, tt=tt: g.tensor_copy(outst_dbg[:], X2[:, tt * D:(tt + 1) * D]),
               [f"x2:{tt}", "odbg"], ["odbg"])
            dma("gpsimd", yd[tt * CH:tt * CH + 128, :], outst_dbg[:],
                ["odbg"], [f"ydone{tt}", "odbg"])
        S.emit("sync", None, reads=[f"ydone{tt}" for tt in range(NTT)], writes=[])
        S.emit("gpsimd", None, reads=[f"ydone{tt}" for tt in range(NTT)], writes=[])
        return nc
    p1.close()

    p2 = ExitStack()

    def sb2(name, shape, dt):
        return p2.enter_context(nc.sbuf_tensor("s_" + name, shape, dt))

    qw2sb = sb2("qw2sb", [128, 32 * D], BF16)
    wexp1 = sb2("wexp1", [128, D], F16)
    bexp1 = sb2("bexp1", [128, D], F16)
    wexp2 = sb2("wexp2", [128, HID], F16)
    bexp2 = sb2("bexp2", [128, HID], F16)
    H2 = sb2("H2", [128, 2 * HID], F16)
    qhtok = sb2("qhtok", [128, HID], BF16)
    qhfm = sb2("qhfm", [128, 2 * 512], BF16)
    qxtok = sb2("qxtok", [128, D], BF16)
    qxfm = sb2("qxfm", [128, 8 * GT], BF16)
    h1b = sb2("h1b", [128, D], F16)
    qw1buf = sb2("qw1buf", [128, 2 * 512], BF16)
    outstage = sb2("outstage", [128, D], F32)
    fobuf = sb2("fobuf", [128, D], F16)
    hn3buf = sb2("hn3buf", [128, D], BF16)
    screw = sb2("screw", [1, 2 * 128], BF16)
    screw2 = sb2("screw2", [1, 2 * 128], BF16)
    c1col = sb2("c1col", [128, 2], F32)
    c2col = sb2("c2col", [128, 2], F32)
    s2cols = sb2("s2cols", [128, 2], F32)
    gacc = sb2("gacc", [128, 2 * 8], F32)

    pxkeys = ["PX0", "PXw", "PXb", "PXb2", "BPX"]
    for (dst, row, n, key) in ((wexp1, 0, D, "wexp1"), (bexp1, 1, D, "bexp1"),
                               (wexp2, 2, HID, "wexp2"), (bexp2, 3, HID, "bexp2")):
        for j0 in range(0, n, 512):
            dma("sync", rstgf[:], ln_d[row:row + 1, j0:j0 + 512], [], ["rstgf"])
            V_(lambda g: g.tensor_copy(rstgb[:], rstgf[:]), ["rstgf"], ["rstgb"])
            T_(lambda g: g.matmul(PX[:, 0:512], onesbf[:], rstgb[:],
                                  start=True, stop=True),
               ["onesbf", "rstgb"], pxkeys)
            V_(lambda g, dst=dst, j0=j0: g.tensor_copy(dst[:, j0:j0 + 512],
                                                       PX[:, 0:512]),
               pxkeys, [key])

    for ch in range(32):
        dma("sync", qw2sb[:, ch * D:(ch + 1) * D], qw2_s[ch * 128:(ch + 1) * 128, :],
            ["wqstore"], [f"qw2r{ch}"])

    if debug_stop == 25:
        for tt in range(NTT):
            dma("sync", qhtok[:, 0:D], qw1_s[tt * 128:(tt + 1) * 128, 0:D],
                ["wqstore", "qhd"], ["qhq"])
            V_(lambda g: g.tensor_copy(outstage[:], qhtok[:, 0:D]),
               ["qhq", "ost"], ["ost", "qhd"])
            dma("gpsimd", yd[tt * CH:tt * CH + 128, :], outstage[:],
                ["ost"], [f"ydone{tt}", "ost"])
        S.emit("sync", None, reads=[f"ydone{tt}" for tt in range(NTT)], writes=[])
        S.emit("gpsimd", None, reads=[f"ydone{tt}" for tt in range(NTT)], writes=[])
        return nc
    NT2 = NTT // 2
    def emitA(g2):
            for tt2 in range(2):
                tt = g2 * 2 + tt2
                x2sl = X2[:, tt * D:(tt + 1) * D]
                V_(lambda g, x2sl=x2sl: g.tensor_scalar(
                    outstage[:], x2sl, 1.0, 0.0, ALU.mult, ALU.add,
                    accum_out=st(0)),
                    [f"x2:{tt}"], ["ost", "f1a"])
                SC(lambda g, x2sl=x2sl: g.activation(
                    g2048f[:, 0:D], x2sl, AF.Square, accum_out=st(1)),
                    [f"x2:{tt}"], ["g2048a", "f1b"])
                ln_stats(0, 1, 2, D, "f1")
                SC(lambda g, x2sl=x2sl: g.activation(h1b[:], x2sl, AF.Identity,
                                                     bias=st(3), scale=st(2)),
                   [f"x2:{tt}", "f1c", "f1d"], ["h1"])
                V_(lambda g: g.tensor_tensor(h1b[:], h1b[:], wexp1[:], ALU.mult),
                   ["h1", "wexp1"], ["h1"])
                V_(lambda g: g.tensor_tensor(h1b[:], h1b[:], bexp1[:], ALU.add),
                   ["h1", "bexp1"], ["h1"])
                V_(lambda g: g.tensor_reduce(st(8), h1b[:], AX.X, ALU.max,
                                             apply_absolute_value=True),
                   ["h1"], ["mx1"])
                V_(lambda g: g.tensor_scalar(st(8), st(8), 1e-5, None, ALU.max),
                   ["mx1"], ["mx1"])
                V_(lambda g: g.tensor_scalar(st(9), st(8), 1.0 / 127.0, None, ALU.mult),
                   ["mx1"], ["s1col"])
                V_(lambda g: g.reciprocal(st(9), st(9)), ["s1col"], ["s1col"])
                V_(lambda g: g.tensor_tensor(st(10), st(9), swcol[:, 0:1], ALU.mult),
                   ["s1col", "swcol0"], ["sc1f"])
                V_(lambda g, tt2=tt2: g.reciprocal(c1col[:, tt2:tt2 + 1], st(10)),
                   ["sc1f"], [f"c1col{tt2}"])
                PL(lambda g: g.tensor_copy(statb1[:], st(10)), ["sc1f"], ["statb1"])
                T_(lambda g: g.matmul(PX[0:1, 0:128], statb1[:], identb[:],
                                      start=True, stop=True),
                   ["statb1", "identb"], ["PX0", "BPX"])
                V_(lambda g, tt2=tt2: g.tensor_copy(
                    screw[:, tt2 * 128:(tt2 + 1) * 128], PX[0:1, 0:128]),
                    ["PX0", "BPX"], [f"sc1row{tt2}"])
                SC(lambda g: g.activation(g2048f[:, 0:D], h1b[:], AF.Identity,
                                          bias=ktcol[:], scale=st(9)),
                   ["h1", "s1col", "ktcol"], ["g2048a"])
                V_(lambda g: g.tensor_copy(qi32buf[:, 0:D], g2048f[:, 0:D]),
                   ["g2048a"], ["qi32"])
                V_(lambda g: g.scalar_tensor_tensor(
                    outstage[:], qi32buf[:, 0:D], 0.5, g2048f[:, 0:D],
                    ALU.add, ALU.is_le), ["qi32", "g2048a", "ost"], ["ost"])
                V_(lambda g: g.scalar_tensor_tensor(
                    qxtok[:], qi32buf[:, 0:D], -KT, outstage[:],
                    ALU.add, ALU.add), ["qi32", "ost"], ["qx"])
                if debug_stop in (22, 23):
                    tt_d = g2 * 2 + tt2
                    srcb = qxtok if debug_stop == 22 else h1b
                    V_(lambda g, srcb=srcb: g.tensor_copy(outstage[:], srcb[:]),
                       ["qx", "h1", "ost"], ["ost"])
                    dma("gpsimd", yd[tt_d * CH:tt_d * CH + 128, :], outstage[:],
                        ["ost"], [f"ydone{tt_d}", "ost"])
                    continue
                for half in range(2):
                    for kq in range(4):
                        kc = half * 4 + kq
                        T_(lambda g, kc=kc, kq=kq: g.matmul(
                            PE_[:, kq * 128:(kq + 1) * 128],
                            qxtok[:, kc * 128:(kc + 1) * 128], identb[:],
                            start=True, stop=True),
                            ["qx", "identb"], [f"PEq{kq}", "BPE"])
                    V_(lambda g, half=half, tt2=tt2: g.tensor_copy(
                        qxfm[:].rearrange("p (k t) -> p k t", k=8)
                        [:, half * 4:half * 4 + 4, tt2 * 128:(tt2 + 1) * 128],
                        PE_[:].rearrange("p (k t) -> p k t", k=4)),
                        [f"PEq{kq}" for kq in range(4)] + ["BPE"],
                        [f"qxfm{k2}_{tt2}" for k2 in range(half * 4, half * 4 + 4)])

    emitA(0)
    for g2 in range(NT2):
        if debug_stop in (22, 23):
            continue
        for hs in range(8):
            for kc in range(8):
                sl = (hs * 8 + kc) % 2
                dma("sync", qw1buf[:, sl * 512:(sl + 1) * 512],
                    qw1_s[kc * 128:(kc + 1) * 128, hs * 512:(hs + 1) * 512],
                    ["wqstore"], [f"qw1b{sl}"])
                for tt2 in range(2):
                    T_(lambda g, kc=kc, sl=sl, tt2=tt2: g.matmul(
                        PA[:, tt2 * 512:(tt2 + 1) * 512],
                        qxfm[:, kc * GT + tt2 * 128:kc * GT + (tt2 + 1) * 128],
                        qw1buf[:, sl * 512:(sl + 1) * 512],
                        start=(kc == 0), stop=False),
                        [f"qxfm{kc}_{tt2}", f"qw1b{sl}"],
                        [f"PAf{tt2}", f"BPA{tt2}"], acc=(kc != 0))
            dma("sync", rstgf[:], b1_d[0:1, hs * 512:(hs + 1) * 512],
                [], ["rstgf"])
            PL(lambda g: g.tensor_copy(rstgb[:], rstgf[:]), ["rstgf"], ["rstgb"])
            for tt2 in range(2):
                T_(lambda g, tt2=tt2: g.matmul(
                    PA[:, tt2 * 512:(tt2 + 1) * 512],
                    screw[0:1, tt2 * 128:(tt2 + 1) * 128],
                    rstgb[:], start=False, stop=True),
                    [f"sc1row{tt2}", "rstgb"], [f"PAf{tt2}", f"BPA{tt2}"], acc=True)
                SC(lambda g, hs=hs, tt2=tt2: g.activation(
                    H2[:, tt2 * HID + hs * 512:tt2 * HID + (hs + 1) * 512],
                    PA[:, tt2 * 512:(tt2 + 1) * 512], AF.Gelu,
                    scale=c1col[:, tt2:tt2 + 1],
                    accum_out=gacc[:, tt2 * 8 + hs:tt2 * 8 + hs + 1]),
                    [f"PAf{tt2}", f"BPA{tt2}", f"c1col{tt2}"],
                    [f"H2_{tt2}", f"gacc{tt2}_{hs}"])
        if debug_stop == 21:
            for tt2 in range(2):
                tt = g2 * 2 + tt2
                V_(lambda g, tt2=tt2: g.tensor_copy(
                    outstage[:], H2[:, tt2 * HID:tt2 * HID + D]),
                    [f"H2_{tt2}", "ost"], ["ost"])
                dma("gpsimd", yd[tt * CH:tt * CH + 128, :], outstage[:],
                    ["ost"], [f"ydone{tt}", "ost"])
            continue
        if g2 + 1 < NT2:
            emitA(g2 + 1)
        for tt2 in range(2):
            tt = g2 * 2 + tt2
            hsl = H2[:, tt2 * HID:(tt2 + 1) * HID]
            V_(lambda g, tt2=tt2: g.tensor_reduce(
                st(12), gacc[:, tt2 * 8:(tt2 + 1) * 8], AX.X, ALU.add),
                [f"gacc{tt2}_{hs}" for hs in range(8)], ["f2a"])
            for qtr in range(4):
                SC(lambda g, hsl=hsl, qtr=qtr: g.activation(
                    g2048f[:, 0:1024], hsl[:, qtr * 1024:(qtr + 1) * 1024],
                    AF.Square, accum_out=st(28 + qtr)),
                    [f"H2_{tt2}"], ["g2048a", f"f2sq{qtr}"])
            V_(lambda g: g.tensor_reduce(st(13), statp[:, 28:32], AX.X, ALU.add),
               [f"f2sq{q2}" for q2 in range(4)], ["f2b"])
            ln_stats(12, 13, 14, HID, "f2")
            SC(lambda g, hsl=hsl, tt2=tt2: g.activation(
                hsl, hsl, AF.Identity, bias=st(15), scale=st(14)),
                [f"H2_{tt2}", "f2c", "f2d"], [f"H2_{tt2}"])
            V_(lambda g, hsl=hsl, tt2=tt2: g.tensor_tensor(
                hsl, hsl, wexp2[:], ALU.mult), [f"H2_{tt2}", "wexp2"], [f"H2_{tt2}"])
            V_(lambda g, hsl=hsl, tt2=tt2: g.tensor_tensor(
                hsl, hsl, bexp2[:], ALU.add), [f"H2_{tt2}", "bexp2"], [f"H2_{tt2}"])
            V_(lambda g, hsl=hsl: g.tensor_reduce(
                st(24), hsl, AX.X, ALU.max, apply_absolute_value=True),
                [f"H2_{tt2}"], ["mx2"])
            V_(lambda g: g.tensor_scalar(st(24), st(24), 1e-5, None, ALU.max),
               ["mx2"], ["mx2"])
            V_(lambda g, tt2=tt2: g.tensor_scalar(
                s2cols[:, tt2:tt2 + 1], st(24), 1.0 / 127.0, None, ALU.mult),
               ["mx2"], [f"s2col{tt2}"])
            V_(lambda g, tt2=tt2: g.reciprocal(s2cols[:, tt2:tt2 + 1],
                                               s2cols[:, tt2:tt2 + 1]),
               [f"s2col{tt2}"], [f"s2col{tt2}"])
            V_(lambda g, tt2=tt2: g.tensor_tensor(st(26), s2cols[:, tt2:tt2 + 1],
                                                  swcol[:, 1:2], ALU.mult),
               [f"s2col{tt2}", "swcol1"], ["sc2f"])
            V_(lambda g, tt2=tt2: g.reciprocal(c2col[:, tt2:tt2 + 1], st(26)),
               ["sc2f"], [f"c2col{tt2}"])
            PL(lambda g: g.tensor_copy(statb1[:], st(26)), ["sc2f"], ["statb1"])
            T_(lambda g: g.matmul(PX[0:1, 0:128], statb1[:], identb[:],
                                  start=True, stop=True),
               ["statb1", "identb"], ["PX0", "BPX"])
            V_(lambda g, tt2=tt2: g.tensor_copy(
                screw2[:, tt2 * 128:(tt2 + 1) * 128], PX[0:1, 0:128]),
                ["PX0", "BPX"], [f"sc2row{tt2}"])
        for tt2 in range(2):
            tt = g2 * 2 + tt2
            hsl = H2[:, tt2 * HID:(tt2 + 1) * HID]
            for qtr in range(4):
                SC(lambda g, hsl=hsl, qtr=qtr, tt2=tt2: g.activation(
                    g2048f[:, 0:1024], hsl[:, qtr * 1024:(qtr + 1) * 1024],
                    AF.Identity, bias=ktcol[:], scale=s2cols[:, tt2:tt2 + 1]),
                    [f"H2_{tt2}", f"s2col{tt2}", "ktcol"], ["g2048a"])
                V_(lambda g: g.tensor_copy(qi32buf[:, 0:1024], g2048f[:, 0:1024]),
                   ["g2048a"], ["qi32"])
                V_(lambda g: g.scalar_tensor_tensor(
                    outstage[:], qi32buf[:, 0:1024], 0.5, g2048f[:, 0:1024],
                    ALU.add, ALU.is_le), ["qi32", "g2048a", "ost"], ["ost"])
                V_(lambda g, qtr=qtr: g.scalar_tensor_tensor(
                    qhtok[:, qtr * 1024:(qtr + 1) * 1024], qi32buf[:, 0:1024],
                    -KT, outstage[:], ALU.add, ALU.add),
                    ["qi32", "ost"], [f"qh{qtr // 2}"])
            for kb in range(8):
                for kq in range(4):
                    kc = kb * 4 + kq
                    T_(lambda g, kc=kc, kq=kq: g.matmul(
                        PE_[:, kq * 128:(kq + 1) * 128],
                        qhtok[:, kc * 128:(kc + 1) * 128], identb[:],
                        start=True, stop=True),
                        [f"qh{kc // 16}", "identb"], [f"PEq{kq}", "BPE"])
                sl2 = kb % 2
                if sl2 == 0:
                    V_(lambda g, sl2=sl2: g.tensor_copy(
                        qhfm[:, sl2 * 512:(sl2 + 1) * 512], PE_[:]),
                        [f"PEq{kq}" for kq in range(4)] + ["BPE"], [f"qhfm{sl2}"])
                else:
                    SC(lambda g, sl2=sl2: g.copy(
                        qhfm[:, sl2 * 512:(sl2 + 1) * 512], PE_[:]),
                        [f"PEq{kq}" for kq in range(4)] + ["BPE"], [f"qhfm{sl2}"])
                for kq in range(4):
                    kc = kb * 4 + kq
                    for eh in range(2):
                        PT = PC if eh == 0 else PD
                        T_(lambda g, kc=kc, eh=eh, PT=PT, sl2=sl2, kq=kq: g.matmul(
                            PT[:], qhfm[:, sl2 * 512 + kq * 128:sl2 * 512 + (kq + 1) * 128],
                            qw2sb[:, kc * D + eh * 512:kc * D + (eh + 1) * 512],
                            start=(kc == 0), stop=False),
                            [f"qhfm{sl2}", f"qw2r{kc}"],
                            [f"Pw2{eh}", "BPC" if eh == 0 else "BPD"],
                            acc=(kc != 0))
            for eh in range(2):
                PT = PC if eh == 0 else PD
                dma("sync", rstgf[:], b2_d[0:1, eh * 512:(eh + 1) * 512],
                    [], ["rstgf"])
                PL(lambda g: g.tensor_copy(rstgb[:], rstgf[:]), ["rstgf"], ["rstgb"])
                T_(lambda g, eh=eh, tt2=tt2, PT=PT: g.matmul(
                    PT[:], screw2[0:1, tt2 * 128:(tt2 + 1) * 128],
                    rstgb[:], start=False, stop=True),
                    [f"sc2row{tt2}", "rstgb"],
                    [f"Pw2{eh}", "BPC" if eh == 0 else "BPD"], acc=True)
                SC(lambda g, eh=eh, tt2=tt2, PT=PT: g.activation(
                    fobuf[:, eh * 512:(eh + 1) * 512], PT[:], AF.Copy,
                    scale=c2col[:, tt2:tt2 + 1], accum_out=st(16 + eh)),
                    [f"Pw2{eh}", "BPC" if eh == 0 else "BPD", f"c2col{tt2}"],
                    [f"fo{eh}", f"foacc{eh}"])
            V_(lambda g: g.tensor_tensor(st(18), st(16), st(17), ALU.add),
               ["foacc0", "foacc1"], ["f3a"])
            SC(lambda g: g.activation(g2048f[:, 0:D], fobuf[:], AF.Square,
                                      accum_out=st(19)),
               ["fo0", "fo1"], ["g2048a", "f3b"])
            ln_stats(18, 19, 20, D, "f3")
            SC(lambda g: g.activation(hn3buf[:], fobuf[:], AF.Identity, bias=st(21),
                                      scale=st(20)),
               ["fo0", "fo1", "f3c", "f3d"], ["hn3"])
            V_(lambda g, tt=tt: g.tensor_tensor(
                outstage[:], hn3buf[:], X2[:, tt * D:(tt + 1) * D], ALU.add),
                ["hn3", f"x2:{tt}"], ["ost"])
            dma("gpsimd", yd[tt * CH:tt * CH + 128, :], outstage[:],
                ["ost"], [f"ydone{tt}", "ost"])

    S.emit("sync", None, reads=[f"ydone{tt}" for tt in range(NTT)], writes=[])
    S.emit("gpsimd", None, reads=[f"ydone{tt}" for tt in range(NTT)], writes=[])
    p2.close()
    es.close()
    return nc


def kernel(**inputs):
    n_cores = 8
    B, Tfull, Dm = inputs["L0_attn_x"].shape
    n_seq = B // n_cores
    n_chunks = Tfull // CH

    x = np.ascontiguousarray(inputs["L0_attn_x"], dtype=np.float32)
    wqT = np.ascontiguousarray(np.asarray(inputs["L0_attn_Wq"]).T)
    wkT = np.ascontiguousarray(np.asarray(inputs["L0_attn_Wk"]).T)
    wvT = np.ascontiguousarray(np.asarray(inputs["L0_attn_Wv"]).T)
    woT = np.ascontiguousarray(np.asarray(inputs["L0_attn_Wo"]).T)
    w1T = np.ascontiguousarray(np.asarray(inputs["L0_ffn_w1"]).T)
    w2T = np.ascontiguousarray(np.asarray(inputs["L0_ffn_w2"]).T)
    gl = np.stack([inputs["L0_attn_W_gamma"], inputs["L0_attn_b_gamma"],
                   inputs["L0_attn_W_lambda"], inputs["L0_attn_b_lambda"]]
                  ).astype(np.float32)
    lnp = np.zeros((4, HID), np.float32)
    lnp[0, :Dm] = inputs["L0_ffn_ln1_w"]
    lnp[1, :Dm] = inputs["L0_ffn_ln1_b"]
    lnp[2, :] = inputs["L0_ffn_ln2_w"]
    lnp[3, :] = inputs["L0_ffn_ln2_b"]
    b1 = np.asarray(inputs["L0_ffn_b1"]).reshape(1, HID).astype(np.float32)
    b2 = np.asarray(inputs["L0_ffn_b2"]).reshape(1, Dm).astype(np.float32)

    identb = np.eye(128, dtype=ml_dtypes.bfloat16)
    identf = np.eye(128, dtype=np.float32)
    ieye2 = np.concatenate([np.eye(64), np.eye(64)], axis=0).astype(ml_dtypes.bfloat16)
    sidx = np.arange(128)[:, None]
    tidx = np.arange(128)[None, :]
    maskmin = np.where(sidx <= tidx, 0.0, -1e30).astype(np.float32)
    selpc = np.zeros((16, 8 * 128), np.float32)
    for pc in range(8):
        selpc[2 * pc, pc * 128:pc * 128 + 64] = 1.0
        selpc[2 * pc + 1, pc * 128 + 64:pc * 128 + 128] = 1.0
    selpc = selpc.astype(ml_dtypes.bfloat16)

    nc = build_program(n_seq, n_chunks)

    shared = {"wqT": wqT, "wkT": wkT, "wvT": wvT, "woT": woT, "w1T": w1T,
              "w2T": w2T, "gl": gl, "lnp": lnp, "b1": b1, "b2": b2,
              "identb": identb, "identf": identf, "ieye2": ieye2,
              "maskmin": maskmin, "selpc": selpc}
    in_maps = []
    for ci in range(n_cores):
        m = dict(shared)
        m["x"] = np.ascontiguousarray(
            x[ci * n_seq:(ci + 1) * n_seq].reshape(n_seq * Tfull, Dm))
        in_maps.append(m)
    res = run_bass_kernel_spmd(nc, in_maps, core_ids=list(range(n_cores)))
    out = np.empty((B, Tfull, Dm), np.float32)
    for ci in range(n_cores):
        out[ci * n_seq:(ci + 1) * n_seq] = np.asarray(
            res.results[ci]["y"]).reshape(n_seq, Tfull, Dm)
    return out


# version 18
# speedup vs baseline: 1.0606x; 1.0005x over previous
import numpy as np
import ml_dtypes
import concourse.bass as bass
import concourse.mybir as mybir
from concourse.bass_utils import run_bass_kernel_spmd
from contextlib import ExitStack

F32 = mybir.dt.float32
BF16 = mybir.dt.bfloat16
I32 = mybir.dt.int32
F16 = mybir.dt.float16
AF = mybir.ActivationFunctionType
ALU = mybir.AluOpType
AX = mybir.AxisListType

D = 1024
H = 16
DK = 64
HID = 4096
CH = 128
KT = 256.0
KW = 256.0

ENGS = ("tensor", "vector", "scalar", "sync", "gpsimd")
DMA_ENGS = {"sync", "gpsimd"}


class Sched:
    NSLOT = 8

    def __init__(self, nc, sems):
        self.nc = nc
        self.sems = sems
        self.idx = {}
        self.qcnt = {e: 0 for e in DMA_ENGS}
        self.last_w = {}
        self.readers = {}
        self.known = {e: {} for e in ENGS}
        self.snap = {}

    @staticmethod
    def phys(pe):
        if pe == "pool":
            return "gpsimd"
        return pe.split(":")[0]

    def _sem(self, pe):
        if ":" in pe:
            e, r = pe.split(":")
            return self.sems[e][int(r)]
        return self.sems[pe]

    def _wait(self, eng, kn, pe, di):
        if kn.get(pe, 0) >= di:
            return
        val = di * 16 if ":" in pe else di
        eng.wait_ge(self._sem(pe), val)
        sn = self.snap.get((pe, di))
        if sn is not None:
            for e3, v in sn.items():
                if v > kn.get(e3, 0):
                    kn[e3] = v
        if di > kn.get(pe, 0):
            kn[pe] = di

    def emit(self, e, fn, reads=(), writes=(), acc=False):
        eng = getattr(self.nc, "gpsimd" if e == "pool" else e)
        kn = self.known[self.phys(e)]
        if e in DMA_ENGS:
            r = self.qcnt[e] % self.NSLOT
            self.qcnt[e] += 1
            pe = f"{e}:{r}"
        else:
            pe = e
        deps = []
        for k in reads:
            w = self.last_w.get(k)
            if w is not None:
                deps.append(w)
        for k in writes:
            w = self.last_w.get(k)
            if w is not None:
                deps.append(w)
            for rd in self.readers.get(k, ()):
                deps.append(rd)
        if e in DMA_ENGS and self.idx.get(pe, 0) > 0:
            deps.append((pe, self.idx[pe]))
        myidx = self.idx.get(pe, 0) + 1
        need = {}
        for (de, di) in deps:
            if de == pe and di >= myidx:
                continue
            if acc and de == "tensor" and e == "tensor":
                continue
            if de not in need or need[de] < di:
                need[de] = di
        order = ([de for de in need if self.phys(de) != e]
                 + [de for de in need if self.phys(de) == e])
        for de in order:
            self._wait(eng, kn, de, need[de])
        if fn is None:
            return None
        inst = fn(eng)
        inst.then_inc(self._sem(pe), 16 if e in DMA_ENGS else 1)
        self.idx[pe] = myidx
        self.snap[(pe, myidx)] = dict(kn)
        op = (pe, myidx)
        for k in writes:
            self.last_w[k] = op
            self.readers[k] = []
        for k in reads:
            self.readers.setdefault(k, []).append(op)
        return op

    def barrier(self):
        tails = dict(self.idx)
        for e in ENGS:
            eng = getattr(self.nc, e)
            kn = self.known[e]
            for pe, di in tails.items():
                if self.phys(pe) == e or di == 0:
                    continue
                self._wait(eng, kn, pe, di)


def build_program(n_seq, n_chunks, debug_stop=0):
    T = n_chunks * CH
    NTOK = n_seq * T
    NTT = n_seq * n_chunks
    GRP = 2
    assert n_chunks % GRP == 0
    GT = GRP * CH

    nc = bass.Bass("TRN2", target_bir_lowering=False)

    xd = nc.dram_tensor("x", [NTOK, D], F32, kind="ExternalInput").ap()
    wT_d = {nm: nc.dram_tensor(nm, [D, D], F32, kind="ExternalInput").ap()
            for nm in ("wqT", "wkT", "wvT", "woT")}
    w1T_d = nc.dram_tensor("w1T", [D, HID], F32, kind="ExternalInput").ap()
    w2T_d = nc.dram_tensor("w2T", [HID, D], F32, kind="ExternalInput").ap()
    gl_d = nc.dram_tensor("gl", [4, H], F32, kind="ExternalInput").ap()
    ln_d = nc.dram_tensor("lnp", [4, HID], F32, kind="ExternalInput").ap()
    b1_d = nc.dram_tensor("b1", [1, HID], F32, kind="ExternalInput").ap()
    b2_d = nc.dram_tensor("b2", [1, D], F32, kind="ExternalInput").ap()
    identb_d = nc.dram_tensor("identb", [128, 128], BF16, kind="ExternalInput").ap()
    identf_d = nc.dram_tensor("identf", [128, 128], F32, kind="ExternalInput").ap()
    ieye2_d = nc.dram_tensor("ieye2", [128, 64], BF16, kind="ExternalInput").ap()
    maskmin_d = nc.dram_tensor("maskmin", [128, 128], F32, kind="ExternalInput").ap()
    selpc_d = nc.dram_tensor("selpc", [16, 8 * 128], BF16, kind="ExternalInput").ap()
    yd = nc.dram_tensor("y", [NTOK, D], F32, kind="ExternalOutput").ap()
    qw1_s = nc.dram_tensor("qw1s", [D, HID], BF16, kind="Internal").ap()
    qw2_s = nc.dram_tensor("qw2s", [HID, D], BF16, kind="Internal").ap()

    es = ExitStack()

    def sb(name, shape, dt):
        return es.enter_context(nc.sbuf_tensor("s_" + name, shape, dt))

    def pt(name, shape, dt=F32):
        return es.enter_context(nc.psum_tensor(name, shape, dt))

    sems = {}
    for e in ENGS:
        if e in DMA_ENGS:
            sems[e] = [es.enter_context(nc.semaphore(f"sem_{e}{r}"))
                       for r in range(Sched.NSLOT)]
        else:
            sems[e] = es.enter_context(nc.semaphore("sem_" + e))
    sems["pool"] = es.enter_context(nc.semaphore("sem_pool"))
    S = Sched(nc, sems)

    X2 = sb("X2", [128, NTT * D], BF16)
    identb = sb("identb", [128, 128], BF16)
    ones1f = sb("ones1f", [1, 128], F32)
    onesbf = sb("onesbf", [1, 128], BF16)
    onescolf = sb("onescolf", [128, 1], F32)
    statp = sb("statp", [128, 64], F32)
    statb1 = sb("statb1", [128, 1], BF16)
    g2048f = sb("g2048f", [128, 1024], F32)
    qi32buf = sb("qi32buf", [128, 1024], I32)
    swcol = sb("swcol", [128, 2], F32)
    rstgf = sb("rstgf", [1, 512], F32)
    rstgb = sb("rstgb", [1, 512], BF16)
    e24col = sb("e24col", [128, 1], F32)
    ktcol = sb("ktcol", [128, 1], F32)
    e30col = sb("e30col", [128, 1], F32)
    nKTcol = sb("nKTcol", [128, 1], F32)
    nKWcol = sb("nKWcol", [128, 1], F32)

    PA = pt("PA", [128, 1024])
    PB = pt("PB", [128, 1024])
    PC = pt("PC", [128, 512])
    PD = pt("PD", [128, 512])
    PE_ = pt("PE", [128, 512])
    PX = pt("PX", [128, 512])
    PD_ALL = ["PDq", "PDv"] + [f"PDs{h}" for h in range(H)]

    def dma(q, out, in_, rk, wk):
        return S.emit(q, lambda g: g.dma_start(out, in_), reads=rk, writes=wk)

    def T_(fn, rk, wk, acc=True):
        return S.emit("tensor", fn, reads=rk, writes=wk, acc=True)

    def V_(fn, rk, wk):
        return S.emit("vector", fn, reads=rk, writes=wk)

    def SC(fn, rk, wk):
        return S.emit("scalar", fn, reads=rk, writes=wk)

    def PL(fn, rk, wk):
        return S.emit("pool", fn, reads=rk, writes=wk)

    def st(i, n=1):
        return statp[:, i:i + n]

    def ln_stats(a, b, c, N, tag):
        V_(lambda g: g.tensor_scalar(st(a), st(a), 1.0 / N, None, ALU.mult),
           [f"{tag}a"], [f"{tag}a"])
        V_(lambda g: g.tensor_scalar(st(b), st(b), 1.0 / N, None, ALU.mult),
           [f"{tag}b"], [f"{tag}b"])
        V_(lambda g: g.scalar_tensor_tensor(st(b), st(a), st(a), st(b),
                                            ALU.mult, ALU.subtract),
           [f"{tag}a", f"{tag}b"], [f"{tag}b"])
        V_(lambda g: g.tensor_scalar(st(b), st(b), -1.0, 1e-5, ALU.mult, ALU.add),
           [f"{tag}b"], [f"{tag}b"])
        SC(lambda g: g.activation(st(c), st(b), AF.Sqrt),
           [f"{tag}b"], [f"{tag}c"])
        V_(lambda g: g.reciprocal(st(c), st(c)), [f"{tag}c"], [f"{tag}c"])
        V_(lambda g: g.scalar_tensor_tensor(st(c + 1), st(a), -1.0, st(c),
                                            ALU.mult, ALU.mult),
           [f"{tag}a", f"{tag}c"], [f"{tag}d"])

    dma("sync", identb[:], identb_d, [], ["identb"])
    V_(lambda g: g.memset(ones1f[:], 1.0), [], ["ones1f"])
    V_(lambda g: g.memset(onesbf[:], 1.0), [], ["onesbf"])
    V_(lambda g: g.memset(onescolf[:], 1.0), [], ["onescolf"])
    V_(lambda g: g.memset(e24col[:], 1e-24), [], ["ccol"])
    V_(lambda g: g.memset(ktcol[:], KT), [], ["ktcol"])
    V_(lambda g: g.memset(e30col[:], 1e-30), [], ["ccol"])
    V_(lambda g: g.memset(nKTcol[:], -KT), [], ["ccol"])
    V_(lambda g: g.memset(nKWcol[:], -KW), [], ["ccol"])

    p1 = ExitStack()

    def sb1(name, shape, dt):
        return p1.enter_context(nc.sbuf_tensor("s_" + name, shape, dt))

    identf = sb1("identf", [128, 128], F32)
    maskmin = sb1("maskmin", [128, 128], F32)
    selpc = sb1("selpc", [16, 8 * 128], BF16)
    zeros16 = sb1("zeros16", [16, 128], F32)
    glb = sb1("glb", [128, 4 * H], F32)
    glrow = sb1("glrow", [1, 4 * H], F32)
    dma("sync", identf[:], identf_d, [], ["identf"])
    dma("sync", maskmin[:], maskmin_d, [], ["maskmin"])
    dma("sync", selpc[:], selpc_d, [], ["selpc"])
    V_(lambda g: g.memset(zeros16[:], 0.0), [], ["zeros16"])
    dma("sync", glrow[:], gl_d.rearrange("a b -> (a b)").rearrange("(o n) -> o n", o=1),
        [], ["glrow"])
    for j in range(4):
        T_(lambda g, j=j: g.matmul(PX[:, j * 16:(j + 1) * 16], ones1f[:],
                                   glrow[0:1, j * 16:(j + 1) * 16],
                                   start=True, stop=True),
           ["ones1f", "glrow"], ["PX0", "BPX"])
    V_(lambda g: g.tensor_copy(glb[:], PX[:, 0:64]), ["PX0", "BPX"], ["glb"])

    wq_sb = sb1("wq_sb", [128, 8 * D], BF16)
    wk_sb = sb1("wk_sb", [128, 8 * D], BF16)
    wv_sb = sb1("wv_sb", [128, 8 * D], BF16)
    wo_sb = sb1("wo_sb", [128, 8 * D], BF16)
    wstage = sb1("wstage", [128, 1024], F32)
    wqout = sb1("wqout", [128, 1024], BF16)
    absacc = sb1("absacc", [128, 64], F32)

    xstage = sb1("xstage", [128, D], F32)
    lnxtok = sb1("lnxtok", [128, D], BF16)
    lnxfm = sb1("lnxfm", [128, 8 * GT], BF16)
    Qsb = sb1("Qsb", [128, 8 * GT], BF16)
    Qpsb = sb1("Qpsb", [128, 8 * GT], BF16)
    PexpS = sb1("PexpS", [128, GT], F32)
    Vsb = sb1("Vsb", [128, 8 * GT], BF16)
    Ktok = sb1("Ktok", [128, GRP * D], BF16)
    Equad = sb1("Equad", [128, 12 * CH], F32)
    Abuf = sb1("Abuf", [128, H * CH], BF16)
    Vtbuf = sb1("Vtbuf", [128, H * DK], BF16)
    Lbcbuf = sb1("Lbcbuf", [128, 8 * CH], F32)
    Yfm = sb1("Yfm", [128, 8 * CH], BF16)
    aobuf = sb1("aobuf", [128, D], BF16)
    hnbuf = sb1("hnbuf", [128, D], BF16)
    Sf32 = sb1("Sf32", [128, 8 * DK], F32)
    Sbf2 = sb1("Sbf2", [128, H * DK], BF16)
    wcol = sb1("wcol", [128, 16], F32)
    wtm = sb1("wtm", [128, 16], F32)
    llrow = sb1("llrow", [1, 16], F32)
    V_(lambda g: g.memset(Sbf2[:], 0.0), [], ["Sbf2"])
    LTf = sb1("LTf", [16, GRP * CH], F32)
    nLT = sb1("nLT", [16, CH], F32)
    negLtok = sb1("negLtok", [128, GRP * H], F32)
    expLTf = sb1("expLTf", [16, CH], F32)
    expLTb = sb1("expLTb", [16, GT], BF16)
    pCrow = sb1("pCrow", [1, 16], F32)
    pCcols = sb1("pCcols", [128, GRP * 16], F32)
    lrtok = sb1("lrtok", [128, 16], F32)
    dsc = sb1("dsc", [128, 16 * 8], F32)
    outst_dbg = sb1("outst_dbg", [128, D], F32) if debug_stop else None

    def dsl(i):
        return dsc[:, i * 16:(i + 1) * 16]

    KB, GA, LA, RHO, ONEMR, KFAC, T1, T2 = range(8)

    for wi, (nm, wsb) in enumerate((("wqT", wq_sb), ("wkT", wk_sb),
                                    ("wvT", wv_sb), ("woT", wo_sb))):
        for kc in range(8):
            dma("sync", wstage[:, 0:D],
                wT_d[nm][kc * 128:(kc + 1) * 128, :], [], ["wstg0"])
            if kc % 2 == 0:
                V_(lambda g, wsb=wsb, kc=kc: g.tensor_copy(
                    wsb[:, kc * D:(kc + 1) * D], wstage[:, 0:D]),
                    ["wstg0"], [f"w{wi}_{kc}"])
            else:
                SC(lambda g, wsb=wsb, kc=kc: g.copy(
                    wsb[:, kc * D:(kc + 1) * D], wstage[:, 0:D]),
                    ["wstg0"], [f"w{wi}_{kc}"])

    def wq_params(which):
        wd, qd, nchunk, cw = ((w1T_d, qw1_s, 8, HID) if which == 0
                              else (w2T_d, qw2_s, 32, D))
        return wd, qd, nchunk, cw // 1024, 1024, (0 if which == 0 else 32)

    def wquant_A(which, items):
        wd, qd, nchunk, nh, hw, acol = wq_params(which)
        for (ch, hf) in items:
            ai = ch * nh + hf
            dma("gpsimd", wstage[:, 0:hw], wd[ch * 128:(ch + 1) * 128,
                                              hf * hw:(hf + 1) * hw],
                [], ["wstg0"])
            V_(lambda g, ai=ai, hw=hw, acol=acol: g.tensor_reduce(
                absacc[:, acol + ai:acol + ai + 1], wstage[:, 0:hw], AX.X,
                ALU.add, apply_absolute_value=True),
                ["wstg0"], [f"aacc{which}_{ai}"])

    def wquant_S(which):
        wd, qd, nchunk, nh, hw, acol = wq_params(which)
        nacc = nchunk * nh
        V_(lambda g, nacc=nacc, acol=acol, which=which: g.tensor_reduce(
            st(60 + which), absacc[:, acol:acol + nacc], AX.X, ALU.add),
            [f"aacc{which}_{a2}" for a2 in range(nacc)], [f"csum{which}"])
        T_(lambda g, which=which: g.matmul(PX[0:1, 432 + which:433 + which],
                                           st(60 + which), onescolf[:],
                                           start=True, stop=True),
           [f"csum{which}", "onescolf"], ["PXw", "BPX"])
        nels = float(D * HID)
        V_(lambda g, which=which, nels=nels: g.tensor_scalar(
            st(56 + which)[0:1, :], PX[0:1, 432 + which:433 + which],
            1.0 / nels, 1e-5, ALU.mult, ALU.max), ["PXw", "BPX"], [f"wmean{which}"])
        V_(lambda g, which=which: g.reciprocal(
            st(58 + which)[0:1, :], st(56 + which)[0:1, :]),
            [f"wmean{which}"], [f"wsc{which}"])
        T_(lambda g, which=which: g.matmul(
            PX[:, 436 + which:437 + which], ones1f[:], st(58 + which)[0:1, :],
            start=True, stop=True), [f"wsc{which}", "ones1f"], ["PXw", "BPX"])
        V_(lambda g, which=which: g.tensor_copy(
            swcol[:, which:which + 1], PX[:, 436 + which:437 + which]),
            ["PXw", "BPX"], [f"swcol{which}"])

    def wquant_B(which, items):
        wd, qd, nchunk, nh, hw, acol = wq_params(which)
        for (ch, hf) in items:
            wsk = ["wstg0"]
            dma("gpsimd", wstage[:, 0:hw], wd[ch * 128:(ch + 1) * 128,
                                              hf * hw:(hf + 1) * hw],
                [], wsk)
            V_(lambda g, hw=hw, which=which: g.tensor_scalar(
                g2048f[:, 0:hw], wstage[:, 0:hw], swcol[:, which:which + 1],
                None, ALU.mult),
                wsk + [f"swcol{which}", "g2048a"], ["g2048a"])
            V_(lambda g, hw=hw: g.tensor_copy(qi32buf[:, 0:hw], g2048f[:, 0:hw]),
               ["g2048a"], ["qi32"])
            V_(lambda g, hw=hw: g.scalar_tensor_tensor(
                wstage[:, 0:hw], qi32buf[:, 0:hw], 0.5, g2048f[:, 0:hw],
                ALU.add, ALU.is_le), ["qi32", "g2048a"] + wsk, wsk)
            V_(lambda g, hw=hw: g.scalar_tensor_tensor(
                wqout[:, 0:hw], qi32buf[:, 0:hw], 0.0, wstage[:, 0:hw],
                ALU.add, ALU.add), ["qi32"] + wsk, ["wqo"])
            V_(lambda g, hw=hw: g.scalar_tensor_tensor(
                wstage[:, 0:hw], qi32buf[:, 0:hw], -0.5, g2048f[:, 0:hw],
                ALU.add, ALU.is_ge), ["qi32", "g2048a"] + wsk, wsk)
            V_(lambda g, hw=hw: g.tensor_tensor(
                wqout[:, 0:hw], wqout[:, 0:hw], wstage[:, 0:hw], ALU.subtract),
                ["wqo"] + wsk, ["wqo"])
            V_(lambda g, hw=hw: g.tensor_scalar(wqout[:, 0:hw], wqout[:, 0:hw],
                                                1.0, -1.0, ALU.min, ALU.max),
               ["wqo"], ["wqo"])
            dma("gpsimd", qd[ch * 128:(ch + 1) * 128, hf * hw:(hf + 1) * hw],
                wqout[:, 0:hw], ["wqo"], ["wqstore"])

    wq_items = {0: [(c, f) for c in range(8) for f in range(4)],
                1: [(c, 0) for c in range(32)]}

    if debug_stop == 2:
        for tt in range(NTT):
            dma("gpsimd", yd[tt * CH:tt * CH + 128, :], wstage[:, 0:D],
                ["wstg0"], [f"ydone{tt}"])
        S.emit("sync", None, reads=[f"ydone{tt}" for tt in range(NTT)], writes=[])
        S.emit("gpsimd", None, reads=[f"ydone{tt}" for tt in range(NTT)], writes=[])
        return nc
    n_grp_b = n_chunks // GRP
    for b in range(n_seq):
        V_(lambda g: g.memset(Sf32[:], 0.0), [], ["Sf32"])
        for gg in range(n_grp_b):
            gi = b * n_grp_b + gg
            ngi = n_seq * n_grp_b
            def slice_items(items, i, n):
                k = len(items)
                return items[(i * k) // n:((i + 1) * k) // n]
            if ngi == 1:
                wquant_A(0, wq_items[0]); wquant_A(1, wq_items[1])
                wquant_S(0); wquant_S(1)
                wquant_B(0, wq_items[0]); wquant_B(1, wq_items[1])
            else:
                half = ngi // 2
                if gi < half:
                    wquant_A(0, slice_items(wq_items[0], gi, half))
                    wquant_A(1, slice_items(wq_items[1], gi, half))
                if gi == half:
                    wquant_S(0)
                    wquant_S(1)
                if gi >= half:
                    i2 = gi - half
                    n2 = ngi - half
                    wquant_B(0, slice_items(wq_items[0], i2, n2))
                    wquant_B(1, slice_items(wq_items[1], i2, n2))
            c0 = gg * GRP
            for cc in range(GRP):
                c = c0 + cc
                tt = b * n_chunks + c
                xs = xstage[:, 0:D]
                xrow0 = b * T + c * CH
                dma("sync", xs, xd[xrow0:xrow0 + 128, :], [], ["xstg0"])
                x2sl = X2[:, tt * D:(tt + 1) * D]
                V_(lambda g, xs=xs, x2sl=x2sl: g.tensor_scalar(
                    x2sl, xs, 1.0, 0.0, ALU.mult, ALU.add, accum_out=st(0)),
                    ["xstg0"], [f"x2:{tt}", "ln1a"])
                SC(lambda g, xs=xs: g.activation(
                    g2048f[:, 0:D], xs, AF.Square, accum_out=st(1)),
                    ["xstg0"], ["g2048a", "ln1b"])
                ln_stats(0, 1, 2, D, "ln1")
                lnx = lnxtok[:, 0:D]
                SC(lambda g, x2sl=x2sl, lnx=lnx: g.activation(
                    lnx, x2sl, AF.Identity, bias=st(3), scale=st(2)),
                    [f"x2:{tt}", "ln1c", "ln1d"], ["lnx0"])
                for half in range(2):
                    for kq in range(4):
                        kc = half * 4 + kq
                        T_(lambda g, lnx=lnx, kc=kc, kq=kq: g.matmul(
                            PE_[:, kq * 128:(kq + 1) * 128],
                            lnx[:, kc * 128:(kc + 1) * 128], identb[:],
                            start=True, stop=True),
                            ["lnx0", "identb"], [f"PEq{kq}", "BPE"])
                    V_(lambda g, half=half, cc=cc: g.tensor_copy(
                        lnxfm[:].rearrange("p (k t) -> p k t", k=8)
                        [:, half * 4:half * 4 + 4, cc * 128:(cc + 1) * 128],
                        PE_[:].rearrange("p (k t) -> p k t", k=4)),
                        [f"PEq{kq}" for kq in range(4)] + ["BPE"],
                        [f"lnxfm{k2}_{cc}" for k2 in range(half * 4, half * 4 + 4)])
                for eh in range(2):
                    for kc in range(8):
                        T_(lambda g, eh=eh, kc=kc, cc=cc: g.matmul(
                            PA[:, eh * 512:(eh + 1) * 512],
                            lnxfm[:, kc * GT + cc * 128:kc * GT + (cc + 1) * 128],
                            wk_sb[:, kc * D + eh * 512:kc * D + (eh + 1) * 512],
                            start=(kc == 0), stop=(kc == 7)),
                            [f"lnxfm{kc}_{cc}", f"w1_{kc}"],
                            ["PA", f"BPA{eh}"], acc=(kc != 0 or eh != 0))
                SC(lambda g: g.activation(g2048f[:, 0:D], PA[:], AF.Square),
                   ["PA", "BPA0", "BPA1"], ["g2048a"])
                V_(lambda g: g.tensor_reduce(
                    dsl(T1), g2048f[:, 0:D].rearrange("p (h j) -> p h j", h=16),
                    AX.X, ALU.add), ["g2048a"], ["dT1"])
                V_(lambda g: g.tensor_reduce(
                    dsl(T2), PA[:].rearrange("p (h j) -> p h j", h=16),
                    AX.X, ALU.add), ["PA", "BPA0", "BPA1"], ["dT2"])
                SC(lambda g: g.activation(dsl(KFAC), dsl(T1), AF.Sqrt,
                                          bias=e24col[:]),
                   ["dT1", "ccol"], ["dKF"])
                V_(lambda g: g.reciprocal(dsl(KFAC), dsl(KFAC)), ["dKF"], ["dKF"])
                V_(lambda g: g.scalar_tensor_tensor(
                    dsl(KB), dsl(T2), 1.0 / 64.0, dsl(KFAC), ALU.mult, ALU.mult),
                    ["dT2", "dKF"], ["dKB"])
                V_(lambda g: g.tensor_tensor(dsl(T1), dsl(KB), glb[:, 0:16], ALU.mult),
                   ["dKB", "glb"], ["dT1"])
                V_(lambda g: g.tensor_tensor(dsl(T1), dsl(T1), glb[:, 16:32], ALU.add),
                   ["dT1", "glb"], ["dT1"])
                SC(lambda g: g.activation(dsl(GA), dsl(T1), AF.Sigmoid), ["dT1"], ["dGA"])
                V_(lambda g: g.tensor_tensor(dsl(T2), dsl(KB), glb[:, 32:48], ALU.mult),
                   ["dKB", "glb"], ["dT2"])
                V_(lambda g: g.tensor_tensor(dsl(T2), dsl(T2), glb[:, 48:64], ALU.add),
                   ["dT2", "glb"], ["dT2"])
                SC(lambda g: g.activation(dsl(LA), dsl(T2), AF.Sigmoid), ["dT2"], ["dLA"])
                V_(lambda g: g.tensor_scalar(dsl(T2), dsl(LA), -1.0, 1.0,
                                             ALU.mult, ALU.add), ["dLA"], ["dT2"])
                V_(lambda g: g.tensor_tensor(dsl(RHO), dsl(T2), dsl(GA), ALU.mult),
                   ["dT2", "dGA"], ["dRHO"])
                V_(lambda g: g.tensor_scalar(dsl(ONEMR), dsl(RHO), -1.0, 1.0,
                                             ALU.mult, ALU.add), ["dRHO"], ["dOM"])
                SC(lambda g: g.activation(lrtok[:], dsl(RHO), AF.Ln,
                                          bias=e30col[:]),
                   ["dRHO", "ccol"], ["lrtok"])
                V_(lambda g: g.scalar_tensor_tensor(dsl(KFAC), dsl(KFAC), 1.0,
                                                    dsl(ONEMR), ALU.mult, ALU.mult),
                   ["dKF", "dOM"], ["dKF"])
                V_(lambda g, cc=cc: g.tensor_tensor(
                    Ktok[:, cc * D:(cc + 1) * D].rearrange("p (h j) -> p h j", h=16),
                    PA[:].rearrange("p (h j) -> p h j", h=16),
                    dsl(KFAC).rearrange("p (h o) -> p h o", o=1)
                    .broadcast_to([128, 16, 64]),
                    ALU.mult), ["PA", "BPA0", "BPA1", "dKF"], [f"ktok{cc}"])
                T_(lambda g: g.matmul(PX[0:16, 0:128], lrtok[:], identf[:],
                                      start=True, stop=True),
                   ["lrtok", "identf"], ["PX0", "BPX"])
                V_(lambda g, cc=cc: g.tensor_tensor_scan(
                    LTf[:, cc * CH:(cc + 1) * CH], PX[0:16, 0:128], zeros16[:],
                    0.0, ALU.add, ALU.add),
                    ["PX0", "BPX", "zeros16"], [f"LTf{cc}"])
                V_(lambda g, cc=cc: g.tensor_scalar(
                    nLT[:], LTf[:, cc * CH:(cc + 1) * CH], -1.0, None, ALU.mult),
                    [f"LTf{cc}"], ["nLT"])
                T_(lambda g: g.matmul(PX[0:128, 128:144], nLT[:], identf[0:16, 0:16],
                                      start=True, stop=True),
                   ["nLT", "identf"], ["PX0", "BPX"])
                V_(lambda g, cc=cc: g.tensor_copy(
                    negLtok[:, cc * H:(cc + 1) * H], PX[0:128, 128:144]),
                    ["PX0", "BPX"], [f"negL{cc}"])
                SC(lambda g, cc=cc: g.activation(
                    expLTf[:], LTf[:, cc * CH:(cc + 1) * CH], AF.Exp),
                    [f"LTf{cc}"], ["expLTf"])
                V_(lambda g, cc=cc: g.tensor_copy(
                    expLTb[:, cc * CH:(cc + 1) * CH], expLTf[:]),
                    ["expLTf"], [f"expLTb{cc}"])
                T_(lambda g: g.matmul(PX[0:1, 144:160], expLTf[:, 127:128],
                                      identf[0:16, 0:16], start=True, stop=True),
                   ["expLTf", "identf"], ["PX0", "BPX"])
                V_(lambda g: g.tensor_copy(pCrow[:], PX[0:1, 144:160]),
                   ["PX0", "BPX"], ["pCrow"])
                T_(lambda g: g.matmul(PX[0:128, 160:176], ones1f[:], pCrow[:],
                                      start=True, stop=True),
                   ["pCrow", "ones1f"], ["PX0", "BPX"])
                V_(lambda g, cc=cc: g.tensor_copy(
                    pCcols[:, cc * 16:(cc + 1) * 16], PX[0:128, 160:176]),
                    ["PX0", "BPX"], [f"pCcols{cc}"])
            for pc in range(8 if debug_stop not in (3,) else 0):
                T_(lambda g, pc=pc: g.matmul(
                    PX[:, 176:176 + GT], selpc[:, pc * 128:(pc + 1) * 128],
                    expLTb[:], start=True, stop=True),
                    [f"expLTb{c2}" for c2 in range(GRP)] + ["selpc"], ["PXb", "BPX"])
                for kc in range(8):
                    T_(lambda g, kc=kc, pc=pc: g.matmul(
                        PD[0:128, 0:GT],
                        wq_sb[:, kc * D + pc * 128:kc * D + (pc + 1) * 128],
                        lnxfm[:, kc * GT:(kc + 1) * GT],
                        start=(kc == 0), stop=(kc == 7)),
                        [f"lnxfm{kc}_{c2}" for c2 in range(GRP)] + [f"w0_{kc}"],
                        ["PDq", "BPD"] + (PD_ALL if kc == 0 else []), acc=(kc != 0))
                V_(lambda g: g.tensor_copy(PexpS[:], PX[:, 176:176 + GT]),
                   ["PXb", "BPX"], ["PexpS"])
                V_(lambda g, pc=pc: g.tensor_tensor(
                    Qpsb[:, pc * GT:(pc + 1) * GT], PD[0:128, 0:GT], PexpS[:],
                    ALU.mult), ["PDq", "BPD", "PexpS"], [f"qp{pc}"])
                V_(lambda g, pc=pc: g.tensor_copy(
                    Qsb[:, pc * GT:(pc + 1) * GT], PD[0:128, 0:GT]),
                    ["PDq", "BPD"], [f"q{pc}"])
                for kc in range(8):
                    T_(lambda g, kc=kc, pc=pc: g.matmul(
                        PD[0:128, 0:GT],
                        wv_sb[:, kc * D + pc * 128:kc * D + (pc + 1) * 128],
                        lnxfm[:, kc * GT:(kc + 1) * GT],
                        start=(kc == 0), stop=(kc == 7)),
                        [f"lnxfm{kc}_{c2}" for c2 in range(GRP)] + [f"w2_{kc}"],
                        ["PDv", "BPD"] + (PD_ALL if kc == 0 else []), acc=(kc != 0))
                SC(lambda g, pc=pc: g.copy(
                    Vsb[:, pc * GT:(pc + 1) * GT], PD[0:128, 0:GT]),
                    ["PDv", "BPD"], [f"v{pc}"])
            QUADS = [[0, 2, 4, 6], [8, 10, 12, 14], [1, 3, 5, 7], [9, 11, 13, 15]]
            acol = {}
            for q in range(4):
                for j, h in enumerate(QUADS[q]):
                    acol[h] = q * 512 + j * 128
            for cc in range(GRP if debug_stop not in (3, 4) else 0):
                c = c0 + cc
                tt = b * n_chunks + c
                T_(lambda g, cc=cc: g.matmul(
                    PX[0:1, 144:160], LTf[:, cc * CH + 127:cc * CH + 128],
                    identf[0:16, 0:16], start=True, stop=True),
                    [f"LTf{cc}", "identf"], ["PX0", "BPX"])
                V_(lambda g: g.tensor_copy(llrow[:], PX[0:1, 144:160]),
                   ["PX0", "BPX"], ["llrow"])
                T_(lambda g: g.matmul(PX[0:128, 160:176], ones1f[:], llrow[:],
                                      start=True, stop=True),
                   ["llrow", "ones1f"], ["PX0", "BPX"])
                V_(lambda g, cc=cc: g.tensor_tensor(
                    wtm[:], PX[0:128, 160:176],
                    negLtok[:, cc * H:(cc + 1) * H], ALU.add),
                    ["PX0", "BPX", f"negL{cc}"], ["wtm"])
                SC(lambda g: g.activation(wcol[:], wtm[:], AF.Exp),
                   ["wtm"], ["wcol"])
                def emit_E(q, cc=cc):
                    heads = QUADS[q]
                    q2 = (cc * 4 + q) % 3
                    eo = q2 * 512
                    l2 = (cc * 4 + q) % 2
                    lo = l2 * 512
                    for j, h in enumerate(heads):
                        dma("gpsimd", Lbcbuf[:, lo + j * 128:lo + (j + 1) * 128],
                            LTf[h:h + 1, cc * CH:(cc + 1) * CH]
                            .rearrange("p (o f) -> p o f", o=1)
                            .broadcast_to([1, 128, 128]),
                            [f"LTf{cc}"], [f"lbc{l2}_{j}"])
                    V_(lambda g, lo=lo: g.scalar_tensor_tensor(
                        Lbcbuf[:, lo:lo + 512].rearrange("p (o f) -> p o f", o=4),
                        Lbcbuf[:, lo:lo + 512].rearrange("p (o f) -> p o f", o=4),
                        1.0,
                        maskmin[:].rearrange("p (o f) -> p o f", o=1)
                        .broadcast_to([128, 4, 128]),
                        ALU.mult, ALU.min),
                        [f"lbc{l2}_{j2}" for j2 in range(4)] + ["maskmin"],
                        [f"lbc{l2}_{j2}" for j2 in range(4)])
                    for j, h in enumerate(heads):
                        SC(lambda g, h=h, j=j, eo=eo, lo=lo: g.activation(
                            Equad[:, eo + j * 128:eo + (j + 1) * 128],
                            Lbcbuf[:, lo + j * 128:lo + (j + 1) * 128], AF.Exp,
                            bias=negLtok[:, cc * H + h:cc * H + h + 1]),
                            [f"lbc{l2}_{j}", f"negL{cc}"], [f"Eq{q2}_{j}"])
                for q in range(3):
                    emit_E(q)
                if debug_stop == 8:
                    continue
                for q in range(4):
                    heads = QUADS[q]
                    qp = 0 if q < 2 else 1
                    APS = PE_ if qp == 0 else PC
                    akey = "PEq" if qp == 0 else "PCo"
                    bkey = "BPE" if qp == 0 else "BPC"
                    q2 = (cc * 4 + q) % 3
                    eo = q2 * 512
                    for j, h in enumerate(heads):
                        hp, pc = h % 2, h // 2
                        T_(lambda g, hp=hp, pc=pc, j=j, cc=cc, APS=APS: g.matmul(
                            APS[:, j * 128:(j + 1) * 128],
                            Vsb[hp * 64:(hp + 1) * 64,
                                pc * GT + cc * 128:pc * GT + (cc + 1) * 128],
                            Qsb[hp * 64:(hp + 1) * 64,
                                pc * GT + cc * 128:pc * GT + (cc + 1) * 128],
                            start=True, stop=True),
                            [f"v{pc}", f"q{pc}"], [f"{akey}{j}", bkey])
                    V_(lambda g, q=q, APS=APS, eo=eo: g.tensor_tensor(
                        Abuf[:, q * 512:(q + 1) * 512], APS[:],
                        Equad[:, eo:eo + 512], ALU.mult),
                        [f"{akey}{j2}" for j2 in range(4)] + [bkey]
                        + [f"Eq{q2}_{j2}" for j2 in range(4)],
                        [f"Ab{q}"])
                    if q == 0:
                        emit_E(3)
                if debug_stop in (8, 9):
                    continue
                for pc in range(8 if debug_stop != 12 else 0):
                    sl = pc % 2
                    T_(lambda g, pc=pc, cc=cc, sl=sl: g.matmul(
                        PX[:, 176 + sl * 128:176 + (sl + 1) * 128],
                        Vsb[:, pc * GT + cc * 128:pc * GT + (cc + 1) * 128],
                        identb[:], start=True, stop=True),
                        [f"v{pc}", "identb"], ["PXb" if sl == 0 else "PXb2", "BPX"])
                    V_(lambda g, pc=pc, sl=sl: g.tensor_tensor(
                        Vtbuf[:, pc * 128:(pc + 1) * 128]
                        .rearrange("p (h j) -> p h j", h=2),
                        PX[:, 176 + sl * 128:176 + (sl + 1) * 128]
                        .rearrange("p (h j) -> p h j", h=2),
                        wcol[:, 2 * pc:2 * pc + 2]
                        .rearrange("p (h o) -> p h o", o=1)
                        .broadcast_to([128, 2, 64]),
                        ALU.mult),
                        ["PXb" if sl == 0 else "PXb2", "BPX", "wcol"], [f"Vt{pc}"])
                for h in range(H if debug_stop not in (5, 12, 13) else 0):
                    hp, pc = h % 2, h // 2
                    first = (c == 0)
                    if not first:
                        T_(lambda g, hp=hp, pc=pc, cc=cc, h=h: g.matmul(
                            PB[hp * 64:(hp + 1) * 64, pc * 128:(pc + 1) * 128],
                            Sbf2[:, h * 64:(h + 1) * 64],
                            Qpsb[:, pc * GT + cc * 128:pc * GT + (cc + 1) * 128],
                            start=True, stop=False),
                            ["Sbf2", f"qp{pc}"], [f"PBy{pc}_{hp}", f"BPB{pc // 4}"])
                    T_(lambda g, hp=hp, pc=pc, cc=cc, h=h, first=first: g.matmul(
                        PB[hp * 64:(hp + 1) * 64, pc * 128:(pc + 1) * 128],
                        Ktok[:, cc * D + h * 64:cc * D + (h + 1) * 64],
                        Abuf[:, acol[h]:acol[h] + 128],
                        start=first, stop=True),
                        [f"ktok{cc}", f"Ab{acol[h] // 512}"],
                        [f"PBy{pc}_{hp}", f"BPB{pc // 4}"])
                for h in range(H if debug_stop not in (5, 12, 13) else 0):
                    hp, pc = h % 2, h // 2
                    T_(lambda g, hp=hp, pc=pc, cc=cc, h=h: g.matmul(
                        PD[hp * 64:(hp + 1) * 64, pc * 64:(pc + 1) * 64],
                        Vtbuf[:, h * 64:(h + 1) * 64],
                        Ktok[:, cc * D + h * 64:cc * D + (h + 1) * 64],
                        start=True, stop=True),
                        [f"Vt{pc}", f"ktok{cc}"],
                        [f"PDs{h}", "BPD"] + (["PDq", "PDv"] if h == 0 else []))
                for h in range(H if debug_stop not in (5, 12, 13) else 0):
                    hp, pc = h % 2, h // 2
                    V_(lambda g, hp=hp, pc=pc, cc=cc, h=h: g.scalar_tensor_tensor(
                        Sf32[hp * 64:(hp + 1) * 64, pc * 64:(pc + 1) * 64],
                        Sf32[hp * 64:(hp + 1) * 64, pc * 64:(pc + 1) * 64],
                        pCcols[hp * 64:(hp + 1) * 64, cc * 16 + h:cc * 16 + h + 1],
                        PD[hp * 64:(hp + 1) * 64, pc * 64:(pc + 1) * 64],
                        ALU.mult, ALU.add),
                        [f"pCcols{cc}", f"PDs{h}", "BPD", "Sf32"], ["Sf32"])
                yp = 0
                for pc in range(8 if debug_stop not in (5, 12, 13) else 0):
                    SC(lambda g, pc=pc, yp=yp: g.copy(
                        Yfm[:, yp * 1024 + pc * 128:yp * 1024 + (pc + 1) * 128],
                        PB[:, pc * 128:(pc + 1) * 128]),
                        [f"PBy{pc}_0", f"PBy{pc}_1", f"BPB{pc // 4}"],
                        [f"yfm{yp}_{pc}"])
                if debug_stop in (12, 13):
                    continue
                SC(lambda g: g.copy(
                    Sbf2[0:64, :].rearrange("p (b two j) -> p b two j", two=2, j=64)
                    [:, :, 0, :],
                    Sf32[0:64, :].rearrange("p (b j) -> p b j", j=64)),
                   ["Sf32"], ["Sbf2"])
                SC(lambda g: g.copy(
                    Sbf2[64:128, :].rearrange("p (b two j) -> p b two j", two=2, j=64)
                    [:, :, 1, :],
                    Sf32[64:128, :].rearrange("p (b j) -> p b j", j=64)),
                   ["Sf32"], ["Sbf2"])
                for eh in range(2 if debug_stop not in (5, 6, 8, 9, 12, 13) else 0):
                    for kc in range(8):
                        T_(lambda g, eh=eh, kc=kc, yp=yp: g.matmul(
                            PA[:, eh * 512:(eh + 1) * 512],
                            Yfm[:, yp * 1024 + kc * 128:yp * 1024 + (kc + 1) * 128],
                            wo_sb[:, kc * D + eh * 512:kc * D + (eh + 1) * 512],
                            start=(kc == 0), stop=(kc == 7)),
                            [f"yfm{yp}_{kc}", f"w3_{kc}"],
                            ["PA", f"BPA{eh}"])
                if debug_stop in (5, 6, 8, 9, 12, 13):
                    continue
                SC(lambda g: g.activation(aobuf[:], PA[:], AF.Copy, accum_out=st(4)),
                   ["PA", "BPA0", "BPA1"], ["ao", "lnaa"])
                SC(lambda g: g.activation(g2048f[:, 0:D], aobuf[:], AF.Square,
                                          accum_out=st(5)),
                   ["ao"], ["g2048a", "lnab"])
                ln_stats(4, 5, 6, D, "lna")
                SC(lambda g: g.activation(hnbuf[:], aobuf[:], AF.Identity, bias=st(7),
                                          scale=st(6)),
                   ["ao", "lnac", "lnad"], ["hn"])
                x2sl = X2[:, tt * D:(tt + 1) * D]
                V_(lambda g, x2sl=x2sl: g.tensor_tensor(x2sl, x2sl, hnbuf[:], ALU.add),
                   [f"x2:{tt}", "hn"], [f"x2:{tt}"])

    S.barrier()
    if debug_stop in (1, 3, 4, 5, 6, 8, 9, 12, 13):
        for tt in range(NTT):
            V_(lambda g, tt=tt: g.tensor_copy(outst_dbg[:], X2[:, tt * D:(tt + 1) * D]),
               [f"x2:{tt}", "odbg"], ["odbg"])
            dma("gpsimd", yd[tt * CH:tt * CH + 128, :], outst_dbg[:],
                ["odbg"], [f"ydone{tt}", "odbg"])
        S.emit("sync", None, reads=[f"ydone{tt}" for tt in range(NTT)], writes=[])
        S.emit("gpsimd", None, reads=[f"ydone{tt}" for tt in range(NTT)], writes=[])
        return nc
    p1.close()

    p2 = ExitStack()

    def sb2(name, shape, dt):
        return p2.enter_context(nc.sbuf_tensor("s_" + name, shape, dt))

    qw2sb = sb2("qw2sb", [128, 32 * D], BF16)
    wexp1 = sb2("wexp1", [128, D], F16)
    bexp1 = sb2("bexp1", [128, D], F16)
    wexp2 = sb2("wexp2", [128, HID], F16)
    bexp2 = sb2("bexp2", [128, HID], F16)
    H2 = sb2("H2", [128, 2 * HID], F16)
    qhtok = sb2("qhtok", [128, HID], BF16)
    qhfm = sb2("qhfm", [128, 2 * 512], BF16)
    qxtok = sb2("qxtok", [128, D], BF16)
    qxfm = sb2("qxfm", [128, 8 * GT], BF16)
    h1b = sb2("h1b", [128, D], F16)
    qw1buf = sb2("qw1buf", [128, 2 * 512], BF16)
    outstage = sb2("outstage", [128, D], F32)
    fobuf = sb2("fobuf", [128, D], F16)
    hn3buf = sb2("hn3buf", [128, D], BF16)
    screw = sb2("screw", [1, 2 * 128], BF16)
    screw2 = sb2("screw2", [1, 2 * 128], BF16)
    c1col = sb2("c1col", [128, 2], F32)
    c2col = sb2("c2col", [128, 2], F32)
    s2cols = sb2("s2cols", [128, 2], F32)
    gacc = sb2("gacc", [128, 2 * 8], F32)

    pxkeys = ["PX0", "PXw", "PXb", "PXb2", "BPX"]
    for (dst, row, n, key) in ((wexp1, 0, D, "wexp1"), (bexp1, 1, D, "bexp1"),
                               (wexp2, 2, HID, "wexp2"), (bexp2, 3, HID, "bexp2")):
        for j0 in range(0, n, 512):
            dma("sync", rstgf[:], ln_d[row:row + 1, j0:j0 + 512], [], ["rstgf"])
            V_(lambda g: g.tensor_copy(rstgb[:], rstgf[:]), ["rstgf"], ["rstgb"])
            T_(lambda g: g.matmul(PX[:, 0:512], onesbf[:], rstgb[:],
                                  start=True, stop=True),
               ["onesbf", "rstgb"], pxkeys)
            V_(lambda g, dst=dst, j0=j0: g.tensor_copy(dst[:, j0:j0 + 512],
                                                       PX[:, 0:512]),
               pxkeys, [key])

    for ch in range(32):
        dma("sync", qw2sb[:, ch * D:(ch + 1) * D], qw2_s[ch * 128:(ch + 1) * 128, :],
            ["wqstore"], [f"qw2r{ch}"])

    if debug_stop == 25:
        for tt in range(NTT):
            dma("sync", qhtok[:, 0:D], qw1_s[tt * 128:(tt + 1) * 128, 0:D],
                ["wqstore", "qhd"], ["qhq"])
            V_(lambda g: g.tensor_copy(outstage[:], qhtok[:, 0:D]),
               ["qhq", "ost"], ["ost", "qhd"])
            dma("gpsimd", yd[tt * CH:tt * CH + 128, :], outstage[:],
                ["ost"], [f"ydone{tt}", "ost"])
        S.emit("sync", None, reads=[f"ydone{tt}" for tt in range(NTT)], writes=[])
        S.emit("gpsimd", None, reads=[f"ydone{tt}" for tt in range(NTT)], writes=[])
        return nc
    NT2 = NTT // 2
    def emitA(g2):
            for tt2 in range(2):
                tt = g2 * 2 + tt2
                x2sl = X2[:, tt * D:(tt + 1) * D]
                V_(lambda g, x2sl=x2sl: g.tensor_scalar(
                    outstage[:], x2sl, 1.0, 0.0, ALU.mult, ALU.add,
                    accum_out=st(0)),
                    [f"x2:{tt}"], ["ost", "f1a"])
                SC(lambda g, x2sl=x2sl: g.activation(
                    g2048f[:, 0:D], x2sl, AF.Square, accum_out=st(1)),
                    [f"x2:{tt}"], ["g2048a", "f1b"])
                ln_stats(0, 1, 2, D, "f1")
                SC(lambda g, x2sl=x2sl: g.activation(h1b[:], x2sl, AF.Identity,
                                                     bias=st(3), scale=st(2)),
                   [f"x2:{tt}", "f1c", "f1d"], ["h1"])
                V_(lambda g: g.tensor_tensor(h1b[:], h1b[:], wexp1[:], ALU.mult),
                   ["h1", "wexp1"], ["h1"])
                V_(lambda g: g.tensor_tensor(h1b[:], h1b[:], bexp1[:], ALU.add),
                   ["h1", "bexp1"], ["h1"])
                V_(lambda g: g.tensor_reduce(st(8), h1b[:], AX.X, ALU.max,
                                             apply_absolute_value=True),
                   ["h1"], ["mx1"])
                V_(lambda g: g.tensor_scalar(st(8), st(8), 1e-5, None, ALU.max),
                   ["mx1"], ["mx1"])
                V_(lambda g: g.tensor_scalar(st(9), st(8), 1.0 / 127.0, None, ALU.mult),
                   ["mx1"], ["s1col"])
                V_(lambda g: g.reciprocal(st(9), st(9)), ["s1col"], ["s1col"])
                V_(lambda g: g.tensor_tensor(st(10), st(9), swcol[:, 0:1], ALU.mult),
                   ["s1col", "swcol0"], ["sc1f"])
                V_(lambda g, tt2=tt2: g.reciprocal(c1col[:, tt2:tt2 + 1], st(10)),
                   ["sc1f"], [f"c1col{tt2}"])
                PL(lambda g: g.tensor_copy(statb1[:], st(10)), ["sc1f"], ["statb1"])
                T_(lambda g: g.matmul(PX[0:1, 0:128], statb1[:], identb[:],
                                      start=True, stop=True),
                   ["statb1", "identb"], ["PX0", "BPX"])
                V_(lambda g, tt2=tt2: g.tensor_copy(
                    screw[:, tt2 * 128:(tt2 + 1) * 128], PX[0:1, 0:128]),
                    ["PX0", "BPX"], [f"sc1row{tt2}"])
                V_(lambda g: g.tensor_scalar(g2048f[:, 0:D], h1b[:], st(9), KT,
                                             ALU.mult, ALU.add),
                   ["h1", "s1col"], ["g2048a"])
                V_(lambda g: g.tensor_copy(qi32buf[:, 0:D], g2048f[:, 0:D]),
                   ["g2048a"], ["qi32"])
                V_(lambda g: g.scalar_tensor_tensor(
                    outstage[:], qi32buf[:, 0:D], 0.5, g2048f[:, 0:D],
                    ALU.add, ALU.is_le), ["qi32", "g2048a", "ost"], ["ost"])
                V_(lambda g: g.scalar_tensor_tensor(
                    qxtok[:], qi32buf[:, 0:D], -KT, outstage[:],
                    ALU.add, ALU.add), ["qi32", "ost"], ["qx"])
                if debug_stop in (22, 23):
                    tt_d = g2 * 2 + tt2
                    srcb = qxtok if debug_stop == 22 else h1b
                    V_(lambda g, srcb=srcb: g.tensor_copy(outstage[:], srcb[:]),
                       ["qx", "h1", "ost"], ["ost"])
                    dma("gpsimd", yd[tt_d * CH:tt_d * CH + 128, :], outstage[:],
                        ["ost"], [f"ydone{tt_d}", "ost"])
                    continue
                for half in range(2):
                    for kq in range(4):
                        kc = half * 4 + kq
                        T_(lambda g, kc=kc, kq=kq: g.matmul(
                            PE_[:, kq * 128:(kq + 1) * 128],
                            qxtok[:, kc * 128:(kc + 1) * 128], identb[:],
                            start=True, stop=True),
                            ["qx", "identb"], [f"PEq{kq}", "BPE"])
                    V_(lambda g, half=half, tt2=tt2: g.tensor_copy(
                        qxfm[:].rearrange("p (k t) -> p k t", k=8)
                        [:, half * 4:half * 4 + 4, tt2 * 128:(tt2 + 1) * 128],
                        PE_[:].rearrange("p (k t) -> p k t", k=4)),
                        [f"PEq{kq}" for kq in range(4)] + ["BPE"],
                        [f"qxfm{k2}_{tt2}" for k2 in range(half * 4, half * 4 + 4)])

    emitA(0)
    for g2 in range(NT2):
        if debug_stop in (22, 23):
            continue
        for hs in range(8):
            for kc in range(8):
                sl = (hs * 8 + kc) % 2
                dma("sync", qw1buf[:, sl * 512:(sl + 1) * 512],
                    qw1_s[kc * 128:(kc + 1) * 128, hs * 512:(hs + 1) * 512],
                    ["wqstore"], [f"qw1b{sl}"])
                for tt2 in range(2):
                    T_(lambda g, kc=kc, sl=sl, tt2=tt2: g.matmul(
                        PA[:, tt2 * 512:(tt2 + 1) * 512],
                        qxfm[:, kc * GT + tt2 * 128:kc * GT + (tt2 + 1) * 128],
                        qw1buf[:, sl * 512:(sl + 1) * 512],
                        start=(kc == 0), stop=False),
                        [f"qxfm{kc}_{tt2}", f"qw1b{sl}"],
                        [f"PAf{tt2}", f"BPA{tt2}"], acc=(kc != 0))
            dma("sync", rstgf[:], b1_d[0:1, hs * 512:(hs + 1) * 512],
                [], ["rstgf"])
            PL(lambda g: g.tensor_copy(rstgb[:], rstgf[:]), ["rstgf"], ["rstgb"])
            for tt2 in range(2):
                T_(lambda g, tt2=tt2: g.matmul(
                    PA[:, tt2 * 512:(tt2 + 1) * 512],
                    screw[0:1, tt2 * 128:(tt2 + 1) * 128],
                    rstgb[:], start=False, stop=True),
                    [f"sc1row{tt2}", "rstgb"], [f"PAf{tt2}", f"BPA{tt2}"], acc=True)
                SC(lambda g, hs=hs, tt2=tt2: g.activation(
                    H2[:, tt2 * HID + hs * 512:tt2 * HID + (hs + 1) * 512],
                    PA[:, tt2 * 512:(tt2 + 1) * 512], AF.Gelu,
                    scale=c1col[:, tt2:tt2 + 1],
                    accum_out=gacc[:, tt2 * 8 + hs:tt2 * 8 + hs + 1]),
                    [f"PAf{tt2}", f"BPA{tt2}", f"c1col{tt2}"],
                    [f"H2_{tt2}", f"gacc{tt2}_{hs}"])
        if debug_stop == 21:
            for tt2 in range(2):
                tt = g2 * 2 + tt2
                V_(lambda g, tt2=tt2: g.tensor_copy(
                    outstage[:], H2[:, tt2 * HID:tt2 * HID + D]),
                    [f"H2_{tt2}", "ost"], ["ost"])
                dma("gpsimd", yd[tt * CH:tt * CH + 128, :], outstage[:],
                    ["ost"], [f"ydone{tt}", "ost"])
            continue
        if g2 + 1 < NT2:
            emitA(g2 + 1)
        for tt2 in range(2):
            tt = g2 * 2 + tt2
            hsl = H2[:, tt2 * HID:(tt2 + 1) * HID]
            V_(lambda g, tt2=tt2: g.tensor_reduce(
                st(12), gacc[:, tt2 * 8:(tt2 + 1) * 8], AX.X, ALU.add),
                [f"gacc{tt2}_{hs}" for hs in range(8)], ["f2a"])
            for qtr in range(4):
                SC(lambda g, hsl=hsl, qtr=qtr: g.activation(
                    g2048f[:, 0:1024], hsl[:, qtr * 1024:(qtr + 1) * 1024],
                    AF.Square, accum_out=st(28 + qtr)),
                    [f"H2_{tt2}"], ["g2048a", f"f2sq{qtr}"])
            V_(lambda g: g.tensor_reduce(st(13), statp[:, 28:32], AX.X, ALU.add),
               [f"f2sq{q2}" for q2 in range(4)], ["f2b"])
            ln_stats(12, 13, 14, HID, "f2")
            SC(lambda g, hsl=hsl, tt2=tt2: g.activation(
                hsl, hsl, AF.Identity, bias=st(15), scale=st(14)),
                [f"H2_{tt2}", "f2c", "f2d"], [f"H2_{tt2}"])
            V_(lambda g, hsl=hsl, tt2=tt2: g.tensor_tensor(
                hsl, hsl, wexp2[:], ALU.mult), [f"H2_{tt2}", "wexp2"], [f"H2_{tt2}"])
            V_(lambda g, hsl=hsl, tt2=tt2: g.tensor_tensor(
                hsl, hsl, bexp2[:], ALU.add), [f"H2_{tt2}", "bexp2"], [f"H2_{tt2}"])
            V_(lambda g, hsl=hsl: g.tensor_reduce(
                st(24), hsl, AX.X, ALU.max, apply_absolute_value=True),
                [f"H2_{tt2}"], ["mx2"])
            V_(lambda g: g.tensor_scalar(st(24), st(24), 1e-5, None, ALU.max),
               ["mx2"], ["mx2"])
            V_(lambda g, tt2=tt2: g.tensor_scalar(
                s2cols[:, tt2:tt2 + 1], st(24), 1.0 / 127.0, None, ALU.mult),
               ["mx2"], [f"s2col{tt2}"])
            V_(lambda g, tt2=tt2: g.reciprocal(s2cols[:, tt2:tt2 + 1],
                                               s2cols[:, tt2:tt2 + 1]),
               [f"s2col{tt2}"], [f"s2col{tt2}"])
            V_(lambda g, tt2=tt2: g.tensor_tensor(st(26), s2cols[:, tt2:tt2 + 1],
                                                  swcol[:, 1:2], ALU.mult),
               [f"s2col{tt2}", "swcol1"], ["sc2f"])
            V_(lambda g, tt2=tt2: g.reciprocal(c2col[:, tt2:tt2 + 1], st(26)),
               ["sc2f"], [f"c2col{tt2}"])
            PL(lambda g: g.tensor_copy(statb1[:], st(26)), ["sc2f"], ["statb1"])
            T_(lambda g: g.matmul(PX[0:1, 0:128], statb1[:], identb[:],
                                  start=True, stop=True),
               ["statb1", "identb"], ["PX0", "BPX"])
            V_(lambda g, tt2=tt2: g.tensor_copy(
                screw2[:, tt2 * 128:(tt2 + 1) * 128], PX[0:1, 0:128]),
                ["PX0", "BPX"], [f"sc2row{tt2}"])
        for tt2 in range(2):
            tt = g2 * 2 + tt2
            hsl = H2[:, tt2 * HID:(tt2 + 1) * HID]
            for qtr in range(4):
                V_(lambda g, hsl=hsl, qtr=qtr, tt2=tt2: g.tensor_scalar(
                    g2048f[:, 0:1024], hsl[:, qtr * 1024:(qtr + 1) * 1024],
                    s2cols[:, tt2:tt2 + 1], KT, ALU.mult, ALU.add),
                    [f"H2_{tt2}", f"s2col{tt2}"], ["g2048a"])
                V_(lambda g: g.tensor_copy(qi32buf[:, 0:1024], g2048f[:, 0:1024]),
                   ["g2048a"], ["qi32"])
                V_(lambda g: g.scalar_tensor_tensor(
                    outstage[:], qi32buf[:, 0:1024], 0.5, g2048f[:, 0:1024],
                    ALU.add, ALU.is_le), ["qi32", "g2048a", "ost"], ["ost"])
                V_(lambda g, qtr=qtr: g.scalar_tensor_tensor(
                    qhtok[:, qtr * 1024:(qtr + 1) * 1024], qi32buf[:, 0:1024],
                    -KT, outstage[:], ALU.add, ALU.add),
                    ["qi32", "ost"], [f"qh{qtr // 2}"])
            for kb in range(8):
                for kq in range(4):
                    kc = kb * 4 + kq
                    T_(lambda g, kc=kc, kq=kq: g.matmul(
                        PE_[:, kq * 128:(kq + 1) * 128],
                        qhtok[:, kc * 128:(kc + 1) * 128], identb[:],
                        start=True, stop=True),
                        [f"qh{kc // 16}", "identb"], [f"PEq{kq}", "BPE"])
                sl2 = kb % 2
                if sl2 == 0:
                    V_(lambda g, sl2=sl2: g.tensor_copy(
                        qhfm[:, sl2 * 512:(sl2 + 1) * 512], PE_[:]),
                        [f"PEq{kq}" for kq in range(4)] + ["BPE"], [f"qhfm{sl2}"])
                else:
                    SC(lambda g, sl2=sl2: g.copy(
                        qhfm[:, sl2 * 512:(sl2 + 1) * 512], PE_[:]),
                        [f"PEq{kq}" for kq in range(4)] + ["BPE"], [f"qhfm{sl2}"])
                for kq in range(4):
                    kc = kb * 4 + kq
                    for eh in range(2):
                        PT = PC if eh == 0 else PD
                        T_(lambda g, kc=kc, eh=eh, PT=PT, sl2=sl2, kq=kq: g.matmul(
                            PT[:], qhfm[:, sl2 * 512 + kq * 128:sl2 * 512 + (kq + 1) * 128],
                            qw2sb[:, kc * D + eh * 512:kc * D + (eh + 1) * 512],
                            start=(kc == 0), stop=False),
                            [f"qhfm{sl2}", f"qw2r{kc}"],
                            [f"Pw2{eh}", "BPC" if eh == 0 else "BPD"],
                            acc=(kc != 0))
            for eh in range(2):
                PT = PC if eh == 0 else PD
                dma("sync", rstgf[:], b2_d[0:1, eh * 512:(eh + 1) * 512],
                    [], ["rstgf"])
                PL(lambda g: g.tensor_copy(rstgb[:], rstgf[:]), ["rstgf"], ["rstgb"])
                T_(lambda g, eh=eh, tt2=tt2, PT=PT: g.matmul(
                    PT[:], screw2[0:1, tt2 * 128:(tt2 + 1) * 128],
                    rstgb[:], start=False, stop=True),
                    [f"sc2row{tt2}", "rstgb"],
                    [f"Pw2{eh}", "BPC" if eh == 0 else "BPD"], acc=True)
                SC(lambda g, eh=eh, tt2=tt2, PT=PT: g.activation(
                    fobuf[:, eh * 512:(eh + 1) * 512], PT[:], AF.Copy,
                    scale=c2col[:, tt2:tt2 + 1], accum_out=st(16 + eh)),
                    [f"Pw2{eh}", "BPC" if eh == 0 else "BPD", f"c2col{tt2}"],
                    [f"fo{eh}", f"foacc{eh}"])
            V_(lambda g: g.tensor_tensor(st(18), st(16), st(17), ALU.add),
               ["foacc0", "foacc1"], ["f3a"])
            SC(lambda g: g.activation(g2048f[:, 0:D], fobuf[:], AF.Square,
                                      accum_out=st(19)),
               ["fo0", "fo1"], ["g2048a", "f3b"])
            ln_stats(18, 19, 20, D, "f3")
            SC(lambda g: g.activation(hn3buf[:], fobuf[:], AF.Identity, bias=st(21),
                                      scale=st(20)),
               ["fo0", "fo1", "f3c", "f3d"], ["hn3"])
            V_(lambda g, tt=tt: g.tensor_tensor(
                outstage[:], hn3buf[:], X2[:, tt * D:(tt + 1) * D], ALU.add),
                ["hn3", f"x2:{tt}"], ["ost"])
            dma("gpsimd", yd[tt * CH:tt * CH + 128, :], outstage[:],
                ["ost"], [f"ydone{tt}", "ost"])

    S.emit("sync", None, reads=[f"ydone{tt}" for tt in range(NTT)], writes=[])
    S.emit("gpsimd", None, reads=[f"ydone{tt}" for tt in range(NTT)], writes=[])
    p2.close()
    es.close()
    return nc


def kernel(**inputs):
    n_cores = 8
    B, Tfull, Dm = inputs["L0_attn_x"].shape
    n_seq = B // n_cores
    n_chunks = Tfull // CH

    x = np.ascontiguousarray(inputs["L0_attn_x"], dtype=np.float32)
    wqT = np.ascontiguousarray(np.asarray(inputs["L0_attn_Wq"]).T)
    wkT = np.ascontiguousarray(np.asarray(inputs["L0_attn_Wk"]).T)
    wvT = np.ascontiguousarray(np.asarray(inputs["L0_attn_Wv"]).T)
    woT = np.ascontiguousarray(np.asarray(inputs["L0_attn_Wo"]).T)
    w1T = np.ascontiguousarray(np.asarray(inputs["L0_ffn_w1"]).T)
    w2T = np.ascontiguousarray(np.asarray(inputs["L0_ffn_w2"]).T)
    gl = np.stack([inputs["L0_attn_W_gamma"], inputs["L0_attn_b_gamma"],
                   inputs["L0_attn_W_lambda"], inputs["L0_attn_b_lambda"]]
                  ).astype(np.float32)
    lnp = np.zeros((4, HID), np.float32)
    lnp[0, :Dm] = inputs["L0_ffn_ln1_w"]
    lnp[1, :Dm] = inputs["L0_ffn_ln1_b"]
    lnp[2, :] = inputs["L0_ffn_ln2_w"]
    lnp[3, :] = inputs["L0_ffn_ln2_b"]
    b1 = np.asarray(inputs["L0_ffn_b1"]).reshape(1, HID).astype(np.float32)
    b2 = np.asarray(inputs["L0_ffn_b2"]).reshape(1, Dm).astype(np.float32)

    identb = np.eye(128, dtype=ml_dtypes.bfloat16)
    identf = np.eye(128, dtype=np.float32)
    ieye2 = np.concatenate([np.eye(64), np.eye(64)], axis=0).astype(ml_dtypes.bfloat16)
    sidx = np.arange(128)[:, None]
    tidx = np.arange(128)[None, :]
    maskmin = np.where(sidx <= tidx, 0.0, -1e30).astype(np.float32)
    selpc = np.zeros((16, 8 * 128), np.float32)
    for pc in range(8):
        selpc[2 * pc, pc * 128:pc * 128 + 64] = 1.0
        selpc[2 * pc + 1, pc * 128 + 64:pc * 128 + 128] = 1.0
    selpc = selpc.astype(ml_dtypes.bfloat16)

    nc = build_program(n_seq, n_chunks)

    shared = {"wqT": wqT, "wkT": wkT, "wvT": wvT, "woT": woT, "w1T": w1T,
              "w2T": w2T, "gl": gl, "lnp": lnp, "b1": b1, "b2": b2,
              "identb": identb, "identf": identf, "ieye2": ieye2,
              "maskmin": maskmin, "selpc": selpc}
    in_maps = []
    for ci in range(n_cores):
        m = dict(shared)
        m["x"] = np.ascontiguousarray(
            x[ci * n_seq:(ci + 1) * n_seq].reshape(n_seq * Tfull, Dm))
        in_maps.append(m)
    res = run_bass_kernel_spmd(nc, in_maps, core_ids=list(range(n_cores)))
    out = np.empty((B, Tfull, Dm), np.float32)
    for ci in range(n_cores):
        out[ci * n_seq:(ci + 1) * n_seq] = np.asarray(
            res.results[ci]["y"]).reshape(n_seq, Tfull, Dm)
    return out


# version 19
# speedup vs baseline: 1.0789x; 1.0173x over previous
import numpy as np
import ml_dtypes
import concourse.bass as bass
import concourse.mybir as mybir
from concourse.bass_utils import run_bass_kernel_spmd
from contextlib import ExitStack

F32 = mybir.dt.float32
BF16 = mybir.dt.bfloat16
I32 = mybir.dt.int32
F16 = mybir.dt.float16
AF = mybir.ActivationFunctionType
ALU = mybir.AluOpType
AX = mybir.AxisListType

D = 1024
H = 16
DK = 64
HID = 4096
CH = 128
KT = 256.0
KW = 256.0

ENGS = ("tensor", "vector", "scalar", "sync", "gpsimd")
DMA_ENGS = {"sync", "gpsimd"}


class Sched:
    NSLOT = 8

    def __init__(self, nc, sems):
        self.nc = nc
        self.sems = sems
        self.idx = {}
        self.qcnt = {e: 0 for e in DMA_ENGS}
        self.last_w = {}
        self.readers = {}
        self.known = {e: {} for e in ENGS}
        self.snap = {}

    @staticmethod
    def phys(pe):
        return pe.split(":")[0]

    def _sem(self, pe):
        if ":" in pe:
            e, r = pe.split(":")
            return self.sems[e][int(r)]
        return self.sems[pe]

    def _wait(self, eng, kn, pe, di):
        if kn.get(pe, 0) >= di:
            return
        val = di * 16 if ":" in pe else di
        eng.wait_ge(self._sem(pe), val)
        sn = self.snap.get((pe, di))
        if sn is not None:
            for e3, v in sn.items():
                if v > kn.get(e3, 0):
                    kn[e3] = v
        if di > kn.get(pe, 0):
            kn[pe] = di

    def emit(self, e, fn, reads=(), writes=(), acc=False):
        eng = getattr(self.nc, e)
        kn = self.known[e]
        if e in DMA_ENGS:
            r = self.qcnt[e] % self.NSLOT
            self.qcnt[e] += 1
            pe = f"{e}:{r}"
        else:
            pe = e
        deps = []
        for k in reads:
            w = self.last_w.get(k)
            if w is not None:
                deps.append(w)
        for k in writes:
            w = self.last_w.get(k)
            if w is not None:
                deps.append(w)
            for rd in self.readers.get(k, ()):
                deps.append(rd)
        if e in DMA_ENGS and self.idx.get(pe, 0) > 0:
            deps.append((pe, self.idx[pe]))
        myidx = self.idx.get(pe, 0) + 1
        need = {}
        for (de, di) in deps:
            if de == pe and di >= myidx:
                continue
            if acc and de == "tensor" and e == "tensor":
                continue
            if de not in need or need[de] < di:
                need[de] = di
        order = ([de for de in need if self.phys(de) != e]
                 + [de for de in need if self.phys(de) == e])
        for de in order:
            self._wait(eng, kn, de, need[de])
        if fn is None:
            return None
        inst = fn(eng)
        inst.then_inc(self._sem(pe), 16 if e in DMA_ENGS else 1)
        self.idx[pe] = myidx
        self.snap[(pe, myidx)] = dict(kn)
        op = (pe, myidx)
        for k in writes:
            self.last_w[k] = op
            self.readers[k] = []
        for k in reads:
            self.readers.setdefault(k, []).append(op)
        return op

    def barrier(self):
        tails = dict(self.idx)
        for e in ENGS:
            eng = getattr(self.nc, e)
            kn = self.known[e]
            for pe, di in tails.items():
                if self.phys(pe) == e or di == 0:
                    continue
                self._wait(eng, kn, pe, di)


def build_program(n_seq, n_chunks, debug_stop=0):
    T = n_chunks * CH
    NTOK = n_seq * T
    NTT = n_seq * n_chunks
    GRP = 2
    assert n_chunks % GRP == 0
    GT = GRP * CH

    nc = bass.Bass("TRN2", target_bir_lowering=False)

    xd = nc.dram_tensor("x", [NTOK, D], F32, kind="ExternalInput").ap()
    wT_d = {nm: nc.dram_tensor(nm, [D, D], F32, kind="ExternalInput").ap()
            for nm in ("wqT", "wkT", "wvT", "woT")}
    w1T_d = nc.dram_tensor("w1T", [D, HID], F32, kind="ExternalInput").ap()
    w2T_d = nc.dram_tensor("w2T", [HID, D], F32, kind="ExternalInput").ap()
    gl_d = nc.dram_tensor("gl", [4, H], F32, kind="ExternalInput").ap()
    ln_d = nc.dram_tensor("lnp", [4, HID], F32, kind="ExternalInput").ap()
    b1_d = nc.dram_tensor("b1", [1, HID], F32, kind="ExternalInput").ap()
    b2_d = nc.dram_tensor("b2", [1, D], F32, kind="ExternalInput").ap()
    identb_d = nc.dram_tensor("identb", [128, 128], BF16, kind="ExternalInput").ap()
    identf_d = nc.dram_tensor("identf", [128, 128], F32, kind="ExternalInput").ap()
    ieye2_d = nc.dram_tensor("ieye2", [128, 64], BF16, kind="ExternalInput").ap()
    maskmin_d = nc.dram_tensor("maskmin", [128, 128], F32, kind="ExternalInput").ap()
    selpc_d = nc.dram_tensor("selpc", [16, 8 * 128], BF16, kind="ExternalInput").ap()
    yd = nc.dram_tensor("y", [NTOK, D], F32, kind="ExternalOutput").ap()
    qw1_s = nc.dram_tensor("qw1s", [D, HID], BF16, kind="Internal").ap()
    qw2_s = nc.dram_tensor("qw2s", [HID, D], BF16, kind="Internal").ap()

    es = ExitStack()

    def sb(name, shape, dt):
        return es.enter_context(nc.sbuf_tensor("s_" + name, shape, dt))

    def pt(name, shape, dt=F32):
        return es.enter_context(nc.psum_tensor(name, shape, dt))

    sems = {}
    for e in ENGS:
        if e in DMA_ENGS:
            sems[e] = [es.enter_context(nc.semaphore(f"sem_{e}{r}"))
                       for r in range(Sched.NSLOT)]
        else:
            sems[e] = es.enter_context(nc.semaphore("sem_" + e))
    S = Sched(nc, sems)

    X2 = sb("X2", [128, NTT * D], BF16)
    identb = sb("identb", [128, 128], BF16)
    ones1f = sb("ones1f", [1, 128], F32)
    onesbf = sb("onesbf", [1, 128], BF16)
    onescolf = sb("onescolf", [128, 1], F32)
    statp = sb("statp", [128, 64], F32)
    statb1 = sb("statb1", [128, 1], BF16)
    g2048f = sb("g2048f", [128, 1024], F32)
    qi32buf = sb("qi32buf", [128, 1024], I32)
    swcol = sb("swcol", [128, 2], F32)
    rstgf = sb("rstgf", [1, 512], F32)
    rstgb = sb("rstgb", [1, 512], BF16)
    e24col = sb("e24col", [128, 1], F32)
    e30col = sb("e30col", [128, 1], F32)
    nKTcol = sb("nKTcol", [128, 1], F32)
    nKWcol = sb("nKWcol", [128, 1], F32)

    PA = pt("PA", [128, 1024])
    PB = pt("PB", [128, 1024])
    PC = pt("PC", [128, 512])
    PD = pt("PD", [128, 512])
    PE_ = pt("PE", [128, 512])
    PX = pt("PX", [128, 512])
    PD_ALL = ["PDq", "PDv"] + [f"PDs{h}" for h in range(H)]

    def dma(q, out, in_, rk, wk):
        return S.emit(q, lambda g: g.dma_start(out, in_), reads=rk, writes=wk)

    def T_(fn, rk, wk, acc=True):
        return S.emit("tensor", fn, reads=rk, writes=wk, acc=True)

    def V_(fn, rk, wk):
        return S.emit("vector", fn, reads=rk, writes=wk)

    def SC(fn, rk, wk):
        return S.emit("scalar", fn, reads=rk, writes=wk)

    def st(i, n=1):
        return statp[:, i:i + n]

    def ln_stats(a, b, c, N, tag):
        V_(lambda g: g.tensor_scalar(st(a), st(a), 1.0 / N, None, ALU.mult),
           [f"{tag}a"], [f"{tag}a"])
        V_(lambda g: g.tensor_scalar(st(b), st(b), 1.0 / N, None, ALU.mult),
           [f"{tag}b"], [f"{tag}b"])
        V_(lambda g: g.scalar_tensor_tensor(st(b), st(a), st(a), st(b),
                                            ALU.mult, ALU.subtract),
           [f"{tag}a", f"{tag}b"], [f"{tag}b"])
        V_(lambda g: g.tensor_scalar(st(b), st(b), -1.0, 1e-5, ALU.mult, ALU.add),
           [f"{tag}b"], [f"{tag}b"])
        SC(lambda g: g.activation(st(c), st(b), AF.Sqrt),
           [f"{tag}b"], [f"{tag}c"])
        V_(lambda g: g.reciprocal(st(c), st(c)), [f"{tag}c"], [f"{tag}c"])
        V_(lambda g: g.scalar_tensor_tensor(st(c + 1), st(a), -1.0, st(c),
                                            ALU.mult, ALU.mult),
           [f"{tag}a", f"{tag}c"], [f"{tag}d"])

    dma("sync", identb[:], identb_d, [], ["identb"])
    V_(lambda g: g.memset(ones1f[:], 1.0), [], ["ones1f"])
    V_(lambda g: g.memset(onesbf[:], 1.0), [], ["onesbf"])
    V_(lambda g: g.memset(onescolf[:], 1.0), [], ["onescolf"])
    V_(lambda g: g.memset(e24col[:], 1e-24), [], ["ccol"])
    V_(lambda g: g.memset(e30col[:], 1e-30), [], ["ccol"])
    V_(lambda g: g.memset(nKTcol[:], -KT), [], ["ccol"])
    V_(lambda g: g.memset(nKWcol[:], -KW), [], ["ccol"])

    p1 = ExitStack()

    def sb1(name, shape, dt):
        return p1.enter_context(nc.sbuf_tensor("s_" + name, shape, dt))

    identf = sb1("identf", [128, 128], F32)
    maskmin = sb1("maskmin", [128, 128], F32)
    selpc = sb1("selpc", [16, 8 * 128], BF16)
    zeros16 = sb1("zeros16", [16, 128], F32)
    glb = sb1("glb", [128, 4 * H], F32)
    glrow = sb1("glrow", [1, 4 * H], F32)
    dma("sync", identf[:], identf_d, [], ["identf"])
    dma("sync", maskmin[:], maskmin_d, [], ["maskmin"])
    dma("sync", selpc[:], selpc_d, [], ["selpc"])
    V_(lambda g: g.memset(zeros16[:], 0.0), [], ["zeros16"])
    dma("sync", glrow[:], gl_d.rearrange("a b -> (a b)").rearrange("(o n) -> o n", o=1),
        [], ["glrow"])
    for j in range(4):
        T_(lambda g, j=j: g.matmul(PX[:, j * 16:(j + 1) * 16], ones1f[:],
                                   glrow[0:1, j * 16:(j + 1) * 16],
                                   start=True, stop=True),
           ["ones1f", "glrow"], ["PX0", "BPX"])
    V_(lambda g: g.tensor_copy(glb[:], PX[:, 0:64]), ["PX0", "BPX"], ["glb"])

    wq_sb = sb1("wq_sb", [128, 8 * D], BF16)
    wk_sb = sb1("wk_sb", [128, 8 * D], BF16)
    wv_sb = sb1("wv_sb", [128, 8 * D], BF16)
    wo_sb = sb1("wo_sb", [128, 8 * D], BF16)
    wstage = sb1("wstage", [128, 1024], F32)
    wqout = sb1("wqout", [128, 1024], BF16)
    absacc = sb1("absacc", [128, 64], F32)

    xstage = sb1("xstage", [128, D], F32)
    lnxtok = sb1("lnxtok", [128, D], BF16)
    lnxfm = sb1("lnxfm", [128, 8 * GT], BF16)
    Qsb = sb1("Qsb", [128, 8 * GT], BF16)
    Qpsb = sb1("Qpsb", [128, 8 * GT], BF16)
    PexpS = sb1("PexpS", [128, GT], F32)
    Vsb = sb1("Vsb", [128, 8 * GT], BF16)
    Ktok = sb1("Ktok", [128, GRP * D], BF16)
    Equad = sb1("Equad", [128, 12 * CH], F32)
    Abuf = sb1("Abuf", [128, H * CH], BF16)
    Vtbuf = sb1("Vtbuf", [128, H * DK], BF16)
    Lbcbuf = sb1("Lbcbuf", [128, 8 * CH], F32)
    Yfm = sb1("Yfm", [128, 8 * CH], BF16)
    aobuf = sb1("aobuf", [128, D], BF16)
    hnbuf = sb1("hnbuf", [128, D], BF16)
    Sf32 = sb1("Sf32", [128, 8 * DK], F32)
    Sbf2 = sb1("Sbf2", [128, H * DK], BF16)
    wcol = sb1("wcol", [128, 16], F32)
    wtm = sb1("wtm", [128, 16], F32)
    llrow = sb1("llrow", [1, 16], F32)
    V_(lambda g: g.memset(Sbf2[:], 0.0), [], ["Sbf2"])
    LTf = sb1("LTf", [16, GRP * CH], F32)
    nLT = sb1("nLT", [16, CH], F32)
    negLtok = sb1("negLtok", [128, GRP * H], F32)
    expLTf = sb1("expLTf", [16, CH], F32)
    expLTb = sb1("expLTb", [16, GT], BF16)
    pCrow = sb1("pCrow", [1, 16], F32)
    pCcols = sb1("pCcols", [128, GRP * 16], F32)
    lrtok = sb1("lrtok", [128, 16], F32)
    dsc = sb1("dsc", [128, 16 * 8], F32)
    outst_dbg = sb1("outst_dbg", [128, D], F32) if debug_stop else None

    def dsl(i):
        return dsc[:, i * 16:(i + 1) * 16]

    KB, GA, LA, RHO, ONEMR, KFAC, T1, T2 = range(8)

    for wi, (nm, wsb) in enumerate((("wqT", wq_sb), ("wkT", wk_sb),
                                    ("wvT", wv_sb), ("woT", wo_sb))):
        for kc in range(8):
            dma("sync", wstage[:, 0:D],
                wT_d[nm][kc * 128:(kc + 1) * 128, :], [], ["wstg0"])
            if kc % 2 == 0:
                V_(lambda g, wsb=wsb, kc=kc: g.tensor_copy(
                    wsb[:, kc * D:(kc + 1) * D], wstage[:, 0:D]),
                    ["wstg0"], [f"w{wi}_{kc}"])
            else:
                SC(lambda g, wsb=wsb, kc=kc: g.copy(
                    wsb[:, kc * D:(kc + 1) * D], wstage[:, 0:D]),
                    ["wstg0"], [f"w{wi}_{kc}"])

    def wq_params(which):
        wd, qd, nchunk, cw = ((w1T_d, qw1_s, 8, HID) if which == 0
                              else (w2T_d, qw2_s, 32, D))
        return wd, qd, nchunk, cw // 1024, 1024, (0 if which == 0 else 32)

    def wquant_A(which, items):
        wd, qd, nchunk, nh, hw, acol = wq_params(which)
        for (ch, hf) in items:
            ai = ch * nh + hf
            dma("gpsimd", wstage[:, 0:hw], wd[ch * 128:(ch + 1) * 128,
                                              hf * hw:(hf + 1) * hw],
                [], ["wstg0"])
            V_(lambda g, ai=ai, hw=hw, acol=acol: g.tensor_reduce(
                absacc[:, acol + ai:acol + ai + 1], wstage[:, 0:hw], AX.X,
                ALU.add, apply_absolute_value=True),
                ["wstg0"], [f"aacc{which}_{ai}"])

    def wquant_S(which):
        wd, qd, nchunk, nh, hw, acol = wq_params(which)
        nacc = nchunk * nh
        V_(lambda g, nacc=nacc, acol=acol, which=which: g.tensor_reduce(
            st(60 + which), absacc[:, acol:acol + nacc], AX.X, ALU.add),
            [f"aacc{which}_{a2}" for a2 in range(nacc)], [f"csum{which}"])
        T_(lambda g, which=which: g.matmul(PX[0:1, 432 + which:433 + which],
                                           st(60 + which), onescolf[:],
                                           start=True, stop=True),
           [f"csum{which}", "onescolf"], ["PXw", "BPX"])
        nels = float(D * HID)
        V_(lambda g, which=which, nels=nels: g.tensor_scalar(
            st(56 + which)[0:1, :], PX[0:1, 432 + which:433 + which],
            1.0 / nels, 1e-5, ALU.mult, ALU.max), ["PXw", "BPX"], [f"wmean{which}"])
        V_(lambda g, which=which: g.reciprocal(
            st(58 + which)[0:1, :], st(56 + which)[0:1, :]),
            [f"wmean{which}"], [f"wsc{which}"])
        T_(lambda g, which=which: g.matmul(
            PX[:, 436 + which:437 + which], ones1f[:], st(58 + which)[0:1, :],
            start=True, stop=True), [f"wsc{which}", "ones1f"], ["PXw", "BPX"])
        V_(lambda g, which=which: g.tensor_copy(
            swcol[:, which:which + 1], PX[:, 436 + which:437 + which]),
            ["PXw", "BPX"], [f"swcol{which}"])

    def wquant_B(which, items):
        wd, qd, nchunk, nh, hw, acol = wq_params(which)
        for (ch, hf) in items:
            wsk = ["wstg0"]
            dma("gpsimd", wstage[:, 0:hw], wd[ch * 128:(ch + 1) * 128,
                                              hf * hw:(hf + 1) * hw],
                [], wsk)
            V_(lambda g, hw=hw, which=which: g.tensor_scalar(
                g2048f[:, 0:hw], wstage[:, 0:hw], swcol[:, which:which + 1],
                None, ALU.mult),
                wsk + [f"swcol{which}", "g2048a"], ["g2048a"])
            V_(lambda g, hw=hw: g.tensor_copy(qi32buf[:, 0:hw], g2048f[:, 0:hw]),
               ["g2048a"], ["qi32"])
            V_(lambda g, hw=hw: g.scalar_tensor_tensor(
                wstage[:, 0:hw], qi32buf[:, 0:hw], 0.5, g2048f[:, 0:hw],
                ALU.add, ALU.is_le), ["qi32", "g2048a"] + wsk, wsk)
            V_(lambda g, hw=hw: g.scalar_tensor_tensor(
                wqout[:, 0:hw], qi32buf[:, 0:hw], 0.0, wstage[:, 0:hw],
                ALU.add, ALU.add), ["qi32"] + wsk, ["wqo"])
            V_(lambda g, hw=hw: g.scalar_tensor_tensor(
                wstage[:, 0:hw], qi32buf[:, 0:hw], -0.5, g2048f[:, 0:hw],
                ALU.add, ALU.is_ge), ["qi32", "g2048a"] + wsk, wsk)
            V_(lambda g, hw=hw: g.tensor_tensor(
                wqout[:, 0:hw], wqout[:, 0:hw], wstage[:, 0:hw], ALU.subtract),
                ["wqo"] + wsk, ["wqo"])
            V_(lambda g, hw=hw: g.tensor_scalar(wqout[:, 0:hw], wqout[:, 0:hw],
                                                1.0, -1.0, ALU.min, ALU.max),
               ["wqo"], ["wqo"])
            dma("gpsimd", qd[ch * 128:(ch + 1) * 128, hf * hw:(hf + 1) * hw],
                wqout[:, 0:hw], ["wqo"], ["wqstore"])

    wq_items = {0: [(c, f) for c in range(8) for f in range(4)],
                1: [(c, 0) for c in range(32)]}

    if debug_stop == 2:
        for tt in range(NTT):
            dma("gpsimd", yd[tt * CH:tt * CH + 128, :], wstage[:, 0:D],
                ["wstg0"], [f"ydone{tt}"])
        S.emit("sync", None, reads=[f"ydone{tt}" for tt in range(NTT)], writes=[])
        S.emit("gpsimd", None, reads=[f"ydone{tt}" for tt in range(NTT)], writes=[])
        return nc
    n_grp_b = n_chunks // GRP
    for b in range(n_seq):
        V_(lambda g: g.memset(Sf32[:], 0.0), [], ["Sf32"])
        for gg in range(n_grp_b):
            gi = b * n_grp_b + gg
            ngi = n_seq * n_grp_b
            def slice_items(items, i, n):
                k = len(items)
                return items[(i * k) // n:((i + 1) * k) // n]
            if ngi == 1:
                wquant_A(0, wq_items[0]); wquant_A(1, wq_items[1])
                wquant_S(0); wquant_S(1)
                wquant_B(0, wq_items[0]); wquant_B(1, wq_items[1])
            else:
                half = ngi // 2
                if gi < half:
                    wquant_A(0, slice_items(wq_items[0], gi, half))
                    wquant_A(1, slice_items(wq_items[1], gi, half))
                if gi == half:
                    wquant_S(0)
                    wquant_S(1)
                if gi >= half:
                    i2 = gi - half
                    n2 = ngi - half
                    wquant_B(0, slice_items(wq_items[0], i2, n2))
                    wquant_B(1, slice_items(wq_items[1], i2, n2))
            c0 = gg * GRP
            for cc in range(GRP):
                c = c0 + cc
                tt = b * n_chunks + c
                xs = xstage[:, 0:D]
                xrow0 = b * T + c * CH
                dma("sync", xs, xd[xrow0:xrow0 + 128, :], [], ["xstg0"])
                x2sl = X2[:, tt * D:(tt + 1) * D]
                V_(lambda g, xs=xs, x2sl=x2sl: g.tensor_scalar(
                    x2sl, xs, 1.0, 0.0, ALU.mult, ALU.add, accum_out=st(0)),
                    ["xstg0"], [f"x2:{tt}", "ln1a"])
                SC(lambda g, xs=xs: g.activation(
                    g2048f[:, 0:D], xs, AF.Square, accum_out=st(1)),
                    ["xstg0"], ["g2048a", "ln1b"])
                ln_stats(0, 1, 2, D, "ln1")
                lnx = lnxtok[:, 0:D]
                SC(lambda g, x2sl=x2sl, lnx=lnx: g.activation(
                    lnx, x2sl, AF.Identity, bias=st(3), scale=st(2)),
                    [f"x2:{tt}", "ln1c", "ln1d"], ["lnx0"])
                for half in range(2):
                    for kq in range(4):
                        kc = half * 4 + kq
                        T_(lambda g, lnx=lnx, kc=kc, kq=kq: g.matmul(
                            PE_[:, kq * 128:(kq + 1) * 128],
                            lnx[:, kc * 128:(kc + 1) * 128], identb[:],
                            start=True, stop=True),
                            ["lnx0", "identb"], [f"PEq{kq}", "BPE"])
                    V_(lambda g, half=half, cc=cc: g.tensor_copy(
                        lnxfm[:].rearrange("p (k t) -> p k t", k=8)
                        [:, half * 4:half * 4 + 4, cc * 128:(cc + 1) * 128],
                        PE_[:].rearrange("p (k t) -> p k t", k=4)),
                        [f"PEq{kq}" for kq in range(4)] + ["BPE"],
                        [f"lnxfm{k2}_{cc}" for k2 in range(half * 4, half * 4 + 4)])
                for eh in range(2):
                    for kc in range(8):
                        T_(lambda g, eh=eh, kc=kc, cc=cc: g.matmul(
                            PA[:, eh * 512:(eh + 1) * 512],
                            lnxfm[:, kc * GT + cc * 128:kc * GT + (cc + 1) * 128],
                            wk_sb[:, kc * D + eh * 512:kc * D + (eh + 1) * 512],
                            start=(kc == 0), stop=(kc == 7)),
                            [f"lnxfm{kc}_{cc}", f"w1_{kc}"],
                            ["PA", f"BPA{eh}"], acc=(kc != 0 or eh != 0))
                SC(lambda g: g.activation(g2048f[:, 0:D], PA[:], AF.Square),
                   ["PA", "BPA0", "BPA1"], ["g2048a"])
                V_(lambda g: g.tensor_reduce(
                    dsl(T1), g2048f[:, 0:D].rearrange("p (h j) -> p h j", h=16),
                    AX.X, ALU.add), ["g2048a"], ["dT1"])
                V_(lambda g: g.tensor_reduce(
                    dsl(T2), PA[:].rearrange("p (h j) -> p h j", h=16),
                    AX.X, ALU.add), ["PA", "BPA0", "BPA1"], ["dT2"])
                SC(lambda g: g.activation(dsl(KFAC), dsl(T1), AF.Sqrt,
                                          bias=e24col[:]),
                   ["dT1", "ccol"], ["dKF"])
                V_(lambda g: g.reciprocal(dsl(KFAC), dsl(KFAC)), ["dKF"], ["dKF"])
                V_(lambda g: g.scalar_tensor_tensor(
                    dsl(KB), dsl(T2), 1.0 / 64.0, dsl(KFAC), ALU.mult, ALU.mult),
                    ["dT2", "dKF"], ["dKB"])
                V_(lambda g: g.tensor_tensor(dsl(T1), dsl(KB), glb[:, 0:16], ALU.mult),
                   ["dKB", "glb"], ["dT1"])
                V_(lambda g: g.tensor_tensor(dsl(T1), dsl(T1), glb[:, 16:32], ALU.add),
                   ["dT1", "glb"], ["dT1"])
                SC(lambda g: g.activation(dsl(GA), dsl(T1), AF.Sigmoid), ["dT1"], ["dGA"])
                V_(lambda g: g.tensor_tensor(dsl(T2), dsl(KB), glb[:, 32:48], ALU.mult),
                   ["dKB", "glb"], ["dT2"])
                V_(lambda g: g.tensor_tensor(dsl(T2), dsl(T2), glb[:, 48:64], ALU.add),
                   ["dT2", "glb"], ["dT2"])
                SC(lambda g: g.activation(dsl(LA), dsl(T2), AF.Sigmoid), ["dT2"], ["dLA"])
                V_(lambda g: g.tensor_scalar(dsl(T2), dsl(LA), -1.0, 1.0,
                                             ALU.mult, ALU.add), ["dLA"], ["dT2"])
                V_(lambda g: g.tensor_tensor(dsl(RHO), dsl(T2), dsl(GA), ALU.mult),
                   ["dT2", "dGA"], ["dRHO"])
                V_(lambda g: g.tensor_scalar(dsl(ONEMR), dsl(RHO), -1.0, 1.0,
                                             ALU.mult, ALU.add), ["dRHO"], ["dOM"])
                SC(lambda g: g.activation(lrtok[:], dsl(RHO), AF.Ln,
                                          bias=e30col[:]),
                   ["dRHO", "ccol"], ["lrtok"])
                V_(lambda g: g.scalar_tensor_tensor(dsl(KFAC), dsl(KFAC), 1.0,
                                                    dsl(ONEMR), ALU.mult, ALU.mult),
                   ["dKF", "dOM"], ["dKF"])
                V_(lambda g, cc=cc: g.tensor_tensor(
                    Ktok[:, cc * D:(cc + 1) * D].rearrange("p (h j) -> p h j", h=16),
                    PA[:].rearrange("p (h j) -> p h j", h=16),
                    dsl(KFAC).rearrange("p (h o) -> p h o", o=1)
                    .broadcast_to([128, 16, 64]),
                    ALU.mult), ["PA", "BPA0", "BPA1", "dKF"], [f"ktok{cc}"])
                T_(lambda g: g.matmul(PX[0:16, 0:128], lrtok[:], identf[:],
                                      start=True, stop=True),
                   ["lrtok", "identf"], ["PX0", "BPX"])
                V_(lambda g, cc=cc: g.tensor_tensor_scan(
                    LTf[:, cc * CH:(cc + 1) * CH], PX[0:16, 0:128], zeros16[:],
                    0.0, ALU.add, ALU.add),
                    ["PX0", "BPX", "zeros16"], [f"LTf{cc}"])
                V_(lambda g, cc=cc: g.tensor_scalar(
                    nLT[:], LTf[:, cc * CH:(cc + 1) * CH], -1.0, None, ALU.mult),
                    [f"LTf{cc}"], ["nLT"])
                T_(lambda g: g.matmul(PX[0:128, 128:144], nLT[:], identf[0:16, 0:16],
                                      start=True, stop=True),
                   ["nLT", "identf"], ["PX0", "BPX"])
                V_(lambda g, cc=cc: g.tensor_copy(
                    negLtok[:, cc * H:(cc + 1) * H], PX[0:128, 128:144]),
                    ["PX0", "BPX"], [f"negL{cc}"])
                SC(lambda g, cc=cc: g.activation(
                    expLTf[:], LTf[:, cc * CH:(cc + 1) * CH], AF.Exp),
                    [f"LTf{cc}"], ["expLTf"])
                V_(lambda g, cc=cc: g.tensor_copy(
                    expLTb[:, cc * CH:(cc + 1) * CH], expLTf[:]),
                    ["expLTf"], [f"expLTb{cc}"])
                T_(lambda g: g.matmul(PX[0:1, 144:160], expLTf[:, 127:128],
                                      identf[0:16, 0:16], start=True, stop=True),
                   ["expLTf", "identf"], ["PX0", "BPX"])
                V_(lambda g: g.tensor_copy(pCrow[:], PX[0:1, 144:160]),
                   ["PX0", "BPX"], ["pCrow"])
                T_(lambda g: g.matmul(PX[0:128, 160:176], ones1f[:], pCrow[:],
                                      start=True, stop=True),
                   ["pCrow", "ones1f"], ["PX0", "BPX"])
                V_(lambda g, cc=cc: g.tensor_copy(
                    pCcols[:, cc * 16:(cc + 1) * 16], PX[0:128, 160:176]),
                    ["PX0", "BPX"], [f"pCcols{cc}"])
            for pc in range(8 if debug_stop not in (3,) else 0):
                T_(lambda g, pc=pc: g.matmul(
                    PX[:, 176:176 + GT], selpc[:, pc * 128:(pc + 1) * 128],
                    expLTb[:], start=True, stop=True),
                    [f"expLTb{c2}" for c2 in range(GRP)] + ["selpc"], ["PXb", "BPX"])
                for kc in range(8):
                    T_(lambda g, kc=kc, pc=pc: g.matmul(
                        PD[0:128, 0:GT],
                        wq_sb[:, kc * D + pc * 128:kc * D + (pc + 1) * 128],
                        lnxfm[:, kc * GT:(kc + 1) * GT],
                        start=(kc == 0), stop=(kc == 7)),
                        [f"lnxfm{kc}_{c2}" for c2 in range(GRP)] + [f"w0_{kc}"],
                        ["PDq", "BPD"] + (PD_ALL if kc == 0 else []), acc=(kc != 0))
                V_(lambda g: g.tensor_copy(PexpS[:], PX[:, 176:176 + GT]),
                   ["PXb", "BPX"], ["PexpS"])
                V_(lambda g, pc=pc: g.tensor_tensor(
                    Qpsb[:, pc * GT:(pc + 1) * GT], PD[0:128, 0:GT], PexpS[:],
                    ALU.mult), ["PDq", "BPD", "PexpS"], [f"qp{pc}"])
                V_(lambda g, pc=pc: g.tensor_copy(
                    Qsb[:, pc * GT:(pc + 1) * GT], PD[0:128, 0:GT]),
                    ["PDq", "BPD"], [f"q{pc}"])
                for kc in range(8):
                    T_(lambda g, kc=kc, pc=pc: g.matmul(
                        PD[0:128, 0:GT],
                        wv_sb[:, kc * D + pc * 128:kc * D + (pc + 1) * 128],
                        lnxfm[:, kc * GT:(kc + 1) * GT],
                        start=(kc == 0), stop=(kc == 7)),
                        [f"lnxfm{kc}_{c2}" for c2 in range(GRP)] + [f"w2_{kc}"],
                        ["PDv", "BPD"] + (PD_ALL if kc == 0 else []), acc=(kc != 0))
                SC(lambda g, pc=pc: g.copy(
                    Vsb[:, pc * GT:(pc + 1) * GT], PD[0:128, 0:GT]),
                    ["PDv", "BPD"], [f"v{pc}"])
            QUADS = [[0, 2, 4, 6], [8, 10, 12, 14], [1, 3, 5, 7], [9, 11, 13, 15]]
            acol = {}
            for q in range(4):
                for j, h in enumerate(QUADS[q]):
                    acol[h] = q * 512 + j * 128
            for cc in range(GRP if debug_stop not in (3, 4) else 0):
                c = c0 + cc
                tt = b * n_chunks + c
                T_(lambda g, cc=cc: g.matmul(
                    PX[0:1, 144:160], LTf[:, cc * CH + 127:cc * CH + 128],
                    identf[0:16, 0:16], start=True, stop=True),
                    [f"LTf{cc}", "identf"], ["PX0", "BPX"])
                V_(lambda g: g.tensor_copy(llrow[:], PX[0:1, 144:160]),
                   ["PX0", "BPX"], ["llrow"])
                T_(lambda g: g.matmul(PX[0:128, 160:176], ones1f[:], llrow[:],
                                      start=True, stop=True),
                   ["llrow", "ones1f"], ["PX0", "BPX"])
                V_(lambda g, cc=cc: g.tensor_tensor(
                    wtm[:], PX[0:128, 160:176],
                    negLtok[:, cc * H:(cc + 1) * H], ALU.add),
                    ["PX0", "BPX", f"negL{cc}"], ["wtm"])
                SC(lambda g: g.activation(wcol[:], wtm[:], AF.Exp),
                   ["wtm"], ["wcol"])
                def emit_E(q, cc=cc):
                    heads = QUADS[q]
                    q2 = (cc * 4 + q) % 3
                    eo = q2 * 512
                    l2 = (cc * 4 + q) % 2
                    lo = l2 * 512
                    for j, h in enumerate(heads):
                        dma("gpsimd", Lbcbuf[:, lo + j * 128:lo + (j + 1) * 128],
                            LTf[h:h + 1, cc * CH:(cc + 1) * CH]
                            .rearrange("p (o f) -> p o f", o=1)
                            .broadcast_to([1, 128, 128]),
                            [f"LTf{cc}"], [f"lbc{l2}_{j}"])
                    V_(lambda g, lo=lo: g.scalar_tensor_tensor(
                        Lbcbuf[:, lo:lo + 512].rearrange("p (o f) -> p o f", o=4),
                        Lbcbuf[:, lo:lo + 512].rearrange("p (o f) -> p o f", o=4),
                        1.0,
                        maskmin[:].rearrange("p (o f) -> p o f", o=1)
                        .broadcast_to([128, 4, 128]),
                        ALU.mult, ALU.min),
                        [f"lbc{l2}_{j2}" for j2 in range(4)] + ["maskmin"],
                        [f"lbc{l2}_{j2}" for j2 in range(4)])
                    for j, h in enumerate(heads):
                        SC(lambda g, h=h, j=j, eo=eo, lo=lo: g.activation(
                            Equad[:, eo + j * 128:eo + (j + 1) * 128],
                            Lbcbuf[:, lo + j * 128:lo + (j + 1) * 128], AF.Exp,
                            bias=negLtok[:, cc * H + h:cc * H + h + 1]),
                            [f"lbc{l2}_{j}", f"negL{cc}"], [f"Eq{q2}_{j}"])
                for q in range(3):
                    emit_E(q)
                if debug_stop == 8:
                    continue
                for q in range(4):
                    heads = QUADS[q]
                    qp = 0 if q < 2 else 1
                    APS = PE_ if qp == 0 else PC
                    akey = "PEq" if qp == 0 else "PCo"
                    bkey = "BPE" if qp == 0 else "BPC"
                    q2 = (cc * 4 + q) % 3
                    eo = q2 * 512
                    for j, h in enumerate(heads):
                        hp, pc = h % 2, h // 2
                        T_(lambda g, hp=hp, pc=pc, j=j, cc=cc, APS=APS: g.matmul(
                            APS[:, j * 128:(j + 1) * 128],
                            Vsb[hp * 64:(hp + 1) * 64,
                                pc * GT + cc * 128:pc * GT + (cc + 1) * 128],
                            Qsb[hp * 64:(hp + 1) * 64,
                                pc * GT + cc * 128:pc * GT + (cc + 1) * 128],
                            start=True, stop=True),
                            [f"v{pc}", f"q{pc}"], [f"{akey}{j}", bkey])
                    V_(lambda g, q=q, APS=APS, eo=eo: g.tensor_tensor(
                        Abuf[:, q * 512:(q + 1) * 512], APS[:],
                        Equad[:, eo:eo + 512], ALU.mult),
                        [f"{akey}{j2}" for j2 in range(4)] + [bkey]
                        + [f"Eq{q2}_{j2}" for j2 in range(4)],
                        [f"Ab{q}"])
                    if q == 0:
                        emit_E(3)
                if debug_stop in (8, 9):
                    continue
                for pc in range(8 if debug_stop != 12 else 0):
                    sl = pc % 2
                    T_(lambda g, pc=pc, cc=cc, sl=sl: g.matmul(
                        PX[:, 176 + sl * 128:176 + (sl + 1) * 128],
                        Vsb[:, pc * GT + cc * 128:pc * GT + (cc + 1) * 128],
                        identb[:], start=True, stop=True),
                        [f"v{pc}", "identb"], ["PXb" if sl == 0 else "PXb2", "BPX"])
                    V_(lambda g, pc=pc, sl=sl: g.tensor_tensor(
                        Vtbuf[:, pc * 128:(pc + 1) * 128]
                        .rearrange("p (h j) -> p h j", h=2),
                        PX[:, 176 + sl * 128:176 + (sl + 1) * 128]
                        .rearrange("p (h j) -> p h j", h=2),
                        wcol[:, 2 * pc:2 * pc + 2]
                        .rearrange("p (h o) -> p h o", o=1)
                        .broadcast_to([128, 2, 64]),
                        ALU.mult),
                        ["PXb" if sl == 0 else "PXb2", "BPX", "wcol"], [f"Vt{pc}"])
                for h in range(H if debug_stop not in (5, 12, 13) else 0):
                    hp, pc = h % 2, h // 2
                    first = (c == 0)
                    if not first:
                        T_(lambda g, hp=hp, pc=pc, cc=cc, h=h: g.matmul(
                            PB[hp * 64:(hp + 1) * 64, pc * 128:(pc + 1) * 128],
                            Sbf2[:, h * 64:(h + 1) * 64],
                            Qpsb[:, pc * GT + cc * 128:pc * GT + (cc + 1) * 128],
                            start=True, stop=False),
                            ["Sbf2", f"qp{pc}"], [f"PBy{pc}_{hp}", f"BPB{pc // 4}"])
                    T_(lambda g, hp=hp, pc=pc, cc=cc, h=h, first=first: g.matmul(
                        PB[hp * 64:(hp + 1) * 64, pc * 128:(pc + 1) * 128],
                        Ktok[:, cc * D + h * 64:cc * D + (h + 1) * 64],
                        Abuf[:, acol[h]:acol[h] + 128],
                        start=first, stop=True),
                        [f"ktok{cc}", f"Ab{acol[h] // 512}"],
                        [f"PBy{pc}_{hp}", f"BPB{pc // 4}"])
                for h in range(H if debug_stop not in (5, 12, 13) else 0):
                    hp, pc = h % 2, h // 2
                    T_(lambda g, hp=hp, pc=pc, cc=cc, h=h: g.matmul(
                        PD[hp * 64:(hp + 1) * 64, pc * 64:(pc + 1) * 64],
                        Vtbuf[:, h * 64:(h + 1) * 64],
                        Ktok[:, cc * D + h * 64:cc * D + (h + 1) * 64],
                        start=True, stop=True),
                        [f"Vt{pc}", f"ktok{cc}"],
                        [f"PDs{h}", "BPD"] + (["PDq", "PDv"] if h == 0 else []))
                for h in range(H if debug_stop not in (5, 12, 13) else 0):
                    hp, pc = h % 2, h // 2
                    V_(lambda g, hp=hp, pc=pc, cc=cc, h=h: g.scalar_tensor_tensor(
                        Sf32[hp * 64:(hp + 1) * 64, pc * 64:(pc + 1) * 64],
                        Sf32[hp * 64:(hp + 1) * 64, pc * 64:(pc + 1) * 64],
                        pCcols[hp * 64:(hp + 1) * 64, cc * 16 + h:cc * 16 + h + 1],
                        PD[hp * 64:(hp + 1) * 64, pc * 64:(pc + 1) * 64],
                        ALU.mult, ALU.add),
                        [f"pCcols{cc}", f"PDs{h}", "BPD", "Sf32"], ["Sf32"])
                yp = 0
                for pc in range(8 if debug_stop not in (5, 12, 13) else 0):
                    SC(lambda g, pc=pc, yp=yp: g.copy(
                        Yfm[:, yp * 1024 + pc * 128:yp * 1024 + (pc + 1) * 128],
                        PB[:, pc * 128:(pc + 1) * 128]),
                        [f"PBy{pc}_0", f"PBy{pc}_1", f"BPB{pc // 4}"],
                        [f"yfm{yp}_{pc}"])
                if debug_stop in (12, 13):
                    continue
                SC(lambda g: g.copy(
                    Sbf2[0:64, :].rearrange("p (b two j) -> p b two j", two=2, j=64)
                    [:, :, 0, :],
                    Sf32[0:64, :].rearrange("p (b j) -> p b j", j=64)),
                   ["Sf32"], ["Sbf2"])
                SC(lambda g: g.copy(
                    Sbf2[64:128, :].rearrange("p (b two j) -> p b two j", two=2, j=64)
                    [:, :, 1, :],
                    Sf32[64:128, :].rearrange("p (b j) -> p b j", j=64)),
                   ["Sf32"], ["Sbf2"])
                for eh in range(2 if debug_stop not in (5, 6, 8, 9, 12, 13) else 0):
                    for kc in range(8):
                        T_(lambda g, eh=eh, kc=kc, yp=yp: g.matmul(
                            PA[:, eh * 512:(eh + 1) * 512],
                            Yfm[:, yp * 1024 + kc * 128:yp * 1024 + (kc + 1) * 128],
                            wo_sb[:, kc * D + eh * 512:kc * D + (eh + 1) * 512],
                            start=(kc == 0), stop=(kc == 7)),
                            [f"yfm{yp}_{kc}", f"w3_{kc}"],
                            ["PA", f"BPA{eh}"])
                if debug_stop in (5, 6, 8, 9, 12, 13):
                    continue
                SC(lambda g: g.activation(aobuf[:], PA[:], AF.Copy, accum_out=st(4)),
                   ["PA", "BPA0", "BPA1"], ["ao", "lnaa"])
                SC(lambda g: g.activation(g2048f[:, 0:D], aobuf[:], AF.Square,
                                          accum_out=st(5)),
                   ["ao"], ["g2048a", "lnab"])
                ln_stats(4, 5, 6, D, "lna")
                SC(lambda g: g.activation(hnbuf[:], aobuf[:], AF.Identity, bias=st(7),
                                          scale=st(6)),
                   ["ao", "lnac", "lnad"], ["hn"])
                x2sl = X2[:, tt * D:(tt + 1) * D]
                V_(lambda g, x2sl=x2sl: g.tensor_tensor(x2sl, x2sl, hnbuf[:], ALU.add),
                   [f"x2:{tt}", "hn"], [f"x2:{tt}"])

    S.barrier()
    if debug_stop in (1, 3, 4, 5, 6, 8, 9, 12, 13):
        for tt in range(NTT):
            V_(lambda g, tt=tt: g.tensor_copy(outst_dbg[:], X2[:, tt * D:(tt + 1) * D]),
               [f"x2:{tt}", "odbg"], ["odbg"])
            dma("gpsimd", yd[tt * CH:tt * CH + 128, :], outst_dbg[:],
                ["odbg"], [f"ydone{tt}", "odbg"])
        S.emit("sync", None, reads=[f"ydone{tt}" for tt in range(NTT)], writes=[])
        S.emit("gpsimd", None, reads=[f"ydone{tt}" for tt in range(NTT)], writes=[])
        return nc
    p1.close()

    p2 = ExitStack()

    def sb2(name, shape, dt):
        return p2.enter_context(nc.sbuf_tensor("s_" + name, shape, dt))

    qw2sb = sb2("qw2sb", [128, 32 * D], BF16)
    wexp1 = sb2("wexp1", [128, D], F16)
    bexp1 = sb2("bexp1", [128, D], F16)
    wexp2 = sb2("wexp2", [128, HID], F16)
    bexp2 = sb2("bexp2", [128, HID], F16)
    H2 = sb2("H2", [128, 2 * HID], F16)
    qhtok = sb2("qhtok", [128, HID], BF16)
    qhfm = sb2("qhfm", [128, 2 * 512], BF16)
    qxtok = sb2("qxtok", [128, D], BF16)
    qxfm = sb2("qxfm", [128, 8 * GT], BF16)
    h1b = sb2("h1b", [128, D], F16)
    qw1buf = sb2("qw1buf", [128, 2 * 512], BF16)
    outstage = sb2("outstage", [128, D], F32)
    fobuf = sb2("fobuf", [128, D], F16)
    hn3buf = sb2("hn3buf", [128, D], BF16)
    screw = sb2("screw", [1, 2 * 128], BF16)
    screw2 = sb2("screw2", [1, 2 * 128], BF16)
    c1col = sb2("c1col", [128, 2], F32)
    c2col = sb2("c2col", [128, 2], F32)
    s2cols = sb2("s2cols", [128, 2], F32)
    gacc = sb2("gacc", [128, 2 * 8], F32)

    pxkeys = ["PX0", "PXw", "PXb", "PXb2", "BPX"]
    for (dst, row, n, key) in ((wexp1, 0, D, "wexp1"), (bexp1, 1, D, "bexp1"),
                               (wexp2, 2, HID, "wexp2"), (bexp2, 3, HID, "bexp2")):
        for j0 in range(0, n, 512):
            dma("sync", rstgf[:], ln_d[row:row + 1, j0:j0 + 512], [], ["rstgf"])
            V_(lambda g: g.tensor_copy(rstgb[:], rstgf[:]), ["rstgf"], ["rstgb"])
            T_(lambda g: g.matmul(PX[:, 0:512], onesbf[:], rstgb[:],
                                  start=True, stop=True),
               ["onesbf", "rstgb"], pxkeys)
            V_(lambda g, dst=dst, j0=j0: g.tensor_copy(dst[:, j0:j0 + 512],
                                                       PX[:, 0:512]),
               pxkeys, [key])

    for ch in range(32):
        dma("sync", qw2sb[:, ch * D:(ch + 1) * D], qw2_s[ch * 128:(ch + 1) * 128, :],
            ["wqstore"], [f"qw2r{ch}"])

    if debug_stop == 25:
        for tt in range(NTT):
            dma("sync", qhtok[:, 0:D], qw1_s[tt * 128:(tt + 1) * 128, 0:D],
                ["wqstore", "qhd"], ["qhq"])
            V_(lambda g: g.tensor_copy(outstage[:], qhtok[:, 0:D]),
               ["qhq", "ost"], ["ost", "qhd"])
            dma("gpsimd", yd[tt * CH:tt * CH + 128, :], outstage[:],
                ["ost"], [f"ydone{tt}", "ost"])
        S.emit("sync", None, reads=[f"ydone{tt}" for tt in range(NTT)], writes=[])
        S.emit("gpsimd", None, reads=[f"ydone{tt}" for tt in range(NTT)], writes=[])
        return nc
    NT2 = NTT // 2
    def emitA(g2):
            for tt2 in range(2):
                tt = g2 * 2 + tt2
                x2sl = X2[:, tt * D:(tt + 1) * D]
                V_(lambda g, x2sl=x2sl: g.tensor_scalar(
                    outstage[:], x2sl, 1.0, 0.0, ALU.mult, ALU.add,
                    accum_out=st(0)),
                    [f"x2:{tt}"], ["ost", "f1a"])
                SC(lambda g, x2sl=x2sl: g.activation(
                    g2048f[:, 0:D], x2sl, AF.Square, accum_out=st(1)),
                    [f"x2:{tt}"], ["g2048a", "f1b"])
                ln_stats(0, 1, 2, D, "f1")
                SC(lambda g, x2sl=x2sl: g.activation(h1b[:], x2sl, AF.Identity,
                                                     bias=st(3), scale=st(2)),
                   [f"x2:{tt}", "f1c", "f1d"], ["h1"])
                V_(lambda g: g.tensor_tensor(h1b[:], h1b[:], wexp1[:], ALU.mult),
                   ["h1", "wexp1"], ["h1"])
                V_(lambda g: g.tensor_tensor(h1b[:], h1b[:], bexp1[:], ALU.add),
                   ["h1", "bexp1"], ["h1"])
                V_(lambda g: g.tensor_reduce(st(8), h1b[:], AX.X, ALU.max,
                                             apply_absolute_value=True),
                   ["h1"], ["mx1"])
                V_(lambda g: g.tensor_scalar(st(8), st(8), 1e-5, None, ALU.max),
                   ["mx1"], ["mx1"])
                V_(lambda g: g.tensor_scalar(st(9), st(8), 1.0 / 127.0, None, ALU.mult),
                   ["mx1"], ["s1col"])
                V_(lambda g: g.reciprocal(st(9), st(9)), ["s1col"], ["s1col"])
                V_(lambda g: g.tensor_tensor(st(10), st(9), swcol[:, 0:1], ALU.mult),
                   ["s1col", "swcol0"], ["sc1f"])
                V_(lambda g, tt2=tt2: g.reciprocal(c1col[:, tt2:tt2 + 1], st(10)),
                   ["sc1f"], [f"c1col{tt2}"])
                V_(lambda g: g.tensor_copy(statb1[:], st(10)), ["sc1f"], ["statb1"])
                T_(lambda g: g.matmul(PX[0:1, 0:128], statb1[:], identb[:],
                                      start=True, stop=True),
                   ["statb1", "identb"], ["PX0", "BPX"])
                V_(lambda g, tt2=tt2: g.tensor_copy(
                    screw[:, tt2 * 128:(tt2 + 1) * 128], PX[0:1, 0:128]),
                    ["PX0", "BPX"], [f"sc1row{tt2}"])
                V_(lambda g: g.tensor_scalar(g2048f[:, 0:D], h1b[:], st(9), KT,
                                             ALU.mult, ALU.add),
                   ["h1", "s1col"], ["g2048a"])
                V_(lambda g: g.tensor_copy(qi32buf[:, 0:D], g2048f[:, 0:D]),
                   ["g2048a"], ["qi32"])
                V_(lambda g: g.scalar_tensor_tensor(
                    outstage[:], qi32buf[:, 0:D], 0.5, g2048f[:, 0:D],
                    ALU.add, ALU.is_le), ["qi32", "g2048a", "ost"], ["ost"])
                V_(lambda g: g.scalar_tensor_tensor(
                    qxtok[:], qi32buf[:, 0:D], -KT, outstage[:],
                    ALU.add, ALU.add), ["qi32", "ost"], ["qx"])
                if debug_stop in (22, 23):
                    tt_d = g2 * 2 + tt2
                    srcb = qxtok if debug_stop == 22 else h1b
                    V_(lambda g, srcb=srcb: g.tensor_copy(outstage[:], srcb[:]),
                       ["qx", "h1", "ost"], ["ost"])
                    dma("gpsimd", yd[tt_d * CH:tt_d * CH + 128, :], outstage[:],
                        ["ost"], [f"ydone{tt_d}", "ost"])
                    continue
                for half in range(2):
                    for kq in range(4):
                        kc = half * 4 + kq
                        T_(lambda g, kc=kc, kq=kq: g.matmul(
                            PE_[:, kq * 128:(kq + 1) * 128],
                            qxtok[:, kc * 128:(kc + 1) * 128], identb[:],
                            start=True, stop=True),
                            ["qx", "identb"], [f"PEq{kq}", "BPE"])
                    V_(lambda g, half=half, tt2=tt2: g.tensor_copy(
                        qxfm[:].rearrange("p (k t) -> p k t", k=8)
                        [:, half * 4:half * 4 + 4, tt2 * 128:(tt2 + 1) * 128],
                        PE_[:].rearrange("p (k t) -> p k t", k=4)),
                        [f"PEq{kq}" for kq in range(4)] + ["BPE"],
                        [f"qxfm{k2}_{tt2}" for k2 in range(half * 4, half * 4 + 4)])

    emitA(0)
    for g2 in range(NT2):
        if debug_stop in (22, 23):
            continue
        for hs in range(8):
            for kc in range(8):
                sl = (hs * 8 + kc) % 2
                dma("sync", qw1buf[:, sl * 512:(sl + 1) * 512],
                    qw1_s[kc * 128:(kc + 1) * 128, hs * 512:(hs + 1) * 512],
                    ["wqstore"], [f"qw1b{sl}"])
                for tt2 in range(2):
                    T_(lambda g, kc=kc, sl=sl, tt2=tt2: g.matmul(
                        PA[:, tt2 * 512:(tt2 + 1) * 512],
                        qxfm[:, kc * GT + tt2 * 128:kc * GT + (tt2 + 1) * 128],
                        qw1buf[:, sl * 512:(sl + 1) * 512],
                        start=(kc == 0), stop=False),
                        [f"qxfm{kc}_{tt2}", f"qw1b{sl}"],
                        [f"PAf{tt2}", f"BPA{tt2}"], acc=(kc != 0))
            dma("sync", rstgf[:], b1_d[0:1, hs * 512:(hs + 1) * 512],
                [], ["rstgf"])
            V_(lambda g: g.tensor_copy(rstgb[:], rstgf[:]), ["rstgf"], ["rstgb"])
            for tt2 in range(2):
                T_(lambda g, tt2=tt2: g.matmul(
                    PA[:, tt2 * 512:(tt2 + 1) * 512],
                    screw[0:1, tt2 * 128:(tt2 + 1) * 128],
                    rstgb[:], start=False, stop=True),
                    [f"sc1row{tt2}", "rstgb"], [f"PAf{tt2}", f"BPA{tt2}"], acc=True)
                SC(lambda g, hs=hs, tt2=tt2: g.activation(
                    H2[:, tt2 * HID + hs * 512:tt2 * HID + (hs + 1) * 512],
                    PA[:, tt2 * 512:(tt2 + 1) * 512], AF.Gelu,
                    scale=c1col[:, tt2:tt2 + 1],
                    accum_out=gacc[:, tt2 * 8 + hs:tt2 * 8 + hs + 1]),
                    [f"PAf{tt2}", f"BPA{tt2}", f"c1col{tt2}"],
                    [f"H2_{tt2}", f"gacc{tt2}_{hs}"])
        if debug_stop == 21:
            for tt2 in range(2):
                tt = g2 * 2 + tt2
                V_(lambda g, tt2=tt2: g.tensor_copy(
                    outstage[:], H2[:, tt2 * HID:tt2 * HID + D]),
                    [f"H2_{tt2}", "ost"], ["ost"])
                dma("gpsimd", yd[tt * CH:tt * CH + 128, :], outstage[:],
                    ["ost"], [f"ydone{tt}", "ost"])
            continue
        if g2 + 1 < NT2:
            emitA(g2 + 1)
        for tt2 in range(2):
            tt = g2 * 2 + tt2
            hsl = H2[:, tt2 * HID:(tt2 + 1) * HID]
            V_(lambda g, tt2=tt2: g.tensor_reduce(
                st(12), gacc[:, tt2 * 8:(tt2 + 1) * 8], AX.X, ALU.add),
                [f"gacc{tt2}_{hs}" for hs in range(8)], ["f2a"])
            for qtr in range(4):
                SC(lambda g, hsl=hsl, qtr=qtr: g.activation(
                    g2048f[:, 0:1024], hsl[:, qtr * 1024:(qtr + 1) * 1024],
                    AF.Square, accum_out=st(28 + qtr)),
                    [f"H2_{tt2}"], ["g2048a", f"f2sq{qtr}"])
            V_(lambda g: g.tensor_reduce(st(13), statp[:, 28:32], AX.X, ALU.add),
               [f"f2sq{q2}" for q2 in range(4)], ["f2b"])
            ln_stats(12, 13, 14, HID, "f2")
            SC(lambda g, hsl=hsl, tt2=tt2: g.activation(
                hsl, hsl, AF.Identity, bias=st(15), scale=st(14)),
                [f"H2_{tt2}", "f2c", "f2d"], [f"H2_{tt2}"])
            V_(lambda g, hsl=hsl, tt2=tt2: g.tensor_tensor(
                hsl, hsl, wexp2[:], ALU.mult), [f"H2_{tt2}", "wexp2"], [f"H2_{tt2}"])
            V_(lambda g, hsl=hsl, tt2=tt2: g.tensor_tensor(
                hsl, hsl, bexp2[:], ALU.add), [f"H2_{tt2}", "bexp2"], [f"H2_{tt2}"])
            V_(lambda g, hsl=hsl: g.tensor_reduce(
                st(24), hsl, AX.X, ALU.max, apply_absolute_value=True),
                [f"H2_{tt2}"], ["mx2"])
            V_(lambda g: g.tensor_scalar(st(24), st(24), 1e-5, None, ALU.max),
               ["mx2"], ["mx2"])
            V_(lambda g, tt2=tt2: g.tensor_scalar(
                s2cols[:, tt2:tt2 + 1], st(24), 1.0 / 127.0, None, ALU.mult),
               ["mx2"], [f"s2col{tt2}"])
            V_(lambda g, tt2=tt2: g.reciprocal(s2cols[:, tt2:tt2 + 1],
                                               s2cols[:, tt2:tt2 + 1]),
               [f"s2col{tt2}"], [f"s2col{tt2}"])
            V_(lambda g, tt2=tt2: g.tensor_tensor(st(26), s2cols[:, tt2:tt2 + 1],
                                                  swcol[:, 1:2], ALU.mult),
               [f"s2col{tt2}", "swcol1"], ["sc2f"])
            V_(lambda g, tt2=tt2: g.reciprocal(c2col[:, tt2:tt2 + 1], st(26)),
               ["sc2f"], [f"c2col{tt2}"])
            V_(lambda g: g.tensor_copy(statb1[:], st(26)), ["sc2f"], ["statb1"])
            T_(lambda g: g.matmul(PX[0:1, 0:128], statb1[:], identb[:],
                                  start=True, stop=True),
               ["statb1", "identb"], ["PX0", "BPX"])
            V_(lambda g, tt2=tt2: g.tensor_copy(
                screw2[:, tt2 * 128:(tt2 + 1) * 128], PX[0:1, 0:128]),
                ["PX0", "BPX"], [f"sc2row{tt2}"])
        for tt2 in range(2):
            tt = g2 * 2 + tt2
            hsl = H2[:, tt2 * HID:(tt2 + 1) * HID]
            for qtr in range(4):
                V_(lambda g, hsl=hsl, qtr=qtr, tt2=tt2: g.tensor_scalar(
                    g2048f[:, 0:1024], hsl[:, qtr * 1024:(qtr + 1) * 1024],
                    s2cols[:, tt2:tt2 + 1], KT, ALU.mult, ALU.add),
                    [f"H2_{tt2}", f"s2col{tt2}"], ["g2048a"])
                V_(lambda g: g.tensor_copy(qi32buf[:, 0:1024], g2048f[:, 0:1024]),
                   ["g2048a"], ["qi32"])
                V_(lambda g: g.scalar_tensor_tensor(
                    outstage[:], qi32buf[:, 0:1024], 0.5, g2048f[:, 0:1024],
                    ALU.add, ALU.is_le), ["qi32", "g2048a", "ost"], ["ost"])
                V_(lambda g, qtr=qtr: g.scalar_tensor_tensor(
                    qhtok[:, qtr * 1024:(qtr + 1) * 1024], qi32buf[:, 0:1024],
                    -KT, outstage[:], ALU.add, ALU.add),
                    ["qi32", "ost"], [f"qh{qtr // 2}"])
            for kb in range(8):
                for kq in range(4):
                    kc = kb * 4 + kq
                    T_(lambda g, kc=kc, kq=kq: g.matmul(
                        PE_[:, kq * 128:(kq + 1) * 128],
                        qhtok[:, kc * 128:(kc + 1) * 128], identb[:],
                        start=True, stop=True),
                        [f"qh{kc // 16}", "identb"], [f"PEq{kq}", "BPE"])
                sl2 = kb % 2
                if sl2 == 0:
                    V_(lambda g, sl2=sl2: g.tensor_copy(
                        qhfm[:, sl2 * 512:(sl2 + 1) * 512], PE_[:]),
                        [f"PEq{kq}" for kq in range(4)] + ["BPE"], [f"qhfm{sl2}"])
                else:
                    SC(lambda g, sl2=sl2: g.copy(
                        qhfm[:, sl2 * 512:(sl2 + 1) * 512], PE_[:]),
                        [f"PEq{kq}" for kq in range(4)] + ["BPE"], [f"qhfm{sl2}"])
                for kq in range(4):
                    kc = kb * 4 + kq
                    for eh in range(2):
                        PTa = ((PC[:], PD[:]) if tt2 == 0
                               else (PB[:, 0:512], PB[:, 512:1024]))[eh]
                        bk = (("BPC", "BPD") if tt2 == 0
                              else ("BPB0", "BPB1"))[eh]
                        T_(lambda g, kc=kc, PTa=PTa, sl2=sl2, kq=kq: g.matmul(
                            PTa, qhfm[:, sl2 * 512 + kq * 128:sl2 * 512 + (kq + 1) * 128],
                            qw2sb[:, kc * D + eh * 512:kc * D + (eh + 1) * 512],
                            start=(kc == 0), stop=False),
                            [f"qhfm{sl2}", f"qw2r{kc}"],
                            [f"Pw2{eh}_{tt2}", bk],
                            acc=(kc != 0))
            for eh in range(2):
                PTa = ((PC[:], PD[:]) if tt2 == 0
                       else (PB[:, 0:512], PB[:, 512:1024]))[eh]
                bk = (("BPC", "BPD") if tt2 == 0
                      else ("BPB0", "BPB1"))[eh]
                dma("sync", rstgf[:], b2_d[0:1, eh * 512:(eh + 1) * 512],
                    [], ["rstgf"])
                V_(lambda g: g.tensor_copy(rstgb[:], rstgf[:]), ["rstgf"], ["rstgb"])
                T_(lambda g, tt2=tt2, PTa=PTa: g.matmul(
                    PTa, screw2[0:1, tt2 * 128:(tt2 + 1) * 128],
                    rstgb[:], start=False, stop=True),
                    [f"sc2row{tt2}", "rstgb"],
                    [f"Pw2{eh}_{tt2}", bk], acc=True)
                SC(lambda g, eh=eh, tt2=tt2, PTa=PTa: g.activation(
                    fobuf[:, eh * 512:(eh + 1) * 512], PTa, AF.Copy,
                    scale=c2col[:, tt2:tt2 + 1], accum_out=st(16 + eh)),
                    [f"Pw2{eh}_{tt2}", bk, f"c2col{tt2}"],
                    [f"fo{eh}", f"foacc{eh}"])
            V_(lambda g: g.tensor_tensor(st(18), st(16), st(17), ALU.add),
               ["foacc0", "foacc1"], ["f3a"])
            SC(lambda g: g.activation(g2048f[:, 0:D], fobuf[:], AF.Square,
                                      accum_out=st(19)),
               ["fo0", "fo1"], ["g2048a", "f3b"])
            ln_stats(18, 19, 20, D, "f3")
            SC(lambda g: g.activation(hn3buf[:], fobuf[:], AF.Identity, bias=st(21),
                                      scale=st(20)),
               ["fo0", "fo1", "f3c", "f3d"], ["hn3"])
            V_(lambda g, tt=tt: g.tensor_tensor(
                outstage[:], hn3buf[:], X2[:, tt * D:(tt + 1) * D], ALU.add),
                ["hn3", f"x2:{tt}"], ["ost"])
            dma("gpsimd", yd[tt * CH:tt * CH + 128, :], outstage[:],
                ["ost"], [f"ydone{tt}", "ost"])

    S.emit("sync", None, reads=[f"ydone{tt}" for tt in range(NTT)], writes=[])
    S.emit("gpsimd", None, reads=[f"ydone{tt}" for tt in range(NTT)], writes=[])
    p2.close()
    es.close()
    return nc


def kernel(**inputs):
    n_cores = 8
    B, Tfull, Dm = inputs["L0_attn_x"].shape
    n_seq = B // n_cores
    n_chunks = Tfull // CH

    x = np.ascontiguousarray(inputs["L0_attn_x"], dtype=np.float32)
    wqT = np.ascontiguousarray(np.asarray(inputs["L0_attn_Wq"]).T)
    wkT = np.ascontiguousarray(np.asarray(inputs["L0_attn_Wk"]).T)
    wvT = np.ascontiguousarray(np.asarray(inputs["L0_attn_Wv"]).T)
    woT = np.ascontiguousarray(np.asarray(inputs["L0_attn_Wo"]).T)
    w1T = np.ascontiguousarray(np.asarray(inputs["L0_ffn_w1"]).T)
    w2T = np.ascontiguousarray(np.asarray(inputs["L0_ffn_w2"]).T)
    gl = np.stack([inputs["L0_attn_W_gamma"], inputs["L0_attn_b_gamma"],
                   inputs["L0_attn_W_lambda"], inputs["L0_attn_b_lambda"]]
                  ).astype(np.float32)
    lnp = np.zeros((4, HID), np.float32)
    lnp[0, :Dm] = inputs["L0_ffn_ln1_w"]
    lnp[1, :Dm] = inputs["L0_ffn_ln1_b"]
    lnp[2, :] = inputs["L0_ffn_ln2_w"]
    lnp[3, :] = inputs["L0_ffn_ln2_b"]
    b1 = np.asarray(inputs["L0_ffn_b1"]).reshape(1, HID).astype(np.float32)
    b2 = np.asarray(inputs["L0_ffn_b2"]).reshape(1, Dm).astype(np.float32)

    identb = np.eye(128, dtype=ml_dtypes.bfloat16)
    identf = np.eye(128, dtype=np.float32)
    ieye2 = np.concatenate([np.eye(64), np.eye(64)], axis=0).astype(ml_dtypes.bfloat16)
    sidx = np.arange(128)[:, None]
    tidx = np.arange(128)[None, :]
    maskmin = np.where(sidx <= tidx, 0.0, -1e30).astype(np.float32)
    selpc = np.zeros((16, 8 * 128), np.float32)
    for pc in range(8):
        selpc[2 * pc, pc * 128:pc * 128 + 64] = 1.0
        selpc[2 * pc + 1, pc * 128 + 64:pc * 128 + 128] = 1.0
    selpc = selpc.astype(ml_dtypes.bfloat16)

    nc = build_program(n_seq, n_chunks)

    shared = {"wqT": wqT, "wkT": wkT, "wvT": wvT, "woT": woT, "w1T": w1T,
              "w2T": w2T, "gl": gl, "lnp": lnp, "b1": b1, "b2": b2,
              "identb": identb, "identf": identf, "ieye2": ieye2,
              "maskmin": maskmin, "selpc": selpc}
    in_maps = []
    for ci in range(n_cores):
        m = dict(shared)
        m["x"] = np.ascontiguousarray(
            x[ci * n_seq:(ci + 1) * n_seq].reshape(n_seq * Tfull, Dm))
        in_maps.append(m)
    res = run_bass_kernel_spmd(nc, in_maps, core_ids=list(range(n_cores)))
    out = np.empty((B, Tfull, Dm), np.float32)
    for ci in range(n_cores):
        out[ci * n_seq:(ci + 1) * n_seq] = np.asarray(
            res.results[ci]["y"]).reshape(n_seq, Tfull, Dm)
    return out
